# version 17
# baseline (speedup 1.0000x reference)
import math
from contextlib import ExitStack

import numpy as np
import ml_dtypes

import concourse.bass as bass
import concourse.mybir as mybir
from concourse.bass_utils import run_bass_kernel_spmd

F32 = mybir.dt.float32
BF16 = mybir.dt.bfloat16
I32 = mybir.dt.int32
AF = mybir.ActivationFunctionType
ALU = mybir.AluOpType
AX = mybir.AxisListType
bfdt = ml_dtypes.bfloat16

H = 4096
NH = 32
B = 4
S = 4096
NCORE = 8
QR = 2048
SLOT_E = (2048, 4096)
NEAR_MAX = 11
WCOEF = 36.0
KVDEPTH = 4
QSCALE = 128.0 ** -0.5

SLOPES = [2.0 ** (-(i + 1) / 4.0) for i in range(NH)]
WIN = [min(int(math.ceil(WCOEF / s)), 8192) for s in SLOPES]
Q0P = {(s, qc): SLOT_E[s] - 1024 + 512 * qc for s in range(2) for qc in range(2)}
NKT = {k: (v + 512) // 128 for k, v in Q0P.items()}
BB_OFF = {}
_o = 0
for _s in range(2):
    for _qc in range(2):
        BB_OFF[(_s, _qc)] = _o
        _o += NKT[(_s, _qc)]
NBV = _o
FAR_HEADS = [h for h in range(NH) if h > NEAR_MAX]
NFAR = len(FAR_HEADS)


def _jmin(h, s, qc):
    return max(0, (Q0P[(s, qc)] - WIN[h]) // 128)


def _chunks_for(par):
    return ([0, 3], [1, 2])[par]


def _stage_core(x, b, par):
    cks = _chunks_for(par)
    xb = x[b]
    xq = np.concatenate([xb[1024 * c:1024 * (c + 1)] for c in cks], axis=0)
    kv = []
    for s in range(2):
        e = SLOT_E[s]
        r = 1024 * (cks[s] + 1)
        lo = max(0, r - e)
        w = xb[lo:r]
        if w.shape[0] < e:
            w = np.concatenate([np.zeros((e - w.shape[0], H), np.float32), w], axis=0)
        kv.append(np.ascontiguousarray(w))
    return np.ascontiguousarray(xq), kv[0], kv[1]


def _const_tables(par):
    cks = _chunks_for(par)
    pads = [max(0, SLOT_E[s] - 1024 * (cks[s] + 1)) for s in range(2)]
    p = np.arange(128)
    f = np.arange(512)
    tstk = np.zeros((28, 128, 512), np.float32)
    i = 0
    for s in range(2):
        for qc in range(2):
            q0 = Q0P[(s, qc)]
            for cv in range(7):
                c = -384 + 128 * cv
                t = (c + p[:, None] - f[None, :]).astype(np.float32)
                bad = (c + p[:, None] > f[None, :]) | ((q0 + c + p[:, None]) < pads[s])
                t[bad] = -1e33
                tstk[i] = t
                i += 1
    mstk = np.zeros((4, 128, 512), np.float32)
    for cv in range(4):
        c = 128 * cv
        mstk[cv][(c + p[:, None]) > f[None, :]] = -1e30
    bbase = np.zeros((128, NBV), np.float32)
    for s in range(2):
        for qc in range(2):
            q0 = Q0P[(s, qc)]
            off = BB_OFF[(s, qc)]
            for j in range(NKT[(s, qc)]):
                v = (128 * j + p - (q0 + 511)).astype(np.float32)
                v[(128 * j + p) < pads[s]] = -1e30
                bbase[:, off + j] = v
    eh2f = np.zeros((128, 4), np.float32)
    eh2f[:, 0] = 1.0
    eh2f[:, 3] = 1.0
    sel2 = np.zeros((2, 256), np.float32)
    sel2[0, 0:128] = 1.0
    sel2[1, 128:256] = 1.0
    ident = np.eye(128, dtype=np.float32)
    simh = np.zeros((128, 32 * 128), np.float32)
    for h in range(NH):
        simh[:, 128 * h:128 * (h + 1)] = np.eye(128, dtype=np.float32) * SLOPES[h]
    return (tstk.astype(bfdt), mstk.astype(bfdt), bbase, eh2f, sel2.astype(bfdt),
            ident.astype(bfdt), simh.astype(bfdt))


class Prog:
    ENGINES = ("sync", "tensor", "vector", "scalar", "gpsimd")

    def __init__(self):
        self.events = []
        self.cnt = {}
        self._waited = {}

    def sem(self, *names):
        for n in names:
            self.cnt[n] = 0

    ENGSEMS = ("t", "v", "s", "g")

    def emit(self, eng, fn, waits=(), incs=()):
        w = []
        for sname, v in waits:
            if v is None or v <= 0:
                continue
            self.cnt.setdefault(sname, 0)
            assert v <= self.cnt[sname], (sname, v, self.cnt[sname])
            if sname not in self.ENGSEMS:
                assert v == self.cnt[sname], ("unsound dma wait", sname, v, self.cnt[sname])
            key = (eng, sname)
            if v > self._waited.get(key, -1):
                self._waited[key] = v
                w.append((sname, v))
        for sname, n in incs:
            self.cnt[sname] = self.cnt.setdefault(sname, 0) + n
        self.events.append((eng, tuple(w), fn, tuple(incs)))
        return self.cnt[incs[0][0]] if incs else None

    def dw(self, stream):
        return (stream, self.cnt.get(stream, 0))

    def allw(self):
        return [(sname, v) for sname, v in self.cnt.items() if v > 0]

    def barrier(self):
        snap = dict(self.cnt)
        for eng in self.ENGINES:
            self.emit(eng, lambda e: e.nop(),
                      waits=[(sname, v) for sname, v in snap.items() if v > 0])

    def lower(self, nc, sem_stack):
        sem_handles = {n: sem_stack.enter_context(nc.semaphore("sem_" + n))
                       for n in self.cnt}
        with nc.Block() as block:
            for eng_name in self.ENGINES:
                evs = [ev for ev in self.events if ev[0] == eng_name]
                if not evs:
                    continue

                def make(my_evs):
                    def body(eng):
                        for (_e, waits, fn, incs) in my_evs:
                            for (sname, v) in waits:
                                eng.wait_ge(sem_handles[sname], v)
                            ins = fn(eng)
                            for (sname, n) in incs:
                                ins.then_inc(sem_handles[sname], n)
                    return body
                getattr(block, eng_name)(make(evs))


def _emit_taps(nc, P, taps, scratch):
    for name in taps:
        src_ap = scratch[name]
        tout = nc.dram_tensor("tap_" + name, list(src_ap.shape), src_ap.dtype,
                              kind="ExternalOutput").ap()
        dn = P.emit("sync", lambda e, o_=tout, i_=src_ap: e.dma_start(out=o_, in_=i_),
                    waits=P.allw(), incs=[("tp", 16)])
        P.emit("sync", lambda e: e.nop(), waits=[("tp", dn)])


def _tap_sbuf(nc, P, name, ap, dt):
    tout = nc.dram_tensor("tap_" + name, list(ap.shape), dt, kind="ExternalOutput").ap()
    dn = P.emit("sync", lambda e: e.dma_start(out=tout, in_=ap), waits=P.allw(),
                incs=[("tp", 16)])
    P.emit("sync", lambda e: e.nop(), waits=[("tp", dn)])


def _finish_early(nc, P, out):
    with nc.sbuf_tensor("zfin", [128, H], F32) as zf:
        zv = P.emit("vector", lambda e: e.memset(zf[:], 0.0), waits=P.allw(), incs=[("v", 1)])
        dn = 0
        for t_ in range(QR // 128):
            dn = P.emit("scalar", lambda e, tt=t_: e.dma_start(
                out=out[128 * tt:128 * (tt + 1), :], in_=zf[:]),
                waits=[("v", zv)], incs=[("fz", 16)])
        P.emit("sync", lambda e: e.nop(), waits=[("fz", dn)])


def build_nc(stop_phase=9, taps=(), dparts=('vtr', 'dq', 'wo')):
    nc = bass.Bass("TRN2", target_bir_lowering=False)
    P = Prog()
    P.sem("t", "v", "s", "g")

    def din(name, shape, dt):
        return nc.dram_tensor(name, shape, dt, kind="ExternalInput").ap()

    xq = din("xq", [QR, H], F32)
    xkv0 = din("xkv0", [SLOT_E[0], H], F32)
    xkv1 = din("xkv1", [SLOT_E[1], H], F32)
    wq = din("wq", [H, H], F32)
    wkv = din("wkv", [H, 256], F32)
    wo = din("wo", [H, H], F32)
    qw = din("qw", [H, H], I32)
    sz = din("sz", [128, 2048], F32)
    fb = din("fb", [128, 32], F32)
    tstk = din("tstk", [28, 128, 512], BF16)
    mstk = din("mstk", [4, 128, 512], BF16)
    simh = din("simh", [128, 32 * 128], BF16)
    bbase = din("bbase", [128, NBV], F32)
    eh2f = din("eh2f", [128, 4], F32)
    sel2 = din("sel2", [2, 256], BF16)
    ident = din("ident", [128, 128], BF16)
    onescol = din("onescol", [128, 1], BF16)
    onescolf = din("onescolf", [128, 1], F32)
    onesrow = din("onesrow", [1, 128], BF16)

    out = nc.dram_tensor("out", [QR, H], F32, kind="ExternalOutput").ap()

    xnh = nc.dram_tensor("xnh", [QR + SLOT_E[0] + SLOT_E[1], H], BF16, kind="Internal").ap()
    qth = nc.dram_tensor("qth", [H, QR], BF16, kind="Internal").ap()
    wqbh = nc.dram_tensor("wqbh", [H, H], BF16, kind="Internal").ap()
    wobh = nc.dram_tensor("wobh", [H, H], BF16, kind="Internal").ap()
    wfth = nc.dram_tensor("wfth", [32, 128, H], BF16, kind="Internal").ap()
    sth = nc.dram_tensor("sth", [H, QR], BF16, kind="Internal").ap()
    sgh = nc.dram_tensor("sgh", [H, QR], BF16, kind="Internal").ap()
    scratch = dict(xnh=xnh, qth=qth, wqbh=wqbh, wobh=wobh, wfth=wfth, sth=sth, sgh=sgh)

    st0 = ExitStack()
    E = st0.enter_context

    c_ident = E(nc.sbuf_tensor("c_ident", [128, 128], BF16))
    c_mstk = E(nc.sbuf_tensor("c_mstk", [128, 4 * 512], BF16))
    c_simh = E(nc.sbuf_tensor("c_simh", [128, 32 * 128], BF16))
    c_bbase = E(nc.sbuf_tensor("c_bbase", [128, NBV], F32))
    c_fb = E(nc.sbuf_tensor("c_fb", [128, 32], F32))
    c_onec = E(nc.sbuf_tensor("c_onec", [128, 1], BF16))
    c_onecf = E(nc.sbuf_tensor("c_onecf", [128, 1], F32))
    c_oner = E(nc.sbuf_tensor("c_oner", [1, 128], BF16))
    kt_s = [E(nc.sbuf_tensor(f"kt{s}", [128, SLOT_E[s]], BF16)) for s in range(2)]
    vt_s = [E(nc.sbuf_tensor(f"vt{s}", [128, SLOT_E[s]], BF16)) for s in range(2)]
    ppb = E(nc.sbuf_tensor("ppb", [128, 6 * 512], BF16))
    accA = E(nc.sbuf_tensor("accA", [128, 2 * 512], F32))
    rd_f = [E(nc.sbuf_tensor(f"rd_f{i}", [1, 512], F32)) for i in range(2)]
    rd_b = [E(nc.sbuf_tensor(f"rd_b{i}", [1, 512], BF16)) for i in range(2)]
    rs_f0 = E(nc.sbuf_tensor("rs_f0", [128, 512], F32))
    rs_f = [rs_f0, rs_f0]

    with ExitStack() as stA:
        EA = stA.enter_context
        xb = [EA(nc.sbuf_tensor(f"xA{i}", [128, H], F32)) for i in range(2)]
        xnb = [EA(nc.sbuf_tensor(f"xnA{i}", [128, H], BF16)) for i in range(2)]
        sq = EA(nc.sbuf_tensor("sqA", [128, H], BF16))
        stt = [EA(nc.sbuf_tensor(f"stA{i}", [128, 8], F32)) for i in range(2)]

        for dst, src in ((c_ident, ident), (c_bbase, bbase),
                         (c_fb, fb), (c_simh, simh),
                         (c_onec, onescol), (c_oner, onesrow),
                         (c_onecf, onescolf)):
            P.emit("sync", lambda e, d=dst, s_=src: e.dma_start(out=d[:], in_=s_),
                   incs=[("aC", 16)])
        P.emit("sync", lambda e: e.dma_start(
            out=c_mstk[:].rearrange("p (c f) -> p c f", c=4), in_=mstk.rearrange("c p f -> p c f")), incs=[("aC", 16)])

        srcs = [(xq, 0, QR // 128), (xkv0, QR // 128, SLOT_E[0] // 128),
                (xkv1, (QR + SLOT_E[0]) // 128, SLOT_E[1] // 128)]
        tiles = []
        for src, base_t, n in srcs:
            for tloc in range(n):
                tiles.append((src, base_t + tloc, tloc))
        last_use = [None, None]
        xnb_guard = [0, 0]
        sq_guard = [0]
        for i, (src, trow, tloc) in enumerate(tiles):
            pb = i % 2
            waits = []
            if last_use[pb] is not None:
                waits = [("v", last_use[pb][0]), ("s", last_use[pb][1])]
            dl = P.emit("sync", lambda e, sr=src, tl=tloc, p_=pb: e.dma_start(
                out=xb[p_][:], in_=sr[128 * tl:128 * (tl + 1), :]),
                waits=waits, incs=[(f"aX{pb}", 16)])
            st_ = stt[pb]
            v1 = P.emit("vector", lambda e, p_=pb, st=st_: e.reduce_sum(
                out=st[:, 0:1], in_=xb[p_][:], axis=AX.X),
                waits=[P.dw(f"aX{pb}")], incs=[("v", 1)])
            s1g = P.emit("scalar", lambda e, p_=pb, st=st_: e.activation(
                sq[:], xb[p_][:], AF.Square, accum_out=st[:, 1:2]),
                waits=[P.dw(f"aX{pb}")], incs=[("s", 1)])
            v2 = P.emit("vector", lambda e, st=st_: e.tensor_scalar(
                out=st[:, 2:3], in0=st[:, 0:1], scalar1=1.0 / H, scalar2=None,
                op0=ALU.mult), waits=[("v", v1)], incs=[("v", 1)])
            v3 = P.emit("vector", lambda e, st=st_: e.scalar_tensor_tensor(
                out=st[:, 3:4], in0=st[:, 2:3], scalar=-1.0, in1=st[:, 2:3],
                op0=ALU.mult, op1=ALU.mult),
                waits=[("v", v2)], incs=[("v", 1)])
            v4a = P.emit("vector", lambda e, st=st_: e.scalar_tensor_tensor(
                out=st[:, 4:5], in0=st[:, 1:2], scalar=1.0 / H, in1=st[:, 3:4],
                op0=ALU.mult, op1=ALU.add),
                waits=[("s", s1g), ("v", v3)], incs=[("v", 1)])
            v4 = P.emit("vector", lambda e, st=st_: e.tensor_scalar(
                out=st[:, 4:5], in0=st[:, 4:5], scalar1=1e-5, scalar2=None,
                op0=ALU.add), waits=[("v", v4a)], incs=[("v", 1)])
            s2 = P.emit("scalar", lambda e, st=st_: e.activation(
                st[:, 5:6], st[:, 4:5], AF.Ln),
                waits=[("v", v4)], incs=[("s", 1)])
            s3 = P.emit("scalar", lambda e, st=st_: e.activation(
                st[:, 6:7], st[:, 5:6], AF.Exp, scale=-0.5),
                waits=[("s", s2)], incs=[("s", 1)])
            v5 = P.emit("vector", lambda e, st=st_: e.scalar_tensor_tensor(
                out=st[:, 7:8], in0=st[:, 2:3], scalar=-1.0, in1=st[:, 6:7],
                op0=ALU.mult, op1=ALU.mult),
                waits=[("s", s3)], incs=[("v", 1)])
            s4 = P.emit("scalar", lambda e, p_=pb, st=st_: e.activation(
                xnb[p_][:], xb[p_][:], AF.Identity, scale=st[:, 6:7], bias=st[:, 7:8]),
                waits=[("v", v5), ("s", s3), (f"aN{pb}", xnb_guard[pb])], incs=[("s", 1)])
            dxn = P.emit("scalar", lambda e, p_=pb, tr=trow: e.dma_start(
                out=xnh[128 * tr:128 * (tr + 1), :], in_=xnb[p_][:]),
                waits=[("s", s4)], incs=[(f"aN{pb}", 16)])
            last_use[pb] = (v1, s4)
            xnb_guard[pb] = dxn
        P.barrier()
    if stop_phase <= 1:
        raise_stop = True
    else:
        raise_stop = False
    if raise_stop:
        _emit_taps(nc, P, taps, scratch)
        _finish_early(nc, P, out)
        P.lower(nc, st0)
        return nc

    vt_tmp = {}
    stB = ExitStack()
    EB = stB.enter_context
    vt_tmp[0] = EB(nc.sbuf_tensor("vtt0", [128, SLOT_E[0]], BF16))
    vt_tmp[1] = EB(nc.sbuf_tensor("vtt1", [128, SLOT_E[1]], BF16))
    with ExitStack() as stB2:
        EB2 = stB2.enter_context
        xnt = [EB2(nc.sbuf_tensor(f"xntB{i}", [128, 2048], BF16)) for i in range(2)]
        wkv_raw = EB2(nc.sbuf_tensor("wkv_raw", [128, 32 * 256], F32))
        wkv_bf = EB2(nc.sbuf_tensor("wkv_bf", [128, 32 * 256], BF16))
        pbig = EB2(nc.psum_tensor("pbigB", [128, 4096], F32))
        dkv = P.emit("sync", lambda e: e.dma_start(
            out=wkv_raw[:].rearrange("p (j f) -> p j f", j=32), in_=wkv.rearrange("(j p) f -> p j f", p=128)), incs=[("bW", 16)])
        vcast = P.emit("vector", lambda e: e.tensor_copy(out=wkv_bf[:], in_=wkv_raw[:]),
                       waits=[("bW", dkv)], incs=[("v", 1)])
        mm_done = 0
        mmj_done = {}
        gj = [0]
        for s in range(2):
            e_s = SLOT_E[s]
            row0 = (QR if s == 0 else QR + SLOT_E[0])
            for half in range(e_s // 2048):
                r0h = row0 + 2048 * half
                for j in range(32):
                    pb = gj[0] % 2
                    dl = P.emit("sync", lambda e, jj=j, p_=pb, r0=r0h: e.dma_start_transpose(
                        xnt[p_][:, 0:2048], xnh[r0:r0 + 2048, 128 * jj:128 * (jj + 1)]),
                        waits=[("v", vcast), ("t", mmj_done.get(gj[0] - 2, 0))],
                        incs=[(f"bX{pb}", 16)])
                    for ch in range(4):
                        P.emit("tensor", lambda e, jj=j, p_=pb, c_=ch: e.matmul(
                            pbig[:, 512 * c_:512 * (c_ + 1)],
                            wkv_bf[:, 256 * jj:256 * jj + 128],
                            xnt[p_][:, 512 * c_:512 * (c_ + 1)],
                            start=(jj == 0), stop=(jj == 31)),
                            waits=[P.dw(f"bX{pb}")], incs=[("t", 1)])
                        mm_done = P.emit("tensor", lambda e, jj=j, p_=pb, c_=ch: e.matmul(
                            pbig[:, 2048 + 512 * c_:2048 + 512 * (c_ + 1)],
                            wkv_bf[:, 256 * jj + 128:256 * jj + 256],
                            xnt[p_][:, 512 * c_:512 * (c_ + 1)],
                            start=(jj == 0), stop=(jj == 31)),
                            incs=[("t", 1)])
                    mmj_done[gj[0]] = mm_done
                    gj[0] += 1
                wv = ws = 0
                for ch in range(4):
                    if ch % 2 == 0:
                        wv = P.emit("vector", lambda e, c_=ch, s_=s, hf=half: e.tensor_copy(
                            out=kt_s[s_][:, 2048 * hf + 512 * c_:2048 * hf + 512 * (c_ + 1)],
                            in_=pbig[:, 512 * c_:512 * (c_ + 1)]),
                            waits=[("t", mm_done)], incs=[("v", 1)])
                        wv = P.emit("vector", lambda e, c_=ch, s_=s, hf=half: e.tensor_copy(
                            out=vt_tmp[s_][:, 2048 * hf + 512 * c_:2048 * hf + 512 * (c_ + 1)],
                            in_=pbig[:, 2048 + 512 * c_:2048 + 512 * (c_ + 1)]),
                            waits=[("t", mm_done)], incs=[("v", 1)])
                    else:
                        ws = P.emit("scalar", lambda e, c_=ch, s_=s, hf=half: e.copy(
                            kt_s[s_][:, 2048 * hf + 512 * c_:2048 * hf + 512 * (c_ + 1)],
                            pbig[:, 512 * c_:512 * (c_ + 1)]),
                            waits=[("t", mm_done)], incs=[("s", 1)])
                        ws = P.emit("scalar", lambda e, c_=ch, s_=s, hf=half: e.copy(
                            vt_tmp[s_][:, 2048 * hf + 512 * c_:2048 * hf + 512 * (c_ + 1)],
                            pbig[:, 2048 + 512 * c_:2048 + 512 * (c_ + 1)]),
                            waits=[("t", mm_done)], incs=[("s", 1)])
                P.emit("tensor", lambda e: e.nop(), waits=[("v", wv), ("s", ws)])
        P.barrier()
    if stop_phase <= 2:
        for nm in taps:
            if nm.startswith("kv:"):
                w = nm.split(":")[1]
                obj = {"kt0": kt_s[0], "kt1": kt_s[1], "vtt0": vt_tmp[0], "vtt1": vt_tmp[1]}[w]
                _tap_sbuf(nc, P, w, obj[:], BF16)
        taps = [t for t in taps if not t.startswith("kv:")]
        _emit_taps(nc, P, taps, scratch)
        _finish_early(nc, P, out)
        P.lower(nc, st0)
        return nc

    with ExitStack() as stC:
        EC = stC.enter_context
        xntq = [EC(nc.sbuf_tensor(f"xntqC{j}", [128, 1024], BF16)) for j in range(32)]
        wqr = [EC(nc.sbuf_tensor(f"wqrC{i}", [128, H], F32)) for i in range(2)]
        wqb = [EC(nc.sbuf_tensor(f"wqbC{i}", [128, H], BF16)) for i in range(2)]
        qst = [EC(nc.sbuf_tensor(f"qstC{i}", [128, 1024], BF16)) for i in range(2)]
        pq = EC(nc.psum_tensor("pqC", [128, 4096], F32))

        prev = {}
        for half in range(2):
            tcol = 1024 * half
            xd = []
            tguard = P.cnt["t"]
            for j in range(32):
                d = P.emit("sync", lambda e, jj=j, tc=tcol: e.dma_start_transpose(
                    xntq[jj][:], xnh[tc:tc + 1024, 128 * jj:128 * (jj + 1)]),
                    waits=[("t", tguard)], incs=[("cQ", 16)])
                xd.append(d)
            cq_full = P.dw("cQ")
            for dt in range(32):
                pb = dt % 2
                waits = []
                if pb in prev:
                    waits = [("v", prev[pb][1]), ("s", prev[pb][2]), P.dw(f"cQT{pb}")]
                if half == 0:
                    dl = P.emit("sync", lambda e, d_=dt, p_=pb: e.dma_start(
                        out=wqr[p_][:].rearrange("p (j f) -> p j f", j=32),
                        in_=wq[:, 128 * d_:128 * (d_ + 1)].rearrange("(j p) f -> p j f", p=128)),
                        waits=waits, incs=[(f"cW{pb}", 16)])
                    cv = P.emit("vector", lambda e, p_=pb: e.tensor_scalar(
                        out=wqb[p_][:], in0=wqr[p_][:], scalar1=QSCALE, scalar2=None,
                        op0=ALU.mult),
                        waits=[P.dw(f"cW{pb}"), P.dw(f"cWB{pb}")] + waits, incs=[("v", 1)])
                    P.emit("scalar", lambda e, d_=dt, p_=pb: e.dma_start(
                        out=wqbh[:, 128 * d_:128 * (d_ + 1)].rearrange("(j p) f -> p j f", p=128),
                        in_=wqb[p_][:].rearrange("p (j f) -> p j f", j=32)),
                        waits=[("v", cv)], incs=[(f"cWB{pb}", 16)])
                    ready = [("v", cv)]
                else:
                    wb_guard = [P.dw("cWB0"), P.dw("cWB1")] if dt == 0 else []
                    dl = P.emit("sync", lambda e, d_=dt, p_=pb: e.dma_start(
                        out=wqb[p_][:].rearrange("p (j f) -> p j f", j=32),
                        in_=wqbh[:, 128 * d_:128 * (d_ + 1)].rearrange("(j p) f -> p j f", p=128)),
                        waits=waits + wb_guard, incs=[(f"cW{pb}", 16)])
                    ready = [P.dw(f"cW{pb}")]
                mm_done = 0
                for j in range(32):
                    for ch in range(2):
                        mm_done = P.emit("tensor", lambda e, jj=j, c_=ch, p_=pb: e.matmul(
                            pq[:, 2048 * p_ + 512 * c_:2048 * p_ + 512 * (c_ + 1)],
                            wqb[p_][:, 128 * jj:128 * (jj + 1)],
                            xntq[jj][:, 512 * c_:512 * (c_ + 1)],
                            start=(jj == 0), stop=(jj == 31)),
                            waits=ready + [cq_full], incs=[("t", 1)])
                evv = P.emit("vector", lambda e, p_=pb: e.tensor_copy(
                    out=qst[p_][:, 0:512], in_=pq[:, 2048 * p_:2048 * p_ + 512]),
                    waits=[("t", mm_done)], incs=[("v", 1)])
                evs = P.emit("scalar", lambda e, p_=pb: e.copy(
                    qst[p_][:, 512:1024], pq[:, 2048 * p_ + 512:2048 * p_ + 1024]),
                    waits=[("t", mm_done)], incs=[("s", 1)])
                dq = P.emit("scalar", lambda e, d_=dt, p_=pb, tc=tcol: e.dma_start(
                    out=qth[128 * d_:128 * (d_ + 1), tc:tc + 1024], in_=qst[p_][:]),
                    waits=[("v", evv), ("s", evs)], incs=[(f"cQT{pb}", 16)])
                prev[pb] = (mm_done, evv, evs, dq)
        P.barrier()
    if stop_phase <= 3:
        _emit_taps(nc, P, taps, scratch)
        _finish_early(nc, P, out)
        P.lower(nc, st0)
        return nc

    with ExitStack() as stD:
        ED = stD.enter_context
        pbf = ED(nc.psum_tensor("pbfD", [128, 2048], BF16))

        tr_done = 0
        ev_v = P.cnt["v"]
        for s in (range(2) if 'vtr' in dparts else ()):
            nt = SLOT_E[s] // 128
            for grp in range(nt // 8):
                half = grp % 2
                for u in range(8):
                    t_ = grp * 8 + u
                    tr_done = P.emit("tensor", lambda e, s_=s, tt=t_, hf=half, uu=u: e.transpose(
                        pbf[:, 1024 * hf + 128 * uu:1024 * hf + 128 * (uu + 1)],
                        vt_tmp[s_][:, 128 * tt:128 * (tt + 1)], c_ident[:]),
                        waits=[("v", ev_v)], incs=[("t", 1)])
                ev_v = P.emit("vector", lambda e, s_=s, g_=grp, hf=half: e.tensor_copy(
                    out=vt_s[s_][:, 1024 * g_:1024 * (g_ + 1)],
                    in_=pbf[:, 1024 * hf:1024 * (hf + 1)]),
                    waits=[("t", tr_done)], incs=[("v", 1)])
        stB.close()
        guardD = [("t", P.cnt["t"]), ("v", P.cnt["v"])]
        c_sz = ED(nc.sbuf_tensor("c_sz", [128, 2048], F32))
        dsz = P.emit("sync", lambda e: e.dma_start(out=c_sz[:], in_=sz),
                     waits=guardD, incs=[("dSZ", 16)])
        P.emit("scalar", lambda e: e.nop(), waits=[P.dw("dSZ")])
        P.emit("vector", lambda e: e.nop(), waits=[P.dw("dSZ")])
        qwb = [ED(nc.sbuf_tensor(f"qwD{i}", [128, H], I32)) for i in range(2)]
        wfo = [ED(nc.sbuf_tensor(f"wfoD{i}", [128, H], BF16)) for i in range(2)]
        wfs = [ED(nc.sbuf_tensor(f"wfsD{i}", [128, H], BF16)) for i in range(2)]
        wor = [ED(nc.sbuf_tensor(f"worD{i}", [128, H], F32)) for i in range(2)]
        wob = [ED(nc.sbuf_tensor(f"wobD{i}", [128, H], BF16)) for i in range(2)]

        prevD = {}
        last_ev = ev_v
        for ot in (range(32) if 'dq' in dparts else ()):
            pb = ot % 2
            waits = list(guardD) if ot == 0 else []
            if pb in prevD:
                waits += [("t", prevD[pb][0])]
            dl = P.emit("sync", lambda e, o_=ot, p_=pb: e.dma_start(
                out=qwb[p_][:], in_=qw[128 * o_:128 * (o_ + 1), :]),
                waits=waits, incs=[(f"dQ{pb}", 16)])
            wv = ws = 0
            for gg in range(32):
                if gg % 2 == 0:
                    ws = P.emit("scalar", lambda e, o_=ot, g_=gg, p_=pb: e.activation(
                        wfo[p_][:, 128 * g_:128 * (g_ + 1)],
                        qwb[p_][:, 128 * g_:128 * (g_ + 1)], AF.Identity,
                        scale=c_sz[:, 32 * o_ + g_:32 * o_ + g_ + 1],
                        bias=c_sz[:, 1024 + 32 * o_ + g_:1024 + 32 * o_ + g_ + 1]),
                        waits=[P.dw(f"dQ{pb}")], incs=[("s", 1)])
                else:
                    wv = P.emit("vector", lambda e, o_=ot, g_=gg, p_=pb: e.tensor_scalar(
                        out=wfo[p_][:, 128 * g_:128 * (g_ + 1)],
                        in0=qwb[p_][:, 128 * g_:128 * (g_ + 1)],
                        scalar1=c_sz[:, 32 * o_ + g_:32 * o_ + g_ + 1],
                        scalar2=c_sz[:, 1024 + 32 * o_ + g_:1024 + 32 * o_ + g_ + 1],
                        op0=ALU.mult, op1=ALU.add),
                        waits=[P.dw(f"dQ{pb}")], incs=[("v", 1)])
            trs = 0
            evl = last_ev
            for grp in range(4):
                half = grp % 2
                for u in range(8):
                    g_ = grp * 8 + u
                    trs = P.emit("tensor", lambda e, gg=g_, hf=half, uu=u, p_=pb: e.transpose(
                        pbf[:, 1024 * hf + 128 * uu:1024 * hf + 128 * (uu + 1)],
                        wfo[p_][:, 128 * gg:128 * (gg + 1)], c_ident[:]),
                        waits=[("v", max(wv, evl)), ("s", ws)], incs=[("t", 1)])
                evl = P.emit("vector", lambda e, g_=grp, hf=half, p_=pb: e.tensor_copy(
                    out=wfs[p_][:, 1024 * g_:1024 * (g_ + 1)],
                    in_=pbf[:, 1024 * hf:1024 * (hf + 1)]),
                    waits=[("t", trs), P.dw(f"dF{pb}")],
                    incs=[("v", 1)])
            last_ev = evl
            dw = P.emit("scalar", lambda e, o_=ot, p_=pb: e.dma_start(
                out=wfth[:, :, 128 * o_:128 * (o_ + 1)].rearrange("g p f -> p g f"),
                in_=wfs[p_][:].rearrange("p (g f) -> p g f", g=32)),
                waits=[("v", evl)], incs=[(f"dF{pb}", 16)])
            prevD[pb] = (trs, dw)

        prevW = {}
        prevCV = {}
        for rt in (range(32) if 'wo' in dparts else ()):
            pb = rt % 2
            dl = P.emit("sync", lambda e, r_=rt, p_=pb: e.dma_start(
                out=wor[p_][:], in_=wo[128 * r_:128 * (r_ + 1), :]),
                waits=[("v", prevCV.get(pb, 0))], incs=[(f"dW{pb}", 16)])
            cv = P.emit("vector", lambda e, p_=pb: e.tensor_copy(
                out=wob[p_][:], in_=wor[p_][:]),
                waits=[P.dw(f"dW{pb}"), P.dw(f"dH{pb}")], incs=[("v", 1)])
            dwo = P.emit("scalar", lambda e, r_=rt, p_=pb: e.dma_start(
                out=wobh[128 * r_:128 * (r_ + 1), :], in_=wob[p_][:]),
                waits=[("v", cv)], incs=[(f"dH{pb}", 16)])
            prevW[pb] = dwo
            prevCV[pb] = cv
        P.barrier()
    if stop_phase <= 4:
        for nm in taps:
            if nm.startswith("kv:"):
                w = nm.split(":")[1]
                obj = {"kt0": kt_s[0], "kt1": kt_s[1], "vt0": vt_s[0], "vt1": vt_s[1]}[w]
                _tap_sbuf(nc, P, w, obj[:], BF16)
        taps = [t for t in taps if not t.startswith("kv:")]
        _emit_taps(nc, P, taps, scratch)
        _finish_early(nc, P, out)
        P.lower(nc, st0)
        return nc

    with ExitStack() as stE:
        EE = stE.enter_context
        qts = EE(nc.sbuf_tensor("qtsE", [128, 32 * 512], BF16))
        tsb = EE(nc.sbuf_tensor("tsbE", [128, 7 * 512], BF16))
        bsl = EE(nc.sbuf_tensor("bslE", [128, NFAR * 32], F32))
        attnT = [EE(nc.sbuf_tensor(f"attnTE{i}", [128, 32 * 512], BF16)) for i in range(2)]
        wosl = [EE(nc.sbuf_tensor(f"woslE{i}", [128, H], BF16)) for i in range(2)]
        stsl = [EE(nc.sbuf_tensor(f"stslE{i}", [128, 1024], BF16)) for i in range(2)]
        wftsl = [EE(nc.sbuf_tensor(f"wftslE{i}", [128, H], BF16)) for i in range(2)]
        gtsl = EE(nc.sbuf_tensor("gtslE", [128, 1024], BF16))
        sgsl = EE(nc.sbuf_tensor("sgslE", [128, 1024], BF16))
        pb_ = [EE(nc.psum_tensor(f"pbE{i}", [128, 512], F32)) for i in range(8)]

        for s in range(2):
            bank_free = {i: ("t", 0) for i in range(8)}
            pp_free = {}
            rcb_guard = [0, 0]
            mul_guard = [0]
            acc_guard = [("t", 0), ("t", 0)]
            rmm_guard = 0
            for qc in range(2):
                q0 = Q0P[(s, qc)]
                ntk = NKT[(s, qc)]
                tcol = 1024 * s + 512 * qc
                tvar0 = (s * 2 + qc) * 7
                guard = [("t", P.cnt["t"])]
                dq_ = P.emit("sync", lambda e, tc=tcol: e.dma_start(
                    out=qts[:].rearrange("p (d q) -> p d q", d=32), in_=qth[:, tc:tc + 512].rearrange("(d p) q -> p d q", p=128)),
                    waits=guard, incs=[("eQ", 16)])
                dt_ = P.emit("sync", lambda e, tv=tvar0: e.dma_start(
                    out=tsb[:].rearrange("p (c f) -> p c f", c=7), in_=tstk[tv:tv + 7].rearrange("c p f -> p c f")),
                    incs=[("eT", 16)])
                eq_full = P.dw("eQ")
                et_full = P.dw("eT")
                sbias = {}
                for fi, h in enumerate(FAR_HEADS):
                    off = BB_OFF[(s, qc)]
                    sbias[h] = P.emit("scalar", lambda e, f_=fi, o_=off, n_=ntk, h_=h: e.activation(
                        bsl[:, 32 * f_:32 * f_ + n_], c_bbase[:, o_:o_ + n_],
                        AF.Identity, scale=SLOPES[h_]),
                        waits=[("s", P.cnt["s"])], incs=[("s", 1)])

                for g2 in range(16):
                    heads = (2 * g2, 2 * g2 + 1)
                    jmins = [_jmin(h, s, qc) for h in heads]
                    j0 = min(jmins)
                    actidx = [0, 0]
                    av_started = [False, False]
                    last_av = [0, 0]
                    last_ac = [("v", 0), ("v", 0)]
                    pend = []

                    def flush_avd():
                        for (i_p, ppi_p, jj_p, ex_p, st_p, sp_p) in pend:
                            av = P.emit("tensor", lambda e, s_=s, jj=jj_p, pi=ppi_p, i_=i_p,
                                        st_=st_p, sp_=sp_p: e.matmul(
                                pb_[4 + i_][:], vt_s[s_][:, 128 * jj:128 * (jj + 1)],
                                ppb[:, 512 * pi:512 * (pi + 1)], start=st_, stop=sp_),
                                waits=[("s", ex_p), bank_free[4 + i_p]], incs=[("t", 1)])
                            eng, sn = (("vector", "v"), ("gpsimd", "g"))[i_p]
                            if st_p:
                                ac = P.emit(eng, lambda e, pi=ppi_p, i_=i_p: e.tensor_copy(
                                    out=accA[:, 512 * i_:512 * (i_ + 1)],
                                    in_=ppb[:, 512 * pi:512 * (pi + 1)]),
                                    waits=[("s", ex_p), acc_guard[i_p]], incs=[(sn, 1)])
                            else:
                                ac = P.emit(eng, lambda e, pi=ppi_p, i_=i_p: e.tensor_tensor(
                                    out=accA[:, 512 * i_:512 * (i_ + 1)],
                                    in0=accA[:, 512 * i_:512 * (i_ + 1)],
                                    in1=ppb[:, 512 * pi:512 * (pi + 1)], op=ALU.add),
                                    waits=[("s", ex_p), last_ac[i_p]], incs=[(sn, 1)])
                            last_ac[i_p] = (sn, ac)
                            last_av[i_p] = av
                            pp_free[ppi_p] = [("t", av), last_ac[i_p]]
                        pend.clear()

                    for j in range(j0, ntk):
                        units = []
                        for i, h in enumerate(heads):
                            if j < jmins[i]:
                                continue
                            sb = 2 * i + (actidx[i] % 2)
                            ppi = 3 * i + (actidx[i] % 3)
                            near = h <= NEAR_MAX
                            diag = (128 * j - q0) >= 0
                            mm = P.emit("tensor", lambda e, s_=s, jj=j, h_=h, b_=sb,
                                        st1=not (near or diag): e.matmul(
                                pb_[b_][:], kt_s[s_][:, 128 * jj:128 * (jj + 1)],
                                qts[:, 512 * h_:512 * (h_ + 1)], start=True, stop=st1),
                                waits=[eq_full, bank_free[sb], et_full], incs=[("t", 1)])
                            if near:
                                cv = (128 * j - q0 + 384) // 128
                                mm = P.emit("tensor", lambda e, b_=sb, c_=cv, h_=h: e.matmul(
                                    pb_[b_][:], c_simh[:, 128 * h_:128 * (h_ + 1)],
                                    tsb[:, 512 * c_:512 * (c_ + 1)],
                                    start=False, stop=True), incs=[("t", 1)])
                            elif diag:
                                cv4 = (128 * j - q0) // 128
                                mm = P.emit("tensor", lambda e, b_=sb, c_=cv4: e.matmul(
                                    pb_[b_][:], c_ident[:],
                                    c_mstk[:, 512 * c_:512 * (c_ + 1)],
                                    start=False, stop=True), incs=[("t", 1)])
                            units.append((i, h, sb, ppi, mm))
                        flush_avd()
                        for (i, h, sb, ppi, mm) in units:
                            wexp = [("t", mm)]
                            if ppi in pp_free:
                                wexp.extend(pp_free[ppi])
                            if h > NEAR_MAX:
                                fi = h - NEAR_MAX - 1
                                wexp.append(("s", sbias[h]))
                                ex = P.emit("scalar", lambda e, b_=sb, pi=ppi, f_=fi, j_=j: e.activation(
                                    ppb[:, 512 * pi:512 * (pi + 1)], pb_[b_][:], AF.Exp,
                                    bias=bsl[:, 32 * f_ + j_:32 * f_ + j_ + 1]),
                                    waits=wexp, incs=[("s", 1)])
                            else:
                                ex = P.emit("scalar", lambda e, b_=sb, pi=ppi: e.activation(
                                    ppb[:, 512 * pi:512 * (pi + 1)], pb_[b_][:], AF.Exp),
                                    waits=wexp, incs=[("s", 1)])
                            bank_free[sb] = ("s", ex)
                            pend.append((i, ppi, j, ex, not av_started[i], j == ntk - 1))
                            av_started[i] = True
                            actidx[i] += 1
                    flush_avd()
                    dmms = {}
                    rcs = {}
                    rcbs = {}
                    for i in range(2):
                        dmms[i] = P.emit("tensor", lambda e, i_=i: e.matmul(
                            pb_[6 + i_][0:1, :], c_onecf[:], accA[:, 512 * i_:512 * (i_ + 1)],
                            start=True, stop=True),
                            waits=[last_ac[i], bank_free[6 + i]], incs=[("t", 1)])
                        acc_guard[i] = ("t", dmms[i])
                    for i in range(2):
                        rcs[i] = P.emit("vector", lambda e, i_=i: e.reciprocal(
                            out=rd_f[i_][:], in_=pb_[6 + i_][0:1, :]),
                            waits=[("t", dmms[i]), ("v", rcb_guard[i])], incs=[("v", 1)])
                    for i in range(2):
                        rcbs[i] = P.emit("vector", lambda e, i_=i: e.tensor_copy(
                            out=rd_b[i_][:], in_=rd_f[i_][:]),
                            waits=[("v", rcs[i]), ("t", rmm_guard)], incs=[("v", 1)])
                        rcb_guard[i] = rcbs[i]
                    prev_mul = 0
                    for i in range(2):
                        h = heads[i]
                        rmm = P.emit("tensor", lambda e, i_=i: e.matmul(
                            pb_[6 + i_][:], c_oner[:], rd_b[i_][:],
                            start=True, stop=True),
                            waits=[("v", rcbs[i])], incs=[("t", 1)])
                        rmm_guard = rmm
                        rcp = P.emit("vector", lambda e, i_=i: e.tensor_copy(
                            out=rs_f[i_][:], in_=pb_[6 + i_][:]),
                            waits=[("t", rmm), ("v", prev_mul), ("v", mul_guard[0])],
                            incs=[("v", 1)])
                        bank_free[6 + i] = ("v", rcp)
                        mul = P.emit("vector", lambda e, h_=h, i_=i, q_=qc: e.tensor_tensor(
                            out=attnT[q_][:, 512 * h_:512 * (h_ + 1)], in0=pb_[4 + i_][:],
                            in1=rs_f[i_][:], op=ALU.mult),
                            waits=[("t", last_av[i]), ("v", rcp)], incs=[("v", 1)])
                        prev_mul = mul
                        mul_guard[0] = mul
                        bank_free[4 + i] = ("v", mul)
            P.barrier()

            prevWo = {}
            evac_guard = {0: [], 1: []}
            for ot in range(32):
                pbu = ot % 2
                dl = P.emit("sync", lambda e, o_=ot, p_=pbu: e.dma_start(
                    out=wosl[p_][:].rearrange("p (j f) -> p j f", j=32),
                    in_=wobh[:, 128 * o_:128 * (o_ + 1)].rearrange("(j p) f -> p j f", p=128)),
                    waits=[("t", prevWo[pbu][1]) if pbu in prevWo else ("t", 0)],
                    incs=[(f"eW{pbu}", 16)])
                bk = 4 * pbu
                mmw = 0
                for ht in range(32):
                    for q_ in range(2):
                        mmw = P.emit("tensor", lambda e, h_=ht, qq=q_, p_=pbu, b_=bk: e.matmul(
                            pb_[b_ + qq][:], wosl[p_][:, 128 * h_:128 * (h_ + 1)],
                            attnT[qq][:, 512 * h_:512 * (h_ + 1)],
                            start=(h_ == 0), stop=(h_ == 31)),
                            waits=[P.dw(f"eW{pbu}")] + evac_guard[pbu], incs=[("t", 1)])
                e1 = P.emit("vector", lambda e, p_=pbu, b_=bk: e.tensor_copy(
                    out=stsl[p_][:, 0:512], in_=pb_[b_][:]),
                    waits=[("t", mmw), P.dw(f"eS{pbu}")], incs=[("v", 1)])
                e2 = P.emit("scalar", lambda e, p_=pbu, b_=bk: e.copy(
                    stsl[p_][:, 512:1024], pb_[b_ + 1][:]),
                    waits=[("t", mmw), P.dw(f"eS{pbu}")], incs=[("s", 1)])
                dst_ = P.emit("scalar", lambda e, o_=ot, p_=pbu, s_=s: e.dma_start(
                    out=sth[128 * o_:128 * (o_ + 1), 1024 * s_:1024 * (s_ + 1)],
                    in_=stsl[p_][:]),
                    waits=[("v", e1), ("s", e2)], incs=[(f"eS{pbu}", 16)])
                prevWo[pbu] = (dst_, mmw)
                evac_guard[pbu] = [("v", e1), ("s", e2)]
            P.barrier()

            xdF = []
            for j in range(32):
                buf = attnT[j // 16]
                coff = 1024 * (j % 16)
                d = P.emit("sync", lambda e, jj=j, b_=buf, c_=coff, s_=s: e.dma_start_transpose(
                    b_[:, c_:c_ + 1024], xnh[1024 * s_:1024 * (s_ + 1), 128 * jj:128 * (jj + 1)]),
                    incs=[("eX", 16)])
                xdF.append(d)
            ex_full = P.dw("eX")
            prevF = {}
            prev_mulf = 0
            sil_guard = {0: 0, 1: 0}
            for ot in range(32):
                pbu = ot % 2
                waits = []
                if pbu in prevF:
                    waits = [("v", prevF[pbu][1])]
                dl = P.emit("sync", lambda e, o_=ot, p_=pbu: e.dma_start(
                    out=wftsl[p_][:].rearrange("p (g f) -> p g f", g=32),
                    in_=wfth[:, :, 128 * o_:128 * (o_ + 1)].rearrange("g p f -> p g f")),
                    waits=waits, incs=[(f"eF{pbu}", 16)])
                dls = P.emit("sync", lambda e, o_=ot, p_=pbu, s_=s: e.dma_start(
                    out=stsl[p_][:], in_=sth[128 * o_:128 * (o_ + 1), 1024 * s_:1024 * (s_ + 1)]),
                    waits=[("v", prev_mulf)], incs=[(f"eR{pbu}", 16)])
                bkf = 2 + 4 * pbu
                mmf = 0
                for gg in range(32):
                    buf = attnT[gg // 16]
                    coff = 1024 * (gg % 16)
                    for q_ in range(2):
                        mmf = P.emit("tensor", lambda e, g_=gg, qq=q_, p_=pbu, b_=buf, c_=coff,
                                     bb=bkf: e.matmul(
                            pb_[bb + qq][:], wftsl[p_][:, 128 * g_:128 * (g_ + 1)],
                            b_[:, c_ + 512 * qq:c_ + 512 * (qq + 1)],
                            start=(g_ == 0), stop=(g_ == 31)),
                            waits=[P.dw(f"eF{pbu}"), ex_full, ("s", sil_guard[pbu])],
                            incs=[("t", 1)])
                sil = 0
                for q_ in range(2):
                    sil = P.emit("scalar", lambda e, o_=ot, qq=q_, bb=bkf: e.activation(
                        gtsl[:, 512 * qq:512 * (qq + 1)], pb_[bb + qq][:], AF.Silu,
                        bias=c_fb[:, o_:o_ + 1]),
                        waits=[("t", mmf), ("v", prev_mulf)], incs=[("s", 1)])
                sil_guard[pbu] = sil
                mulf = P.emit("vector", lambda e, p_=pbu: e.tensor_tensor(
                    out=sgsl[:], in0=gtsl[:], in1=stsl[p_][:], op=ALU.mult),
                    waits=[("s", sil), P.dw(f"eR{pbu}"), P.dw("eG")],
                    incs=[("v", 1)])
                prev_mulf = mulf
                dsg = P.emit("scalar", lambda e, o_=ot, s_=s: e.dma_start(
                    out=sgh[128 * o_:128 * (o_ + 1), 1024 * s_:1024 * (s_ + 1)],
                    in_=sgsl[:]),
                    waits=[("v", mulf)], incs=[("eG", 16)])
                prevF[pbu] = (dsg, mulf)
            P.barrier()

    if stop_phase <= 5:
        _emit_taps(nc, P, taps, scratch)
        _finish_early(nc, P, out)
        P.lower(nc, st0)
        return nc

    with ExitStack() as stF:
        EF = stF.enter_context
        sgt = [EF(nc.sbuf_tensor(f"sgtF{i}", [128, H], BF16)) for i in range(4)]
        xcb0 = [EF(nc.sbuf_tensor(f"xcF{i}", [128, H], F32)) for i in range(2)]
        osb0 = [EF(nc.sbuf_tensor(f"osF{i}", [128, H], F32)) for i in range(2)]
        xcb = xcb0 + xcb0
        osb = osb0 + osb0
        lastF = {}
        lastF2 = {}
        for t_ in range(16):
            pbu = t_ % 4
            wv = [("v", lastF[pbu])] if pbu in lastF else []
            wv2 = [("v", lastF2[t_ % 2])] if (t_ % 2) in lastF2 else []
            d1_ = P.emit("sync", lambda e, tt=t_, p_=pbu: e.dma_start_transpose(
                sgt[p_][:], sgh[:, 128 * tt:128 * (tt + 1)]),
                waits=wv, incs=[(f"fS{pbu}", 16)])
            d2_ = P.emit("scalar", lambda e, tt=t_, p_=pbu: e.dma_start(
                out=xcb[p_][:], in_=xq[128 * tt:128 * (tt + 1), :]),
                waits=wv2, incs=[(f"fX{t_ % 2}", 16)])
            cv = P.emit("vector", lambda e, p_=pbu: e.tensor_tensor(
                out=osb[p_][:], in0=xcb[p_][:], in1=sgt[p_][:], op=ALU.add),
                waits=[P.dw(f"fS{pbu}"), P.dw(f"fX{t_ % 2}"), P.dw(f"fO{t_ % 2}")],
                incs=[("v", 1)])
            do = P.emit("scalar", lambda e, tt=t_, p_=pbu: e.dma_start(
                out=out[128 * tt:128 * (tt + 1), :], in_=osb[p_][:]),
                waits=[("v", cv)], incs=[(f"fO{t_ % 2}", 16)])
            lastF[pbu] = cv
            lastF2[t_ % 2] = cv
        P.emit("sync", lambda e: e.nop(), waits=[P.dw("fO0"), P.dw("fO1")])


    _emit_taps(nc, P, taps, scratch)
    P.lower(nc, st0)
    st0.close()
    return nc


_NC_CACHE = [None]


def kernel(L0_attn_hidden_states, L0_attn_Wq, L0_attn_Wkv, L0_attn_Wo,
           L0_ffn_qweight, L0_ffn_scales, L0_ffn_scaled_zeros, L0_ffn_bias):
    x = np.ascontiguousarray(np.asarray(L0_attn_hidden_states, dtype=np.float32))
    wq_a = np.ascontiguousarray(np.asarray(L0_attn_Wq, dtype=np.float32))
    wkv_a = np.ascontiguousarray(np.asarray(L0_attn_Wkv, dtype=np.float32))
    wo_a = np.ascontiguousarray(np.asarray(L0_attn_Wo, dtype=np.float32))
    qw_a = np.ascontiguousarray(np.asarray(L0_ffn_qweight, dtype=np.int32).reshape(H, H))
    sc_a = np.asarray(L0_ffn_scales, dtype=np.float32).reshape(H, 32)
    zz_a = np.asarray(L0_ffn_scaled_zeros, dtype=np.float32).reshape(H, 32)
    fb_a = np.asarray(L0_ffn_bias, dtype=np.float32).reshape(H)

    sz = np.zeros((128, 2048), np.float32)
    sz[:, 0:1024] = sc_a.reshape(32, 128, 32).transpose(1, 0, 2).reshape(128, 1024)
    sz[:, 1024:2048] = zz_a.reshape(32, 128, 32).transpose(1, 0, 2).reshape(128, 1024)
    fbp = np.ascontiguousarray(fb_a.reshape(32, 128).T)

    if _NC_CACHE[0] is None:
        _NC_CACHE[0] = build_nc()
    nc = _NC_CACHE[0]

    in_maps = []
    for core in range(NCORE):
        b, par = core // 2, core % 2
        xq_c, xkv0_c, xkv1_c = _stage_core(x, b, par)
        tstk, mstk, bbase, eh2f, sel2, ident_b, simh_b = _const_tables(par)
        in_maps.append(dict(
            simh=simh_b,
            xq=xq_c, xkv0=xkv0_c, xkv1=xkv1_c, wq=wq_a, wkv=wkv_a, wo=wo_a,
            qw=qw_a, sz=sz, fb=fbp, tstk=tstk, mstk=mstk, bbase=bbase,
            eh2f=eh2f, sel2=sel2, ident=ident_b,
            onescol=np.ones((128, 1), np.float32).astype(bfdt),
            onescolf=np.ones((128, 1), np.float32),
            onesrow=np.ones((1, 128), np.float32).astype(bfdt)))

    res = run_bass_kernel_spmd(nc, in_maps, core_ids=list(range(NCORE)))

    outx = np.empty((B, S, H), np.float32)
    for core in range(NCORE):
        b, par = core // 2, core % 2
        cks = _chunks_for(par)
        oc = np.asarray(res.results[core]["out"], dtype=np.float32)
        for si, c in enumerate(cks):
            outx[b, 1024 * c:1024 * (c + 1)] = oc[1024 * si:1024 * (si + 1)]
    return outx


# version 18 (final)
# speedup vs baseline: 1.0480x; 1.0480x over previous
import math
from contextlib import ExitStack

import numpy as np
import ml_dtypes

import concourse.bass as bass
import concourse.mybir as mybir
from concourse.bass_utils import run_bass_kernel_spmd

F32 = mybir.dt.float32
BF16 = mybir.dt.bfloat16
I32 = mybir.dt.int32
AF = mybir.ActivationFunctionType
ALU = mybir.AluOpType
AX = mybir.AxisListType
bfdt = ml_dtypes.bfloat16

H = 4096
NH = 32
B = 4
S = 4096
NCORE = 8
QR = 2048
SLOT_E = (2048, 4096)
NEAR_MAX = 11
WCOEF = 36.0
KVDEPTH = 4
QSCALE = 128.0 ** -0.5

SLOPES = [2.0 ** (-(i + 1) / 4.0) for i in range(NH)]
WIN = [min(int(math.ceil(WCOEF / s)), 8192) for s in SLOPES]
Q0P = {(s, qc): SLOT_E[s] - 1024 + 512 * qc for s in range(2) for qc in range(2)}
NKT = {k: (v + 512) // 128 for k, v in Q0P.items()}
BB_OFF = {}
_o = 0
for _s in range(2):
    for _qc in range(2):
        BB_OFF[(_s, _qc)] = _o
        _o += NKT[(_s, _qc)]
NBV = _o
FAR_HEADS = [h for h in range(NH) if h > NEAR_MAX]
NFAR = len(FAR_HEADS)


def _jmin(h, s, qc):
    return max(0, (Q0P[(s, qc)] - WIN[h]) // 128)


def _chunks_for(par):
    return ([0, 3], [1, 2])[par]


def _stage_core(x, b, par):
    cks = _chunks_for(par)
    xb = x[b]
    xq = np.concatenate([xb[1024 * c:1024 * (c + 1)] for c in cks], axis=0)
    kv = []
    for s in range(2):
        e = SLOT_E[s]
        r = 1024 * (cks[s] + 1)
        lo = max(0, r - e)
        w = xb[lo:r]
        if w.shape[0] < e:
            w = np.concatenate([np.zeros((e - w.shape[0], H), np.float32), w], axis=0)
        kv.append(np.ascontiguousarray(w))
    return np.ascontiguousarray(xq), kv[0], kv[1]


def _const_tables(par):
    cks = _chunks_for(par)
    pads = [max(0, SLOT_E[s] - 1024 * (cks[s] + 1)) for s in range(2)]
    p = np.arange(128)
    f = np.arange(512)
    tstk = np.zeros((28, 128, 512), np.float32)
    i = 0
    for s in range(2):
        for qc in range(2):
            q0 = Q0P[(s, qc)]
            for cv in range(7):
                c = -384 + 128 * cv
                t = (c + p[:, None] - f[None, :]).astype(np.float32)
                bad = (c + p[:, None] > f[None, :]) | ((q0 + c + p[:, None]) < pads[s])
                t[bad] = -1e33
                tstk[i] = t
                i += 1
    mstk = np.zeros((4, 128, 512), np.float32)
    for cv in range(4):
        c = 128 * cv
        mstk[cv][(c + p[:, None]) > f[None, :]] = -1e30
    bbase = np.zeros((128, NBV), np.float32)
    for s in range(2):
        for qc in range(2):
            q0 = Q0P[(s, qc)]
            off = BB_OFF[(s, qc)]
            for j in range(NKT[(s, qc)]):
                v = (128 * j + p - (q0 + 511)).astype(np.float32)
                v[(128 * j + p) < pads[s]] = -1e30
                bbase[:, off + j] = v
    eh2f = np.zeros((128, 4), np.float32)
    eh2f[:, 0] = 1.0
    eh2f[:, 3] = 1.0
    sel2 = np.zeros((2, 256), np.float32)
    sel2[0, 0:128] = 1.0
    sel2[1, 128:256] = 1.0
    ident = np.eye(128, dtype=np.float32)
    simh = np.zeros((128, 32 * 128), np.float32)
    for h in range(NH):
        simh[:, 128 * h:128 * (h + 1)] = np.eye(128, dtype=np.float32) * SLOPES[h]
    return (tstk.astype(bfdt), mstk.astype(bfdt), bbase, eh2f, sel2.astype(bfdt),
            ident.astype(bfdt), simh.astype(bfdt))


class Prog:
    ENGINES = ("sync", "tensor", "vector", "scalar", "gpsimd")

    def __init__(self):
        self.events = []
        self.cnt = {}
        self._waited = {}

    def sem(self, *names):
        for n in names:
            self.cnt[n] = 0

    ENGSEMS = ("t", "v", "s", "g")

    def emit(self, eng, fn, waits=(), incs=()):
        w = []
        for sname, v in waits:
            if v is None or v <= 0:
                continue
            self.cnt.setdefault(sname, 0)
            assert v <= self.cnt[sname], (sname, v, self.cnt[sname])
            if sname not in self.ENGSEMS:
                assert v == self.cnt[sname], ("unsound dma wait", sname, v, self.cnt[sname])
            key = (eng, sname)
            if v > self._waited.get(key, -1):
                self._waited[key] = v
                w.append((sname, v))
        for sname, n in incs:
            self.cnt[sname] = self.cnt.setdefault(sname, 0) + n
        self.events.append((eng, tuple(w), fn, tuple(incs)))
        return self.cnt[incs[0][0]] if incs else None

    def dw(self, stream):
        return (stream, self.cnt.get(stream, 0))

    def allw(self):
        return [(sname, v) for sname, v in self.cnt.items() if v > 0]

    def barrier(self):
        snap = dict(self.cnt)
        for eng in self.ENGINES:
            self.emit(eng, lambda e: e.nop(),
                      waits=[(sname, v) for sname, v in snap.items() if v > 0])

    def lower(self, nc, sem_stack):
        sem_handles = {n: sem_stack.enter_context(nc.semaphore("sem_" + n))
                       for n in self.cnt}
        with nc.Block() as block:
            for eng_name in self.ENGINES:
                evs = [ev for ev in self.events if ev[0] == eng_name]
                if not evs:
                    continue

                def make(my_evs):
                    def body(eng):
                        for (_e, waits, fn, incs) in my_evs:
                            for (sname, v) in waits:
                                eng.wait_ge(sem_handles[sname], v)
                            ins = fn(eng)
                            for (sname, n) in incs:
                                ins.then_inc(sem_handles[sname], n)
                    return body
                getattr(block, eng_name)(make(evs))


def _emit_taps(nc, P, taps, scratch):
    for name in taps:
        src_ap = scratch[name]
        tout = nc.dram_tensor("tap_" + name, list(src_ap.shape), src_ap.dtype,
                              kind="ExternalOutput").ap()
        dn = P.emit("sync", lambda e, o_=tout, i_=src_ap: e.dma_start(out=o_, in_=i_),
                    waits=P.allw(), incs=[("tp", 16)])
        P.emit("sync", lambda e: e.nop(), waits=[("tp", dn)])


def _tap_sbuf(nc, P, name, ap, dt):
    tout = nc.dram_tensor("tap_" + name, list(ap.shape), dt, kind="ExternalOutput").ap()
    dn = P.emit("sync", lambda e: e.dma_start(out=tout, in_=ap), waits=P.allw(),
                incs=[("tp", 16)])
    P.emit("sync", lambda e: e.nop(), waits=[("tp", dn)])


def _finish_early(nc, P, out):
    with nc.sbuf_tensor("zfin", [128, H], F32) as zf:
        zv = P.emit("vector", lambda e: e.memset(zf[:], 0.0), waits=P.allw(), incs=[("v", 1)])
        dn = 0
        for t_ in range(QR // 128):
            dn = P.emit("scalar", lambda e, tt=t_: e.dma_start(
                out=out[128 * tt:128 * (tt + 1), :], in_=zf[:]),
                waits=[("v", zv)], incs=[("fz", 16)])
        P.emit("sync", lambda e: e.nop(), waits=[("fz", dn)])


def build_nc(stop_phase=9, taps=(), dparts=('vtr', 'dq', 'wo')):
    nc = bass.Bass("TRN2", target_bir_lowering=False)
    P = Prog()
    P.sem("t", "v", "s", "g")

    def din(name, shape, dt):
        return nc.dram_tensor(name, shape, dt, kind="ExternalInput").ap()

    xq = din("xq", [QR, H], F32)
    xkv0 = din("xkv0", [SLOT_E[0], H], F32)
    xkv1 = din("xkv1", [SLOT_E[1], H], F32)
    wq = din("wq", [H, H], F32)
    wkv = din("wkv", [H, 256], F32)
    wo = din("wo", [H, H], F32)
    qw = din("qw", [H, H], I32)
    sz = din("sz", [128, 2048], F32)
    fb = din("fb", [128, 32], F32)
    tstk = din("tstk", [28, 128, 512], BF16)
    mstk = din("mstk", [4, 128, 512], BF16)
    simh = din("simh", [128, 32 * 128], BF16)
    bbase = din("bbase", [128, NBV], F32)
    eh2f = din("eh2f", [128, 4], F32)
    sel2 = din("sel2", [2, 256], BF16)
    ident = din("ident", [128, 128], BF16)
    onescol = din("onescol", [128, 1], BF16)
    onescolf = din("onescolf", [128, 1], F32)
    onesrow = din("onesrow", [1, 128], BF16)

    out = nc.dram_tensor("out", [QR, H], F32, kind="ExternalOutput").ap()

    xnh = nc.dram_tensor("xnh", [QR + SLOT_E[0] + SLOT_E[1], H], BF16, kind="Internal").ap()
    qth = nc.dram_tensor("qth", [H, QR], BF16, kind="Internal").ap()
    wqbh = nc.dram_tensor("wqbh", [H, H], BF16, kind="Internal").ap()
    wobh = nc.dram_tensor("wobh", [H, H], BF16, kind="Internal").ap()
    wfth = nc.dram_tensor("wfth", [32, 128, H], BF16, kind="Internal").ap()
    sth = nc.dram_tensor("sth", [H, QR], BF16, kind="Internal").ap()
    sgh = nc.dram_tensor("sgh", [H, QR], BF16, kind="Internal").ap()
    scratch = dict(xnh=xnh, qth=qth, wqbh=wqbh, wobh=wobh, wfth=wfth, sth=sth, sgh=sgh)

    st0 = ExitStack()
    E = st0.enter_context

    c_ident = E(nc.sbuf_tensor("c_ident", [128, 128], BF16))
    c_mstk = E(nc.sbuf_tensor("c_mstk", [128, 4 * 512], BF16))
    c_simh = E(nc.sbuf_tensor("c_simh", [128, 32 * 128], BF16))
    c_bbase = E(nc.sbuf_tensor("c_bbase", [128, NBV], F32))
    c_fb = E(nc.sbuf_tensor("c_fb", [128, 32], F32))
    c_onec = E(nc.sbuf_tensor("c_onec", [128, 1], BF16))
    c_onecf = E(nc.sbuf_tensor("c_onecf", [128, 1], F32))
    c_oner = E(nc.sbuf_tensor("c_oner", [1, 128], BF16))
    kt_s = [E(nc.sbuf_tensor(f"kt{s}", [128, SLOT_E[s]], BF16)) for s in range(2)]
    vt_s = [E(nc.sbuf_tensor(f"vt{s}", [128, SLOT_E[s]], BF16)) for s in range(2)]
    ppb = E(nc.sbuf_tensor("ppb", [128, 6 * 512], BF16))
    accA = E(nc.sbuf_tensor("accA", [128, 2 * 512], F32))
    rd_f = [E(nc.sbuf_tensor(f"rd_f{i}", [1, 512], F32)) for i in range(2)]
    rd_b = [E(nc.sbuf_tensor(f"rd_b{i}", [1, 512], BF16)) for i in range(2)]
    rs_f0 = E(nc.sbuf_tensor("rs_f0", [128, 512], F32))
    rs_f = [rs_f0, rs_f0]

    with ExitStack() as stA:
        EA = stA.enter_context
        xb = [EA(nc.sbuf_tensor(f"xA{i}", [128, H], F32)) for i in range(2)]
        xnb = [EA(nc.sbuf_tensor(f"xnA{i}", [128, H], BF16)) for i in range(2)]
        sq = EA(nc.sbuf_tensor("sqA", [128, H], BF16))
        stt = [EA(nc.sbuf_tensor(f"stA{i}", [128, 8], F32)) for i in range(2)]

        for dst, src in ((c_ident, ident), (c_bbase, bbase),
                         (c_fb, fb), (c_simh, simh),
                         (c_onec, onescol), (c_oner, onesrow),
                         (c_onecf, onescolf)):
            P.emit("sync", lambda e, d=dst, s_=src: e.dma_start(out=d[:], in_=s_),
                   incs=[("aC", 16)])
        P.emit("sync", lambda e: e.dma_start(
            out=c_mstk[:].rearrange("p (c f) -> p c f", c=4), in_=mstk.rearrange("c p f -> p c f")), incs=[("aC", 16)])

        srcs = [(xq, 0, QR // 128), (xkv0, QR // 128, SLOT_E[0] // 128),
                (xkv1, (QR + SLOT_E[0]) // 128, SLOT_E[1] // 128)]
        tiles = []
        for src, base_t, n in srcs:
            for tloc in range(n):
                tiles.append((src, base_t + tloc, tloc))
        last_use = [None, None]
        xnb_guard = [0, 0]
        sq_guard = [0]
        for i, (src, trow, tloc) in enumerate(tiles):
            pb = i % 2
            waits = []
            if last_use[pb] is not None:
                waits = [("v", last_use[pb][0]), ("s", last_use[pb][1])]
            dl = P.emit("sync", lambda e, sr=src, tl=tloc, p_=pb: e.dma_start(
                out=xb[p_][:], in_=sr[128 * tl:128 * (tl + 1), :]),
                waits=waits, incs=[(f"aX{pb}", 16)])
            st_ = stt[pb]
            v1 = P.emit("vector", lambda e, p_=pb, st=st_: e.reduce_sum(
                out=st[:, 0:1], in_=xb[p_][:], axis=AX.X),
                waits=[P.dw(f"aX{pb}")], incs=[("v", 1)])
            s1g = P.emit("scalar", lambda e, p_=pb, st=st_: e.activation(
                sq[:], xb[p_][:], AF.Square, accum_out=st[:, 1:2]),
                waits=[P.dw(f"aX{pb}")], incs=[("s", 1)])
            v2 = P.emit("vector", lambda e, st=st_: e.tensor_scalar(
                out=st[:, 2:3], in0=st[:, 0:1], scalar1=1.0 / H, scalar2=None,
                op0=ALU.mult), waits=[("v", v1)], incs=[("v", 1)])
            v3 = P.emit("vector", lambda e, st=st_: e.scalar_tensor_tensor(
                out=st[:, 3:4], in0=st[:, 2:3], scalar=-1.0, in1=st[:, 2:3],
                op0=ALU.mult, op1=ALU.mult),
                waits=[("v", v2)], incs=[("v", 1)])
            v4a = P.emit("vector", lambda e, st=st_: e.scalar_tensor_tensor(
                out=st[:, 4:5], in0=st[:, 1:2], scalar=1.0 / H, in1=st[:, 3:4],
                op0=ALU.mult, op1=ALU.add),
                waits=[("s", s1g), ("v", v3)], incs=[("v", 1)])
            v4 = P.emit("vector", lambda e, st=st_: e.tensor_scalar(
                out=st[:, 4:5], in0=st[:, 4:5], scalar1=1e-5, scalar2=None,
                op0=ALU.add), waits=[("v", v4a)], incs=[("v", 1)])
            s2 = P.emit("scalar", lambda e, st=st_: e.activation(
                st[:, 5:6], st[:, 4:5], AF.Ln),
                waits=[("v", v4)], incs=[("s", 1)])
            s3 = P.emit("scalar", lambda e, st=st_: e.activation(
                st[:, 6:7], st[:, 5:6], AF.Exp, scale=-0.5),
                waits=[("s", s2)], incs=[("s", 1)])
            v5 = P.emit("vector", lambda e, st=st_: e.scalar_tensor_tensor(
                out=st[:, 7:8], in0=st[:, 2:3], scalar=-1.0, in1=st[:, 6:7],
                op0=ALU.mult, op1=ALU.mult),
                waits=[("s", s3)], incs=[("v", 1)])
            s4 = P.emit("scalar", lambda e, p_=pb, st=st_: e.activation(
                xnb[p_][:], xb[p_][:], AF.Identity, scale=st[:, 6:7], bias=st[:, 7:8]),
                waits=[("v", v5), ("s", s3), (f"aN{pb}", xnb_guard[pb])], incs=[("s", 1)])
            dxn = P.emit("scalar", lambda e, p_=pb, tr=trow: e.dma_start(
                out=xnh[128 * tr:128 * (tr + 1), :], in_=xnb[p_][:]),
                waits=[("s", s4)], incs=[(f"aN{pb}", 16)])
            last_use[pb] = (v1, s4)
            xnb_guard[pb] = dxn
        P.barrier()
    if stop_phase <= 1:
        raise_stop = True
    else:
        raise_stop = False
    if raise_stop:
        _emit_taps(nc, P, taps, scratch)
        _finish_early(nc, P, out)
        P.lower(nc, st0)
        return nc

    vt_tmp = {}
    stB = ExitStack()
    EB = stB.enter_context
    vt_tmp[0] = EB(nc.sbuf_tensor("vtt0", [128, SLOT_E[0]], BF16))
    vt_tmp[1] = EB(nc.sbuf_tensor("vtt1", [128, SLOT_E[1]], BF16))
    with ExitStack() as stB2:
        EB2 = stB2.enter_context
        xnt = [EB2(nc.sbuf_tensor(f"xntB{i}", [128, 2048], BF16)) for i in range(4)]
        wkv_raw = EB2(nc.sbuf_tensor("wkv_raw", [128, 32 * 256], F32))
        wkv_bf = EB2(nc.sbuf_tensor("wkv_bf", [128, 32 * 256], BF16))
        pbig = EB2(nc.psum_tensor("pbigB", [128, 4096], F32))
        dkv = P.emit("sync", lambda e: e.dma_start(
            out=wkv_raw[:].rearrange("p (j f) -> p j f", j=32), in_=wkv.rearrange("(j p) f -> p j f", p=128)), incs=[("bW", 16)])
        vcast = P.emit("vector", lambda e: e.tensor_copy(out=wkv_bf[:], in_=wkv_raw[:]),
                       waits=[("bW", dkv)], incs=[("v", 1)])
        mm_done = 0
        mmj_done = {}
        gj = [0]
        for s in range(2):
            e_s = SLOT_E[s]
            row0 = (QR if s == 0 else QR + SLOT_E[0])
            for half in range(e_s // 2048):
                r0h = row0 + 2048 * half
                for j in range(32):
                    pb = gj[0] % 4
                    dl = P.emit("sync", lambda e, jj=j, p_=pb, r0=r0h: e.dma_start_transpose(
                        xnt[p_][:, 0:2048], xnh[r0:r0 + 2048, 128 * jj:128 * (jj + 1)]),
                        waits=[("v", vcast), ("t", mmj_done.get(gj[0] - 4, 0))],
                        incs=[(f"bX{pb}", 16)])
                    for ch in range(4):
                        P.emit("tensor", lambda e, jj=j, p_=pb, c_=ch: e.matmul(
                            pbig[:, 512 * c_:512 * (c_ + 1)],
                            wkv_bf[:, 256 * jj:256 * jj + 128],
                            xnt[p_][:, 512 * c_:512 * (c_ + 1)],
                            start=(jj == 0), stop=(jj == 31)),
                            waits=[P.dw(f"bX{pb}")], incs=[("t", 1)])
                        mm_done = P.emit("tensor", lambda e, jj=j, p_=pb, c_=ch: e.matmul(
                            pbig[:, 2048 + 512 * c_:2048 + 512 * (c_ + 1)],
                            wkv_bf[:, 256 * jj + 128:256 * jj + 256],
                            xnt[p_][:, 512 * c_:512 * (c_ + 1)],
                            start=(jj == 0), stop=(jj == 31)),
                            incs=[("t", 1)])
                    mmj_done[gj[0]] = mm_done
                    gj[0] += 1
                wv = ws = 0
                for ch in range(4):
                    if ch % 2 == 0:
                        wv = P.emit("vector", lambda e, c_=ch, s_=s, hf=half: e.tensor_copy(
                            out=kt_s[s_][:, 2048 * hf + 512 * c_:2048 * hf + 512 * (c_ + 1)],
                            in_=pbig[:, 512 * c_:512 * (c_ + 1)]),
                            waits=[("t", mm_done)], incs=[("v", 1)])
                        wv = P.emit("vector", lambda e, c_=ch, s_=s, hf=half: e.tensor_copy(
                            out=vt_tmp[s_][:, 2048 * hf + 512 * c_:2048 * hf + 512 * (c_ + 1)],
                            in_=pbig[:, 2048 + 512 * c_:2048 + 512 * (c_ + 1)]),
                            waits=[("t", mm_done)], incs=[("v", 1)])
                    else:
                        ws = P.emit("scalar", lambda e, c_=ch, s_=s, hf=half: e.copy(
                            kt_s[s_][:, 2048 * hf + 512 * c_:2048 * hf + 512 * (c_ + 1)],
                            pbig[:, 512 * c_:512 * (c_ + 1)]),
                            waits=[("t", mm_done)], incs=[("s", 1)])
                        ws = P.emit("scalar", lambda e, c_=ch, s_=s, hf=half: e.copy(
                            vt_tmp[s_][:, 2048 * hf + 512 * c_:2048 * hf + 512 * (c_ + 1)],
                            pbig[:, 2048 + 512 * c_:2048 + 512 * (c_ + 1)]),
                            waits=[("t", mm_done)], incs=[("s", 1)])
                P.emit("tensor", lambda e: e.nop(), waits=[("v", wv), ("s", ws)])
        P.barrier()
    if stop_phase <= 2:
        for nm in taps:
            if nm.startswith("kv:"):
                w = nm.split(":")[1]
                obj = {"kt0": kt_s[0], "kt1": kt_s[1], "vtt0": vt_tmp[0], "vtt1": vt_tmp[1]}[w]
                _tap_sbuf(nc, P, w, obj[:], BF16)
        taps = [t for t in taps if not t.startswith("kv:")]
        _emit_taps(nc, P, taps, scratch)
        _finish_early(nc, P, out)
        P.lower(nc, st0)
        return nc

    with ExitStack() as stC:
        EC = stC.enter_context
        xntq = [EC(nc.sbuf_tensor(f"xntqC{j}", [128, 1024], BF16)) for j in range(32)]
        wqr = [EC(nc.sbuf_tensor(f"wqrC{i}", [128, H], F32)) for i in range(2)]
        wqb = [EC(nc.sbuf_tensor(f"wqbC{i}", [128, H], BF16)) for i in range(2)]
        qst = [EC(nc.sbuf_tensor(f"qstC{i}", [128, 1024], BF16)) for i in range(2)]
        pq = EC(nc.psum_tensor("pqC", [128, 4096], F32))

        prev = {}
        for half in range(2):
            tcol = 1024 * half
            xd = []
            tguard = P.cnt["t"]
            for j in range(32):
                d = P.emit("sync", lambda e, jj=j, tc=tcol: e.dma_start_transpose(
                    xntq[jj][:], xnh[tc:tc + 1024, 128 * jj:128 * (jj + 1)]),
                    waits=[("t", tguard)], incs=[("cQ", 16)])
                xd.append(d)
            cq_full = P.dw("cQ")
            for dt in range(32):
                pb = dt % 2
                waits = []
                if pb in prev:
                    waits = [("v", prev[pb][1]), ("s", prev[pb][2]), P.dw(f"cQT{pb}")]
                if half == 0:
                    dl = P.emit("sync", lambda e, d_=dt, p_=pb: e.dma_start(
                        out=wqr[p_][:].rearrange("p (j f) -> p j f", j=32),
                        in_=wq[:, 128 * d_:128 * (d_ + 1)].rearrange("(j p) f -> p j f", p=128)),
                        waits=waits, incs=[(f"cW{pb}", 16)])
                    cv = P.emit("vector", lambda e, p_=pb: e.tensor_scalar(
                        out=wqb[p_][:], in0=wqr[p_][:], scalar1=QSCALE, scalar2=None,
                        op0=ALU.mult),
                        waits=[P.dw(f"cW{pb}"), P.dw(f"cWB{pb}")] + waits, incs=[("v", 1)])
                    P.emit("scalar", lambda e, d_=dt, p_=pb: e.dma_start(
                        out=wqbh[:, 128 * d_:128 * (d_ + 1)].rearrange("(j p) f -> p j f", p=128),
                        in_=wqb[p_][:].rearrange("p (j f) -> p j f", j=32)),
                        waits=[("v", cv)], incs=[(f"cWB{pb}", 16)])
                    ready = [("v", cv)]
                else:
                    wb_guard = [P.dw("cWB0"), P.dw("cWB1")] if dt == 0 else []
                    dl = P.emit("sync", lambda e, d_=dt, p_=pb: e.dma_start(
                        out=wqb[p_][:].rearrange("p (j f) -> p j f", j=32),
                        in_=wqbh[:, 128 * d_:128 * (d_ + 1)].rearrange("(j p) f -> p j f", p=128)),
                        waits=waits + wb_guard, incs=[(f"cW{pb}", 16)])
                    ready = [P.dw(f"cW{pb}")]
                mm_done = 0
                for j in range(32):
                    for ch in range(2):
                        mm_done = P.emit("tensor", lambda e, jj=j, c_=ch, p_=pb: e.matmul(
                            pq[:, 2048 * p_ + 512 * c_:2048 * p_ + 512 * (c_ + 1)],
                            wqb[p_][:, 128 * jj:128 * (jj + 1)],
                            xntq[jj][:, 512 * c_:512 * (c_ + 1)],
                            start=(jj == 0), stop=(jj == 31)),
                            waits=ready + [cq_full], incs=[("t", 1)])
                evv = P.emit("vector", lambda e, p_=pb: e.tensor_copy(
                    out=qst[p_][:, 0:512], in_=pq[:, 2048 * p_:2048 * p_ + 512]),
                    waits=[("t", mm_done)], incs=[("v", 1)])
                evs = P.emit("scalar", lambda e, p_=pb: e.copy(
                    qst[p_][:, 512:1024], pq[:, 2048 * p_ + 512:2048 * p_ + 1024]),
                    waits=[("t", mm_done)], incs=[("s", 1)])
                dq = P.emit("scalar", lambda e, d_=dt, p_=pb, tc=tcol: e.dma_start(
                    out=qth[128 * d_:128 * (d_ + 1), tc:tc + 1024], in_=qst[p_][:]),
                    waits=[("v", evv), ("s", evs)], incs=[(f"cQT{pb}", 16)])
                prev[pb] = (mm_done, evv, evs, dq)
        P.barrier()
    if stop_phase <= 3:
        _emit_taps(nc, P, taps, scratch)
        _finish_early(nc, P, out)
        P.lower(nc, st0)
        return nc

    with ExitStack() as stD:
        ED = stD.enter_context
        pbf = ED(nc.psum_tensor("pbfD", [128, 2048], BF16))

        tr_done = 0
        ev_v = P.cnt["v"]
        for s in (range(2) if 'vtr' in dparts else ()):
            nt = SLOT_E[s] // 128
            for grp in range(nt // 8):
                half = grp % 2
                for u in range(8):
                    t_ = grp * 8 + u
                    tr_done = P.emit("tensor", lambda e, s_=s, tt=t_, hf=half, uu=u: e.transpose(
                        pbf[:, 1024 * hf + 128 * uu:1024 * hf + 128 * (uu + 1)],
                        vt_tmp[s_][:, 128 * tt:128 * (tt + 1)], c_ident[:]),
                        waits=[("v", ev_v)], incs=[("t", 1)])
                ev_v = P.emit("vector", lambda e, s_=s, g_=grp, hf=half: e.tensor_copy(
                    out=vt_s[s_][:, 1024 * g_:1024 * (g_ + 1)],
                    in_=pbf[:, 1024 * hf:1024 * (hf + 1)]),
                    waits=[("t", tr_done)], incs=[("v", 1)])
        stB.close()
        guardD = [("t", P.cnt["t"]), ("v", P.cnt["v"])]
        c_sz = ED(nc.sbuf_tensor("c_sz", [128, 2048], F32))
        dsz = P.emit("sync", lambda e: e.dma_start(out=c_sz[:], in_=sz),
                     waits=guardD, incs=[("dSZ", 16)])
        P.emit("scalar", lambda e: e.nop(), waits=[P.dw("dSZ")])
        P.emit("vector", lambda e: e.nop(), waits=[P.dw("dSZ")])
        qwb = [ED(nc.sbuf_tensor(f"qwD{i}", [128, H], I32)) for i in range(2)]
        wfo = [ED(nc.sbuf_tensor(f"wfoD{i}", [128, H], BF16)) for i in range(2)]
        wfs = [ED(nc.sbuf_tensor(f"wfsD{i}", [128, H], BF16)) for i in range(2)]
        wor = [ED(nc.sbuf_tensor(f"worD{i}", [128, H], F32)) for i in range(2)]
        wob = [ED(nc.sbuf_tensor(f"wobD{i}", [128, H], BF16)) for i in range(2)]

        prevD = {}
        last_ev = ev_v
        for ot in (range(32) if 'dq' in dparts else ()):
            pb = ot % 2
            waits = list(guardD) if ot == 0 else []
            if pb in prevD:
                waits += [("t", prevD[pb][0])]
            dl = P.emit("sync", lambda e, o_=ot, p_=pb: e.dma_start(
                out=qwb[p_][:], in_=qw[128 * o_:128 * (o_ + 1), :]),
                waits=waits, incs=[(f"dQ{pb}", 16)])
            wv = ws = 0
            for gg in range(32):
                if gg % 2 == 0:
                    ws = P.emit("scalar", lambda e, o_=ot, g_=gg, p_=pb: e.activation(
                        wfo[p_][:, 128 * g_:128 * (g_ + 1)],
                        qwb[p_][:, 128 * g_:128 * (g_ + 1)], AF.Identity,
                        scale=c_sz[:, 32 * o_ + g_:32 * o_ + g_ + 1],
                        bias=c_sz[:, 1024 + 32 * o_ + g_:1024 + 32 * o_ + g_ + 1]),
                        waits=[P.dw(f"dQ{pb}")], incs=[("s", 1)])
                else:
                    wv = P.emit("vector", lambda e, o_=ot, g_=gg, p_=pb: e.tensor_scalar(
                        out=wfo[p_][:, 128 * g_:128 * (g_ + 1)],
                        in0=qwb[p_][:, 128 * g_:128 * (g_ + 1)],
                        scalar1=c_sz[:, 32 * o_ + g_:32 * o_ + g_ + 1],
                        scalar2=c_sz[:, 1024 + 32 * o_ + g_:1024 + 32 * o_ + g_ + 1],
                        op0=ALU.mult, op1=ALU.add),
                        waits=[P.dw(f"dQ{pb}")], incs=[("v", 1)])
            trs = 0
            evl = last_ev
            for grp in range(4):
                half = grp % 2
                for u in range(8):
                    g_ = grp * 8 + u
                    trs = P.emit("tensor", lambda e, gg=g_, hf=half, uu=u, p_=pb: e.transpose(
                        pbf[:, 1024 * hf + 128 * uu:1024 * hf + 128 * (uu + 1)],
                        wfo[p_][:, 128 * gg:128 * (gg + 1)], c_ident[:]),
                        waits=[("v", max(wv, evl)), ("s", ws)], incs=[("t", 1)])
                evl = P.emit("vector", lambda e, g_=grp, hf=half, p_=pb: e.tensor_copy(
                    out=wfs[p_][:, 1024 * g_:1024 * (g_ + 1)],
                    in_=pbf[:, 1024 * hf:1024 * (hf + 1)]),
                    waits=[("t", trs), P.dw(f"dF{pb}")],
                    incs=[("v", 1)])
            last_ev = evl
            dw = P.emit("scalar", lambda e, o_=ot, p_=pb: e.dma_start(
                out=wfth[:, :, 128 * o_:128 * (o_ + 1)].rearrange("g p f -> p g f"),
                in_=wfs[p_][:].rearrange("p (g f) -> p g f", g=32)),
                waits=[("v", evl)], incs=[(f"dF{pb}", 16)])
            prevD[pb] = (trs, dw)

        prevW = {}
        prevCV = {}
        for rt in (range(32) if 'wo' in dparts else ()):
            pb = rt % 2
            dl = P.emit("sync", lambda e, r_=rt, p_=pb: e.dma_start(
                out=wor[p_][:], in_=wo[128 * r_:128 * (r_ + 1), :]),
                waits=[("v", prevCV.get(pb, 0))], incs=[(f"dW{pb}", 16)])
            cv = P.emit("vector", lambda e, p_=pb: e.tensor_copy(
                out=wob[p_][:], in_=wor[p_][:]),
                waits=[P.dw(f"dW{pb}"), P.dw(f"dH{pb}")], incs=[("v", 1)])
            dwo = P.emit("scalar", lambda e, r_=rt, p_=pb: e.dma_start(
                out=wobh[128 * r_:128 * (r_ + 1), :], in_=wob[p_][:]),
                waits=[("v", cv)], incs=[(f"dH{pb}", 16)])
            prevW[pb] = dwo
            prevCV[pb] = cv
        P.barrier()
    if stop_phase <= 4:
        for nm in taps:
            if nm.startswith("kv:"):
                w = nm.split(":")[1]
                obj = {"kt0": kt_s[0], "kt1": kt_s[1], "vt0": vt_s[0], "vt1": vt_s[1]}[w]
                _tap_sbuf(nc, P, w, obj[:], BF16)
        taps = [t for t in taps if not t.startswith("kv:")]
        _emit_taps(nc, P, taps, scratch)
        _finish_early(nc, P, out)
        P.lower(nc, st0)
        return nc

    with ExitStack() as stE:
        EE = stE.enter_context
        qts = EE(nc.sbuf_tensor("qtsE", [128, 32 * 512], BF16))
        tsb = EE(nc.sbuf_tensor("tsbE", [128, 7 * 512], BF16))
        bsl = EE(nc.sbuf_tensor("bslE", [128, NFAR * 32], F32))
        attnT = [EE(nc.sbuf_tensor(f"attnTE{i}", [128, 32 * 512], BF16)) for i in range(2)]
        wosl = [EE(nc.sbuf_tensor(f"woslE{i}", [128, H], BF16)) for i in range(2)]
        stsl = [EE(nc.sbuf_tensor(f"stslE{i}", [128, 1024], BF16)) for i in range(2)]
        wftsl = [EE(nc.sbuf_tensor(f"wftslE{i}", [128, H], BF16)) for i in range(2)]
        gtsl = EE(nc.sbuf_tensor("gtslE", [128, 1024], BF16))
        sgsl = EE(nc.sbuf_tensor("sgslE", [128, 1024], BF16))
        pb_ = [EE(nc.psum_tensor(f"pbE{i}", [128, 512], F32)) for i in range(8)]

        for s in range(2):
            bank_free = {i: ("t", 0) for i in range(8)}
            pp_free = {}
            rcb_guard = [0, 0]
            mul_guard = [0]
            acc_guard = [("t", 0), ("t", 0)]
            rmm_guard = 0
            for qc in range(2):
                q0 = Q0P[(s, qc)]
                ntk = NKT[(s, qc)]
                tcol = 1024 * s + 512 * qc
                tvar0 = (s * 2 + qc) * 7
                guard = [("t", P.cnt["t"])]
                dq_ = P.emit("sync", lambda e, tc=tcol: e.dma_start(
                    out=qts[:].rearrange("p (d q) -> p d q", d=32), in_=qth[:, tc:tc + 512].rearrange("(d p) q -> p d q", p=128)),
                    waits=guard, incs=[("eQ", 16)])
                dt_ = P.emit("sync", lambda e, tv=tvar0: e.dma_start(
                    out=tsb[:].rearrange("p (c f) -> p c f", c=7), in_=tstk[tv:tv + 7].rearrange("c p f -> p c f")),
                    incs=[("eT", 16)])
                eq_full = P.dw("eQ")
                et_full = P.dw("eT")
                sbias = {}
                for fi, h in enumerate(FAR_HEADS):
                    off = BB_OFF[(s, qc)]
                    sbias[h] = P.emit("scalar", lambda e, f_=fi, o_=off, n_=ntk, h_=h: e.activation(
                        bsl[:, 32 * f_:32 * f_ + n_], c_bbase[:, o_:o_ + n_],
                        AF.Identity, scale=SLOPES[h_]),
                        waits=[("s", P.cnt["s"])], incs=[("s", 1)])

                for g2 in range(16):
                    heads = (2 * g2, 2 * g2 + 1)
                    jmins = [_jmin(h, s, qc) for h in heads]
                    j0 = min(jmins)
                    actidx = [0, 0]
                    av_started = [False, False]
                    last_av = [0, 0]
                    last_ac = [("v", 0), ("v", 0)]
                    pend = []

                    def flush_avd():
                        for (i_p, ppi_p, jj_p, ex_p, st_p, sp_p) in pend:
                            av = P.emit("tensor", lambda e, s_=s, jj=jj_p, pi=ppi_p, i_=i_p,
                                        st_=st_p, sp_=sp_p: e.matmul(
                                pb_[4 + i_][:], vt_s[s_][:, 128 * jj:128 * (jj + 1)],
                                ppb[:, 512 * pi:512 * (pi + 1)], start=st_, stop=sp_),
                                waits=[("s", ex_p), bank_free[4 + i_p]], incs=[("t", 1)])
                            eng, sn = (("vector", "v"), ("gpsimd", "g"))[i_p]
                            if st_p:
                                ac = P.emit(eng, lambda e, pi=ppi_p, i_=i_p: e.tensor_copy(
                                    out=accA[:, 512 * i_:512 * (i_ + 1)],
                                    in_=ppb[:, 512 * pi:512 * (pi + 1)]),
                                    waits=[("s", ex_p), acc_guard[i_p]], incs=[(sn, 1)])
                            else:
                                ac = P.emit(eng, lambda e, pi=ppi_p, i_=i_p: e.tensor_tensor(
                                    out=accA[:, 512 * i_:512 * (i_ + 1)],
                                    in0=accA[:, 512 * i_:512 * (i_ + 1)],
                                    in1=ppb[:, 512 * pi:512 * (pi + 1)], op=ALU.add),
                                    waits=[("s", ex_p), last_ac[i_p]], incs=[(sn, 1)])
                            last_ac[i_p] = (sn, ac)
                            last_av[i_p] = av
                            pp_free[ppi_p] = [("t", av), last_ac[i_p]]
                        pend.clear()

                    for j in range(j0, ntk):
                        units = []
                        for i, h in enumerate(heads):
                            if j < jmins[i]:
                                continue
                            sb = 2 * i + (actidx[i] % 2)
                            ppi = 3 * i + (actidx[i] % 3)
                            near = h <= NEAR_MAX
                            diag = (128 * j - q0) >= 0
                            mm = P.emit("tensor", lambda e, s_=s, jj=j, h_=h, b_=sb,
                                        st1=not (near or diag): e.matmul(
                                pb_[b_][:], kt_s[s_][:, 128 * jj:128 * (jj + 1)],
                                qts[:, 512 * h_:512 * (h_ + 1)], start=True, stop=st1),
                                waits=[eq_full, bank_free[sb], et_full], incs=[("t", 1)])
                            if near:
                                cv = (128 * j - q0 + 384) // 128
                                mm = P.emit("tensor", lambda e, b_=sb, c_=cv, h_=h: e.matmul(
                                    pb_[b_][:], c_simh[:, 128 * h_:128 * (h_ + 1)],
                                    tsb[:, 512 * c_:512 * (c_ + 1)],
                                    start=False, stop=True), incs=[("t", 1)])
                            elif diag:
                                cv4 = (128 * j - q0) // 128
                                mm = P.emit("tensor", lambda e, b_=sb, c_=cv4: e.matmul(
                                    pb_[b_][:], c_ident[:],
                                    c_mstk[:, 512 * c_:512 * (c_ + 1)],
                                    start=False, stop=True), incs=[("t", 1)])
                            units.append((i, h, sb, ppi, mm))
                        flush_avd()
                        for (i, h, sb, ppi, mm) in units:
                            wexp = [("t", mm)]
                            if ppi in pp_free:
                                wexp.extend(pp_free[ppi])
                            if h > NEAR_MAX:
                                fi = h - NEAR_MAX - 1
                                wexp.append(("s", sbias[h]))
                                ex = P.emit("scalar", lambda e, b_=sb, pi=ppi, f_=fi, j_=j: e.activation(
                                    ppb[:, 512 * pi:512 * (pi + 1)], pb_[b_][:], AF.Exp,
                                    bias=bsl[:, 32 * f_ + j_:32 * f_ + j_ + 1]),
                                    waits=wexp, incs=[("s", 1)])
                            else:
                                ex = P.emit("scalar", lambda e, b_=sb, pi=ppi: e.activation(
                                    ppb[:, 512 * pi:512 * (pi + 1)], pb_[b_][:], AF.Exp),
                                    waits=wexp, incs=[("s", 1)])
                            bank_free[sb] = ("s", ex)
                            pend.append((i, ppi, j, ex, not av_started[i], j == ntk - 1))
                            av_started[i] = True
                            actidx[i] += 1
                    flush_avd()
                    dmms = {}
                    rcs = {}
                    rcbs = {}
                    for i in range(2):
                        dmms[i] = P.emit("tensor", lambda e, i_=i: e.matmul(
                            pb_[6 + i_][0:1, :], c_onecf[:], accA[:, 512 * i_:512 * (i_ + 1)],
                            start=True, stop=True),
                            waits=[last_ac[i], bank_free[6 + i]], incs=[("t", 1)])
                        acc_guard[i] = ("t", dmms[i])
                    for i in range(2):
                        rcs[i] = P.emit("vector", lambda e, i_=i: e.reciprocal(
                            out=rd_f[i_][:], in_=pb_[6 + i_][0:1, :]),
                            waits=[("t", dmms[i]), ("v", rcb_guard[i])], incs=[("v", 1)])
                    for i in range(2):
                        rcbs[i] = P.emit("vector", lambda e, i_=i: e.tensor_copy(
                            out=rd_b[i_][:], in_=rd_f[i_][:]),
                            waits=[("v", rcs[i]), ("t", rmm_guard)], incs=[("v", 1)])
                        rcb_guard[i] = rcbs[i]
                    prev_mul = 0
                    for i in range(2):
                        h = heads[i]
                        rmm = P.emit("tensor", lambda e, i_=i: e.matmul(
                            pb_[6 + i_][:], c_oner[:], rd_b[i_][:],
                            start=True, stop=True),
                            waits=[("v", rcbs[i])], incs=[("t", 1)])
                        rmm_guard = rmm
                        rcp = P.emit("vector", lambda e, i_=i: e.tensor_copy(
                            out=rs_f[i_][:], in_=pb_[6 + i_][:]),
                            waits=[("t", rmm), ("v", prev_mul), ("v", mul_guard[0])],
                            incs=[("v", 1)])
                        bank_free[6 + i] = ("v", rcp)
                        mul = P.emit("vector", lambda e, h_=h, i_=i, q_=qc: e.tensor_tensor(
                            out=attnT[q_][:, 512 * h_:512 * (h_ + 1)], in0=pb_[4 + i_][:],
                            in1=rs_f[i_][:], op=ALU.mult),
                            waits=[("t", last_av[i]), ("v", rcp)], incs=[("v", 1)])
                        prev_mul = mul
                        mul_guard[0] = mul
                        bank_free[4 + i] = ("v", mul)
            P.barrier()

            prevWo = {}
            evac_guard = {0: [], 1: []}
            for ot in range(32):
                pbu = ot % 2
                dl = P.emit("sync", lambda e, o_=ot, p_=pbu: e.dma_start(
                    out=wosl[p_][:].rearrange("p (j f) -> p j f", j=32),
                    in_=wobh[:, 128 * o_:128 * (o_ + 1)].rearrange("(j p) f -> p j f", p=128)),
                    waits=[("t", prevWo[pbu][1]) if pbu in prevWo else ("t", 0)],
                    incs=[(f"eW{pbu}", 16)])
                bk = 4 * pbu
                mmw = 0
                for ht in range(32):
                    for q_ in range(2):
                        mmw = P.emit("tensor", lambda e, h_=ht, qq=q_, p_=pbu, b_=bk: e.matmul(
                            pb_[b_ + qq][:], wosl[p_][:, 128 * h_:128 * (h_ + 1)],
                            attnT[qq][:, 512 * h_:512 * (h_ + 1)],
                            start=(h_ == 0), stop=(h_ == 31)),
                            waits=[P.dw(f"eW{pbu}")] + evac_guard[pbu], incs=[("t", 1)])
                e1 = P.emit("vector", lambda e, p_=pbu, b_=bk: e.tensor_copy(
                    out=stsl[p_][:, 0:512], in_=pb_[b_][:]),
                    waits=[("t", mmw), P.dw(f"eS{pbu}")], incs=[("v", 1)])
                e2 = P.emit("scalar", lambda e, p_=pbu, b_=bk: e.copy(
                    stsl[p_][:, 512:1024], pb_[b_ + 1][:]),
                    waits=[("t", mmw), P.dw(f"eS{pbu}")], incs=[("s", 1)])
                dst_ = P.emit("scalar", lambda e, o_=ot, p_=pbu, s_=s: e.dma_start(
                    out=sth[128 * o_:128 * (o_ + 1), 1024 * s_:1024 * (s_ + 1)],
                    in_=stsl[p_][:]),
                    waits=[("v", e1), ("s", e2)], incs=[(f"eS{pbu}", 16)])
                prevWo[pbu] = (dst_, mmw)
                evac_guard[pbu] = [("v", e1), ("s", e2)]
            P.barrier()

            xdF = []
            for j in range(32):
                buf = attnT[j // 16]
                coff = 1024 * (j % 16)
                d = P.emit("sync", lambda e, jj=j, b_=buf, c_=coff, s_=s: e.dma_start_transpose(
                    b_[:, c_:c_ + 1024], xnh[1024 * s_:1024 * (s_ + 1), 128 * jj:128 * (jj + 1)]),
                    incs=[("eX", 16)])
                xdF.append(d)
            ex_full = P.dw("eX")
            prevF = {}
            prev_mulf = 0
            sil_guard = {0: 0, 1: 0}
            for ot in range(32):
                pbu = ot % 2
                waits = []
                if pbu in prevF:
                    waits = [("v", prevF[pbu][1])]
                dl = P.emit("sync", lambda e, o_=ot, p_=pbu: e.dma_start(
                    out=wftsl[p_][:].rearrange("p (g f) -> p g f", g=32),
                    in_=wfth[:, :, 128 * o_:128 * (o_ + 1)].rearrange("g p f -> p g f")),
                    waits=waits, incs=[(f"eF{pbu}", 16)])
                dls = P.emit("sync", lambda e, o_=ot, p_=pbu, s_=s: e.dma_start(
                    out=stsl[p_][:], in_=sth[128 * o_:128 * (o_ + 1), 1024 * s_:1024 * (s_ + 1)]),
                    waits=[("v", prev_mulf)], incs=[(f"eR{pbu}", 16)])
                bkf = 2 + 4 * pbu
                mmf = 0
                for gg in range(32):
                    buf = attnT[gg // 16]
                    coff = 1024 * (gg % 16)
                    for q_ in range(2):
                        mmf = P.emit("tensor", lambda e, g_=gg, qq=q_, p_=pbu, b_=buf, c_=coff,
                                     bb=bkf: e.matmul(
                            pb_[bb + qq][:], wftsl[p_][:, 128 * g_:128 * (g_ + 1)],
                            b_[:, c_ + 512 * qq:c_ + 512 * (qq + 1)],
                            start=(g_ == 0), stop=(g_ == 31)),
                            waits=[P.dw(f"eF{pbu}"), ex_full, ("s", sil_guard[pbu])],
                            incs=[("t", 1)])
                sil = 0
                for q_ in range(2):
                    sil = P.emit("scalar", lambda e, o_=ot, qq=q_, bb=bkf: e.activation(
                        gtsl[:, 512 * qq:512 * (qq + 1)], pb_[bb + qq][:], AF.Silu,
                        bias=c_fb[:, o_:o_ + 1]),
                        waits=[("t", mmf), ("v", prev_mulf)], incs=[("s", 1)])
                sil_guard[pbu] = sil
                mulf = P.emit("vector", lambda e, p_=pbu: e.tensor_tensor(
                    out=sgsl[:], in0=gtsl[:], in1=stsl[p_][:], op=ALU.mult),
                    waits=[("s", sil), P.dw(f"eR{pbu}"), P.dw("eG")],
                    incs=[("v", 1)])
                prev_mulf = mulf
                dsg = P.emit("scalar", lambda e, o_=ot, s_=s: e.dma_start(
                    out=sgh[128 * o_:128 * (o_ + 1), 1024 * s_:1024 * (s_ + 1)],
                    in_=sgsl[:]),
                    waits=[("v", mulf)], incs=[("eG", 16)])
                prevF[pbu] = (dsg, mulf)
            P.barrier()

    if stop_phase <= 5:
        _emit_taps(nc, P, taps, scratch)
        _finish_early(nc, P, out)
        P.lower(nc, st0)
        return nc

    with ExitStack() as stF:
        EF = stF.enter_context
        sgt = [EF(nc.sbuf_tensor(f"sgtF{i}", [128, H], BF16)) for i in range(4)]
        xcb0 = [EF(nc.sbuf_tensor(f"xcF{i}", [128, H], F32)) for i in range(2)]
        osb0 = [EF(nc.sbuf_tensor(f"osF{i}", [128, H], F32)) for i in range(2)]
        xcb = xcb0 + xcb0
        osb = osb0 + osb0
        lastF = {}
        lastF2 = {}
        for t_ in range(16):
            pbu = t_ % 4
            wv = [("v", lastF[pbu])] if pbu in lastF else []
            wv2 = [("v", lastF2[t_ % 2])] if (t_ % 2) in lastF2 else []
            d1_ = P.emit("sync", lambda e, tt=t_, p_=pbu: e.dma_start_transpose(
                sgt[p_][:], sgh[:, 128 * tt:128 * (tt + 1)]),
                waits=wv, incs=[(f"fS{pbu}", 16)])
            d2_ = P.emit("scalar", lambda e, tt=t_, p_=pbu: e.dma_start(
                out=xcb[p_][:], in_=xq[128 * tt:128 * (tt + 1), :]),
                waits=wv2, incs=[(f"fX{t_ % 2}", 16)])
            cv = P.emit("vector", lambda e, p_=pbu: e.tensor_tensor(
                out=osb[p_][:], in0=xcb[p_][:], in1=sgt[p_][:], op=ALU.add),
                waits=[P.dw(f"fS{pbu}"), P.dw(f"fX{t_ % 2}"), P.dw(f"fO{t_ % 2}")],
                incs=[("v", 1)])
            do = P.emit("scalar", lambda e, tt=t_, p_=pbu: e.dma_start(
                out=out[128 * tt:128 * (tt + 1), :], in_=osb[p_][:]),
                waits=[("v", cv)], incs=[(f"fO{t_ % 2}", 16)])
            lastF[pbu] = cv
            lastF2[t_ % 2] = cv
        P.emit("sync", lambda e: e.nop(), waits=[P.dw("fO0"), P.dw("fO1")])


    _emit_taps(nc, P, taps, scratch)
    P.lower(nc, st0)
    st0.close()
    return nc


_NC_CACHE = [None]


def kernel(L0_attn_hidden_states, L0_attn_Wq, L0_attn_Wkv, L0_attn_Wo,
           L0_ffn_qweight, L0_ffn_scales, L0_ffn_scaled_zeros, L0_ffn_bias):
    x = np.ascontiguousarray(np.asarray(L0_attn_hidden_states, dtype=np.float32))
    wq_a = np.ascontiguousarray(np.asarray(L0_attn_Wq, dtype=np.float32))
    wkv_a = np.ascontiguousarray(np.asarray(L0_attn_Wkv, dtype=np.float32))
    wo_a = np.ascontiguousarray(np.asarray(L0_attn_Wo, dtype=np.float32))
    qw_a = np.ascontiguousarray(np.asarray(L0_ffn_qweight, dtype=np.int32).reshape(H, H))
    sc_a = np.asarray(L0_ffn_scales, dtype=np.float32).reshape(H, 32)
    zz_a = np.asarray(L0_ffn_scaled_zeros, dtype=np.float32).reshape(H, 32)
    fb_a = np.asarray(L0_ffn_bias, dtype=np.float32).reshape(H)

    sz = np.zeros((128, 2048), np.float32)
    sz[:, 0:1024] = sc_a.reshape(32, 128, 32).transpose(1, 0, 2).reshape(128, 1024)
    sz[:, 1024:2048] = zz_a.reshape(32, 128, 32).transpose(1, 0, 2).reshape(128, 1024)
    fbp = np.ascontiguousarray(fb_a.reshape(32, 128).T)

    if _NC_CACHE[0] is None:
        _NC_CACHE[0] = build_nc()
    nc = _NC_CACHE[0]

    in_maps = []
    for core in range(NCORE):
        b, par = core // 2, core % 2
        xq_c, xkv0_c, xkv1_c = _stage_core(x, b, par)
        tstk, mstk, bbase, eh2f, sel2, ident_b, simh_b = _const_tables(par)
        in_maps.append(dict(
            simh=simh_b,
            xq=xq_c, xkv0=xkv0_c, xkv1=xkv1_c, wq=wq_a, wkv=wkv_a, wo=wo_a,
            qw=qw_a, sz=sz, fb=fbp, tstk=tstk, mstk=mstk, bbase=bbase,
            eh2f=eh2f, sel2=sel2, ident=ident_b,
            onescol=np.ones((128, 1), np.float32).astype(bfdt),
            onescolf=np.ones((128, 1), np.float32),
            onesrow=np.ones((1, 128), np.float32).astype(bfdt)))

    res = run_bass_kernel_spmd(nc, in_maps, core_ids=list(range(NCORE)))

    outx = np.empty((B, S, H), np.float32)
    for core in range(NCORE):
        b, par = core // 2, core % 2
        cks = _chunks_for(par)
        oc = np.asarray(res.results[core]["out"], dtype=np.float32)
        for si, c in enumerate(cks):
            outx[b, 1024 * c:1024 * (c + 1)] = oc[1024 * si:1024 * (si + 1)]
    return outx


# version 19
# speedup vs baseline: 1.0511x; 1.0030x over previous
import math
from contextlib import ExitStack

import numpy as np
import ml_dtypes

import concourse.bass as bass
import concourse.mybir as mybir
from concourse.bass_utils import run_bass_kernel_spmd

F32 = mybir.dt.float32
BF16 = mybir.dt.bfloat16
I32 = mybir.dt.int32
AF = mybir.ActivationFunctionType
ALU = mybir.AluOpType
AX = mybir.AxisListType
bfdt = ml_dtypes.bfloat16

H = 4096
NH = 32
B = 4
S = 4096
NCORE = 8
QR = 2048
SLOT_E = (2048, 4096)
NEAR_MAX = 11
WCOEF = 30.0
KVDEPTH = 4
QSCALE = 128.0 ** -0.5

SLOPES = [2.0 ** (-(i + 1) / 4.0) for i in range(NH)]
WIN = [min(int(math.ceil(WCOEF / s)), 8192) for s in SLOPES]
Q0P = {(s, qc): SLOT_E[s] - 1024 + 512 * qc for s in range(2) for qc in range(2)}
NKT = {k: (v + 512) // 128 for k, v in Q0P.items()}
BB_OFF = {}
_o = 0
for _s in range(2):
    for _qc in range(2):
        BB_OFF[(_s, _qc)] = _o
        _o += NKT[(_s, _qc)]
NBV = _o
FAR_HEADS = [h for h in range(NH) if h > NEAR_MAX]
NFAR = len(FAR_HEADS)


def _jmin(h, s, qc):
    return max(0, (Q0P[(s, qc)] - WIN[h]) // 128)


def _chunks_for(par):
    return ([0, 3], [1, 2])[par]


def _stage_core(x, b, par):
    cks = _chunks_for(par)
    xb = x[b]
    xq = np.concatenate([xb[1024 * c:1024 * (c + 1)] for c in cks], axis=0)
    kv = []
    for s in range(2):
        e = SLOT_E[s]
        r = 1024 * (cks[s] + 1)
        lo = max(0, r - e)
        w = xb[lo:r]
        if w.shape[0] < e:
            w = np.concatenate([np.zeros((e - w.shape[0], H), np.float32), w], axis=0)
        kv.append(np.ascontiguousarray(w))
    return np.ascontiguousarray(xq), kv[0], kv[1]


def _const_tables(par):
    cks = _chunks_for(par)
    pads = [max(0, SLOT_E[s] - 1024 * (cks[s] + 1)) for s in range(2)]
    p = np.arange(128)
    f = np.arange(512)
    tstk = np.zeros((28, 128, 512), np.float32)
    i = 0
    for s in range(2):
        for qc in range(2):
            q0 = Q0P[(s, qc)]
            for cv in range(7):
                c = -384 + 128 * cv
                t = (c + p[:, None] - f[None, :]).astype(np.float32)
                bad = (c + p[:, None] > f[None, :]) | ((q0 + c + p[:, None]) < pads[s])
                t[bad] = -1e33
                tstk[i] = t
                i += 1
    mstk = np.zeros((4, 128, 512), np.float32)
    for cv in range(4):
        c = 128 * cv
        mstk[cv][(c + p[:, None]) > f[None, :]] = -1e30
    bbase = np.zeros((128, NBV), np.float32)
    for s in range(2):
        for qc in range(2):
            q0 = Q0P[(s, qc)]
            off = BB_OFF[(s, qc)]
            for j in range(NKT[(s, qc)]):
                v = (128 * j + p - (q0 + 511)).astype(np.float32)
                v[(128 * j + p) < pads[s]] = -1e30
                bbase[:, off + j] = v
    eh2f = np.zeros((128, 4), np.float32)
    eh2f[:, 0] = 1.0
    eh2f[:, 3] = 1.0
    sel2 = np.zeros((2, 256), np.float32)
    sel2[0, 0:128] = 1.0
    sel2[1, 128:256] = 1.0
    ident = np.eye(128, dtype=np.float32)
    simh = np.zeros((128, 32 * 128), np.float32)
    for h in range(NH):
        simh[:, 128 * h:128 * (h + 1)] = np.eye(128, dtype=np.float32) * SLOPES[h]
    return (tstk.astype(bfdt), mstk.astype(bfdt), bbase, eh2f, sel2.astype(bfdt),
            ident.astype(bfdt), simh.astype(bfdt))


class Prog:
    ENGINES = ("sync", "tensor", "vector", "scalar", "gpsimd")

    def __init__(self):
        self.events = []
        self.cnt = {}
        self._waited = {}

    def sem(self, *names):
        for n in names:
            self.cnt[n] = 0

    ENGSEMS = ("t", "v", "s", "g")

    def emit(self, eng, fn, waits=(), incs=()):
        w = []
        for sname, v in waits:
            if v is None or v <= 0:
                continue
            self.cnt.setdefault(sname, 0)
            assert v <= self.cnt[sname], (sname, v, self.cnt[sname])
            if sname not in self.ENGSEMS:
                assert v == self.cnt[sname], ("unsound dma wait", sname, v, self.cnt[sname])
            key = (eng, sname)
            if v > self._waited.get(key, -1):
                self._waited[key] = v
                w.append((sname, v))
        for sname, n in incs:
            self.cnt[sname] = self.cnt.setdefault(sname, 0) + n
        self.events.append((eng, tuple(w), fn, tuple(incs)))
        return self.cnt[incs[0][0]] if incs else None

    def dw(self, stream):
        return (stream, self.cnt.get(stream, 0))

    def allw(self):
        return [(sname, v) for sname, v in self.cnt.items() if v > 0]

    def barrier(self):
        snap = dict(self.cnt)
        for eng in self.ENGINES:
            self.emit(eng, lambda e: e.nop(),
                      waits=[(sname, v) for sname, v in snap.items() if v > 0])

    def lower(self, nc, sem_stack):
        sem_handles = {n: sem_stack.enter_context(nc.semaphore("sem_" + n))
                       for n in self.cnt}
        with nc.Block() as block:
            for eng_name in self.ENGINES:
                evs = [ev for ev in self.events if ev[0] == eng_name]
                if not evs:
                    continue

                def make(my_evs):
                    def body(eng):
                        for (_e, waits, fn, incs) in my_evs:
                            for (sname, v) in waits:
                                eng.wait_ge(sem_handles[sname], v)
                            ins = fn(eng)
                            for (sname, n) in incs:
                                ins.then_inc(sem_handles[sname], n)
                    return body
                getattr(block, eng_name)(make(evs))


def _emit_taps(nc, P, taps, scratch):
    for name in taps:
        src_ap = scratch[name]
        tout = nc.dram_tensor("tap_" + name, list(src_ap.shape), src_ap.dtype,
                              kind="ExternalOutput").ap()
        dn = P.emit("sync", lambda e, o_=tout, i_=src_ap: e.dma_start(out=o_, in_=i_),
                    waits=P.allw(), incs=[("tp", 16)])
        P.emit("sync", lambda e: e.nop(), waits=[("tp", dn)])


def _tap_sbuf(nc, P, name, ap, dt):
    tout = nc.dram_tensor("tap_" + name, list(ap.shape), dt, kind="ExternalOutput").ap()
    dn = P.emit("sync", lambda e: e.dma_start(out=tout, in_=ap), waits=P.allw(),
                incs=[("tp", 16)])
    P.emit("sync", lambda e: e.nop(), waits=[("tp", dn)])


def _finish_early(nc, P, out):
    with nc.sbuf_tensor("zfin", [128, H], F32) as zf:
        zv = P.emit("vector", lambda e: e.memset(zf[:], 0.0), waits=P.allw(), incs=[("v", 1)])
        dn = 0
        for t_ in range(QR // 128):
            dn = P.emit("scalar", lambda e, tt=t_: e.dma_start(
                out=out[128 * tt:128 * (tt + 1), :], in_=zf[:]),
                waits=[("v", zv)], incs=[("fz", 16)])
        P.emit("sync", lambda e: e.nop(), waits=[("fz", dn)])


def build_nc(stop_phase=9, taps=(), dparts=('vtr', 'dq', 'wo')):
    nc = bass.Bass("TRN2", target_bir_lowering=False)
    P = Prog()
    P.sem("t", "v", "s", "g")

    def din(name, shape, dt):
        return nc.dram_tensor(name, shape, dt, kind="ExternalInput").ap()

    xq = din("xq", [QR, H], F32)
    xkv0 = din("xkv0", [SLOT_E[0], H], F32)
    xkv1 = din("xkv1", [SLOT_E[1], H], F32)
    wq = din("wq", [H, H], F32)
    wkv = din("wkv", [H, 256], F32)
    wo = din("wo", [H, H], F32)
    qw = din("qw", [H, H], I32)
    sz = din("sz", [128, 2048], F32)
    fb = din("fb", [128, 32], F32)
    tstk = din("tstk", [28, 128, 512], BF16)
    mstk = din("mstk", [4, 128, 512], BF16)
    simh = din("simh", [128, 32 * 128], BF16)
    bbase = din("bbase", [128, NBV], F32)
    eh2f = din("eh2f", [128, 4], F32)
    sel2 = din("sel2", [2, 256], BF16)
    ident = din("ident", [128, 128], BF16)
    onescol = din("onescol", [128, 1], BF16)
    onescolf = din("onescolf", [128, 1], F32)
    onesrow = din("onesrow", [1, 128], BF16)

    out = nc.dram_tensor("out", [QR, H], F32, kind="ExternalOutput").ap()

    xnh = nc.dram_tensor("xnh", [QR + SLOT_E[0] + SLOT_E[1], H], BF16, kind="Internal").ap()
    qth = nc.dram_tensor("qth", [H, QR], BF16, kind="Internal").ap()
    wqbh = nc.dram_tensor("wqbh", [H, H], BF16, kind="Internal").ap()
    wobh = nc.dram_tensor("wobh", [H, H], BF16, kind="Internal").ap()
    wfth = nc.dram_tensor("wfth", [32, 128, H], BF16, kind="Internal").ap()
    sth = nc.dram_tensor("sth", [H, QR], BF16, kind="Internal").ap()
    sgh = nc.dram_tensor("sgh", [H, QR], BF16, kind="Internal").ap()
    scratch = dict(xnh=xnh, qth=qth, wqbh=wqbh, wobh=wobh, wfth=wfth, sth=sth, sgh=sgh)

    st0 = ExitStack()
    E = st0.enter_context

    c_ident = E(nc.sbuf_tensor("c_ident", [128, 128], BF16))
    c_mstk = E(nc.sbuf_tensor("c_mstk", [128, 4 * 512], BF16))
    c_simh = E(nc.sbuf_tensor("c_simh", [128, 32 * 128], BF16))
    c_bbase = E(nc.sbuf_tensor("c_bbase", [128, NBV], F32))
    c_fb = E(nc.sbuf_tensor("c_fb", [128, 32], F32))
    c_onec = E(nc.sbuf_tensor("c_onec", [128, 1], BF16))
    c_onecf = E(nc.sbuf_tensor("c_onecf", [128, 1], F32))
    c_oner = E(nc.sbuf_tensor("c_oner", [1, 128], BF16))
    kt_s = [E(nc.sbuf_tensor(f"kt{s}", [128, SLOT_E[s]], BF16)) for s in range(2)]
    vt_s = [E(nc.sbuf_tensor(f"vt{s}", [128, SLOT_E[s]], BF16)) for s in range(2)]
    ppb = E(nc.sbuf_tensor("ppb", [128, 8 * 512], BF16))
    accA = E(nc.sbuf_tensor("accA", [128, 2 * 512], F32))
    rd_f = [E(nc.sbuf_tensor(f"rd_f{i}", [1, 512], F32)) for i in range(2)]
    rd_b = [E(nc.sbuf_tensor(f"rd_b{i}", [1, 512], BF16)) for i in range(2)]
    rs_f0 = E(nc.sbuf_tensor("rs_f0", [128, 512], F32))
    rs_f = [rs_f0, rs_f0]

    with ExitStack() as stA:
        EA = stA.enter_context
        xb = [EA(nc.sbuf_tensor(f"xA{i}", [128, H], F32)) for i in range(2)]
        xnb = [EA(nc.sbuf_tensor(f"xnA{i}", [128, H], BF16)) for i in range(2)]
        sq = EA(nc.sbuf_tensor("sqA", [128, H], BF16))
        stt = [EA(nc.sbuf_tensor(f"stA{i}", [128, 8], F32)) for i in range(2)]

        for dst, src in ((c_ident, ident), (c_bbase, bbase),
                         (c_fb, fb), (c_simh, simh),
                         (c_onec, onescol), (c_oner, onesrow),
                         (c_onecf, onescolf)):
            P.emit("sync", lambda e, d=dst, s_=src: e.dma_start(out=d[:], in_=s_),
                   incs=[("aC", 16)])
        P.emit("sync", lambda e: e.dma_start(
            out=c_mstk[:].rearrange("p (c f) -> p c f", c=4), in_=mstk.rearrange("c p f -> p c f")), incs=[("aC", 16)])

        srcs = [(xq, 0, QR // 128), (xkv0, QR // 128, SLOT_E[0] // 128),
                (xkv1, (QR + SLOT_E[0]) // 128, SLOT_E[1] // 128)]
        tiles = []
        for src, base_t, n in srcs:
            for tloc in range(n):
                tiles.append((src, base_t + tloc, tloc))
        last_use = [None, None]
        xnb_guard = [0, 0]
        sq_guard = [0]
        for i, (src, trow, tloc) in enumerate(tiles):
            pb = i % 2
            waits = []
            if last_use[pb] is not None:
                waits = [("v", last_use[pb][0]), ("s", last_use[pb][1])]
            dl = P.emit("sync", lambda e, sr=src, tl=tloc, p_=pb: e.dma_start(
                out=xb[p_][:], in_=sr[128 * tl:128 * (tl + 1), :]),
                waits=waits, incs=[(f"aX{pb}", 16)])
            st_ = stt[pb]
            v1 = P.emit("vector", lambda e, p_=pb, st=st_: e.reduce_sum(
                out=st[:, 0:1], in_=xb[p_][:], axis=AX.X),
                waits=[P.dw(f"aX{pb}")], incs=[("v", 1)])
            s1g = P.emit("scalar", lambda e, p_=pb, st=st_: e.activation(
                sq[:], xb[p_][:], AF.Square, accum_out=st[:, 1:2]),
                waits=[P.dw(f"aX{pb}")], incs=[("s", 1)])
            v2 = P.emit("vector", lambda e, st=st_: e.tensor_scalar(
                out=st[:, 2:3], in0=st[:, 0:1], scalar1=1.0 / H, scalar2=None,
                op0=ALU.mult), waits=[("v", v1)], incs=[("v", 1)])
            v3 = P.emit("vector", lambda e, st=st_: e.scalar_tensor_tensor(
                out=st[:, 3:4], in0=st[:, 2:3], scalar=-1.0, in1=st[:, 2:3],
                op0=ALU.mult, op1=ALU.mult),
                waits=[("v", v2)], incs=[("v", 1)])
            v4a = P.emit("vector", lambda e, st=st_: e.scalar_tensor_tensor(
                out=st[:, 4:5], in0=st[:, 1:2], scalar=1.0 / H, in1=st[:, 3:4],
                op0=ALU.mult, op1=ALU.add),
                waits=[("s", s1g), ("v", v3)], incs=[("v", 1)])
            v4 = P.emit("vector", lambda e, st=st_: e.tensor_scalar(
                out=st[:, 4:5], in0=st[:, 4:5], scalar1=1e-5, scalar2=None,
                op0=ALU.add), waits=[("v", v4a)], incs=[("v", 1)])
            s2 = P.emit("scalar", lambda e, st=st_: e.activation(
                st[:, 5:6], st[:, 4:5], AF.Ln),
                waits=[("v", v4)], incs=[("s", 1)])
            s3 = P.emit("scalar", lambda e, st=st_: e.activation(
                st[:, 6:7], st[:, 5:6], AF.Exp, scale=-0.5),
                waits=[("s", s2)], incs=[("s", 1)])
            v5 = P.emit("vector", lambda e, st=st_: e.scalar_tensor_tensor(
                out=st[:, 7:8], in0=st[:, 2:3], scalar=-1.0, in1=st[:, 6:7],
                op0=ALU.mult, op1=ALU.mult),
                waits=[("s", s3)], incs=[("v", 1)])
            s4 = P.emit("scalar", lambda e, p_=pb, st=st_: e.activation(
                xnb[p_][:], xb[p_][:], AF.Identity, scale=st[:, 6:7], bias=st[:, 7:8]),
                waits=[("v", v5), ("s", s3), (f"aN{pb}", xnb_guard[pb])], incs=[("s", 1)])
            dxn = P.emit("scalar", lambda e, p_=pb, tr=trow: e.dma_start(
                out=xnh[128 * tr:128 * (tr + 1), :], in_=xnb[p_][:]),
                waits=[("s", s4)], incs=[(f"aN{pb}", 16)])
            last_use[pb] = (v1, s4)
            xnb_guard[pb] = dxn
        P.barrier()
    if stop_phase <= 1:
        raise_stop = True
    else:
        raise_stop = False
    if raise_stop:
        _emit_taps(nc, P, taps, scratch)
        _finish_early(nc, P, out)
        P.lower(nc, st0)
        return nc

    vt_tmp = {}
    stB = ExitStack()
    EB = stB.enter_context
    vt_tmp[0] = EB(nc.sbuf_tensor("vtt0", [128, SLOT_E[0]], BF16))
    vt_tmp[1] = EB(nc.sbuf_tensor("vtt1", [128, SLOT_E[1]], BF16))
    with ExitStack() as stB2:
        EB2 = stB2.enter_context
        xnt = [EB2(nc.sbuf_tensor(f"xntB{i}", [128, 2048], BF16)) for i in range(4)]
        wkv_raw = EB2(nc.sbuf_tensor("wkv_raw", [128, 32 * 256], F32))
        wkv_bf = EB2(nc.sbuf_tensor("wkv_bf", [128, 32 * 256], BF16))
        pbig = EB2(nc.psum_tensor("pbigB", [128, 4096], F32))
        dkv = P.emit("sync", lambda e: e.dma_start(
            out=wkv_raw[:].rearrange("p (j f) -> p j f", j=32), in_=wkv.rearrange("(j p) f -> p j f", p=128)), incs=[("bW", 16)])
        vcast = P.emit("vector", lambda e: e.tensor_copy(out=wkv_bf[:], in_=wkv_raw[:]),
                       waits=[("bW", dkv)], incs=[("v", 1)])
        mm_done = 0
        mmj_done = {}
        gj = [0]
        for s in range(2):
            e_s = SLOT_E[s]
            row0 = (QR if s == 0 else QR + SLOT_E[0])
            for half in range(e_s // 2048):
                r0h = row0 + 2048 * half
                for j in range(32):
                    pb = gj[0] % 4
                    dl = P.emit("sync", lambda e, jj=j, p_=pb, r0=r0h: e.dma_start_transpose(
                        xnt[p_][:, 0:2048], xnh[r0:r0 + 2048, 128 * jj:128 * (jj + 1)]),
                        waits=[("v", vcast), ("t", mmj_done.get(gj[0] - 4, 0))],
                        incs=[(f"bX{pb}", 16)])
                    for ch in range(4):
                        P.emit("tensor", lambda e, jj=j, p_=pb, c_=ch: e.matmul(
                            pbig[:, 512 * c_:512 * (c_ + 1)],
                            wkv_bf[:, 256 * jj:256 * jj + 128],
                            xnt[p_][:, 512 * c_:512 * (c_ + 1)],
                            start=(jj == 0), stop=(jj == 31)),
                            waits=[P.dw(f"bX{pb}")], incs=[("t", 1)])
                        mm_done = P.emit("tensor", lambda e, jj=j, p_=pb, c_=ch: e.matmul(
                            pbig[:, 2048 + 512 * c_:2048 + 512 * (c_ + 1)],
                            wkv_bf[:, 256 * jj + 128:256 * jj + 256],
                            xnt[p_][:, 512 * c_:512 * (c_ + 1)],
                            start=(jj == 0), stop=(jj == 31)),
                            incs=[("t", 1)])
                    mmj_done[gj[0]] = mm_done
                    gj[0] += 1
                wv = ws = 0
                for ch in range(4):
                    if ch % 2 == 0:
                        wv = P.emit("vector", lambda e, c_=ch, s_=s, hf=half: e.tensor_copy(
                            out=kt_s[s_][:, 2048 * hf + 512 * c_:2048 * hf + 512 * (c_ + 1)],
                            in_=pbig[:, 512 * c_:512 * (c_ + 1)]),
                            waits=[("t", mm_done)], incs=[("v", 1)])
                        wv = P.emit("vector", lambda e, c_=ch, s_=s, hf=half: e.tensor_copy(
                            out=vt_tmp[s_][:, 2048 * hf + 512 * c_:2048 * hf + 512 * (c_ + 1)],
                            in_=pbig[:, 2048 + 512 * c_:2048 + 512 * (c_ + 1)]),
                            waits=[("t", mm_done)], incs=[("v", 1)])
                    else:
                        ws = P.emit("scalar", lambda e, c_=ch, s_=s, hf=half: e.copy(
                            kt_s[s_][:, 2048 * hf + 512 * c_:2048 * hf + 512 * (c_ + 1)],
                            pbig[:, 512 * c_:512 * (c_ + 1)]),
                            waits=[("t", mm_done)], incs=[("s", 1)])
                        ws = P.emit("scalar", lambda e, c_=ch, s_=s, hf=half: e.copy(
                            vt_tmp[s_][:, 2048 * hf + 512 * c_:2048 * hf + 512 * (c_ + 1)],
                            pbig[:, 2048 + 512 * c_:2048 + 512 * (c_ + 1)]),
                            waits=[("t", mm_done)], incs=[("s", 1)])
                P.emit("tensor", lambda e: e.nop(), waits=[("v", wv), ("s", ws)])
        P.barrier()
    if stop_phase <= 2:
        for nm in taps:
            if nm.startswith("kv:"):
                w = nm.split(":")[1]
                obj = {"kt0": kt_s[0], "kt1": kt_s[1], "vtt0": vt_tmp[0], "vtt1": vt_tmp[1]}[w]
                _tap_sbuf(nc, P, w, obj[:], BF16)
        taps = [t for t in taps if not t.startswith("kv:")]
        _emit_taps(nc, P, taps, scratch)
        _finish_early(nc, P, out)
        P.lower(nc, st0)
        return nc

    with ExitStack() as stC:
        EC = stC.enter_context
        xntq = [EC(nc.sbuf_tensor(f"xntqC{j}", [128, 1024], BF16)) for j in range(32)]
        wqr = [EC(nc.sbuf_tensor(f"wqrC{i}", [128, H], F32)) for i in range(2)]
        wqb = [EC(nc.sbuf_tensor(f"wqbC{i}", [128, H], BF16)) for i in range(2)]
        qst = [EC(nc.sbuf_tensor(f"qstC{i}", [128, 1024], BF16)) for i in range(2)]
        pq = EC(nc.psum_tensor("pqC", [128, 4096], F32))

        prev = {}
        for half in range(2):
            tcol = 1024 * half
            xd = []
            tguard = P.cnt["t"]
            for j in range(32):
                d = P.emit("sync", lambda e, jj=j, tc=tcol: e.dma_start_transpose(
                    xntq[jj][:], xnh[tc:tc + 1024, 128 * jj:128 * (jj + 1)]),
                    waits=[("t", tguard)], incs=[("cQ", 16)])
                xd.append(d)
            cq_full = P.dw("cQ")
            for dt in range(32):
                pb = dt % 2
                waits = []
                if pb in prev:
                    waits = [("v", prev[pb][1]), ("s", prev[pb][2]), P.dw(f"cQT{pb}")]
                if half == 0:
                    dl = P.emit("sync", lambda e, d_=dt, p_=pb: e.dma_start(
                        out=wqr[p_][:].rearrange("p (j f) -> p j f", j=32),
                        in_=wq[:, 128 * d_:128 * (d_ + 1)].rearrange("(j p) f -> p j f", p=128)),
                        waits=waits, incs=[(f"cW{pb}", 16)])
                    cv = P.emit("vector", lambda e, p_=pb: e.tensor_scalar(
                        out=wqb[p_][:], in0=wqr[p_][:], scalar1=QSCALE, scalar2=None,
                        op0=ALU.mult),
                        waits=[P.dw(f"cW{pb}"), P.dw(f"cWB{pb}")] + waits, incs=[("v", 1)])
                    P.emit("scalar", lambda e, d_=dt, p_=pb: e.dma_start(
                        out=wqbh[:, 128 * d_:128 * (d_ + 1)].rearrange("(j p) f -> p j f", p=128),
                        in_=wqb[p_][:].rearrange("p (j f) -> p j f", j=32)),
                        waits=[("v", cv)], incs=[(f"cWB{pb}", 16)])
                    ready = [("v", cv)]
                else:
                    wb_guard = [P.dw("cWB0"), P.dw("cWB1")] if dt == 0 else []
                    dl = P.emit("sync", lambda e, d_=dt, p_=pb: e.dma_start(
                        out=wqb[p_][:].rearrange("p (j f) -> p j f", j=32),
                        in_=wqbh[:, 128 * d_:128 * (d_ + 1)].rearrange("(j p) f -> p j f", p=128)),
                        waits=waits + wb_guard, incs=[(f"cW{pb}", 16)])
                    ready = [P.dw(f"cW{pb}")]
                mm_done = 0
                for j in range(32):
                    for ch in range(2):
                        mm_done = P.emit("tensor", lambda e, jj=j, c_=ch, p_=pb: e.matmul(
                            pq[:, 2048 * p_ + 512 * c_:2048 * p_ + 512 * (c_ + 1)],
                            wqb[p_][:, 128 * jj:128 * (jj + 1)],
                            xntq[jj][:, 512 * c_:512 * (c_ + 1)],
                            start=(jj == 0), stop=(jj == 31)),
                            waits=ready + [cq_full], incs=[("t", 1)])
                evv = P.emit("vector", lambda e, p_=pb: e.tensor_copy(
                    out=qst[p_][:, 0:512], in_=pq[:, 2048 * p_:2048 * p_ + 512]),
                    waits=[("t", mm_done)], incs=[("v", 1)])
                evs = P.emit("scalar", lambda e, p_=pb: e.copy(
                    qst[p_][:, 512:1024], pq[:, 2048 * p_ + 512:2048 * p_ + 1024]),
                    waits=[("t", mm_done)], incs=[("s", 1)])
                dq = P.emit("scalar", lambda e, d_=dt, p_=pb, tc=tcol: e.dma_start(
                    out=qth[128 * d_:128 * (d_ + 1), tc:tc + 1024], in_=qst[p_][:]),
                    waits=[("v", evv), ("s", evs)], incs=[(f"cQT{pb}", 16)])
                prev[pb] = (mm_done, evv, evs, dq)
        P.barrier()
    if stop_phase <= 3:
        _emit_taps(nc, P, taps, scratch)
        _finish_early(nc, P, out)
        P.lower(nc, st0)
        return nc

    with ExitStack() as stD:
        ED = stD.enter_context
        pbf = ED(nc.psum_tensor("pbfD", [128, 2048], BF16))

        tr_done = 0
        ev_v = P.cnt["v"]
        for s in (range(2) if 'vtr' in dparts else ()):
            nt = SLOT_E[s] // 128
            for grp in range(nt // 8):
                half = grp % 2
                for u in range(8):
                    t_ = grp * 8 + u
                    tr_done = P.emit("tensor", lambda e, s_=s, tt=t_, hf=half, uu=u: e.transpose(
                        pbf[:, 1024 * hf + 128 * uu:1024 * hf + 128 * (uu + 1)],
                        vt_tmp[s_][:, 128 * tt:128 * (tt + 1)], c_ident[:]),
                        waits=[("v", ev_v)], incs=[("t", 1)])
                ev_v = P.emit("vector", lambda e, s_=s, g_=grp, hf=half: e.tensor_copy(
                    out=vt_s[s_][:, 1024 * g_:1024 * (g_ + 1)],
                    in_=pbf[:, 1024 * hf:1024 * (hf + 1)]),
                    waits=[("t", tr_done)], incs=[("v", 1)])
        stB.close()
        guardD = [("t", P.cnt["t"]), ("v", P.cnt["v"])]
        c_sz = ED(nc.sbuf_tensor("c_sz", [128, 2048], F32))
        dsz = P.emit("sync", lambda e: e.dma_start(out=c_sz[:], in_=sz),
                     waits=guardD, incs=[("dSZ", 16)])
        P.emit("scalar", lambda e: e.nop(), waits=[P.dw("dSZ")])
        P.emit("vector", lambda e: e.nop(), waits=[P.dw("dSZ")])
        qwb = [ED(nc.sbuf_tensor(f"qwD{i}", [128, H], I32)) for i in range(2)]
        wfo = [ED(nc.sbuf_tensor(f"wfoD{i}", [128, H], BF16)) for i in range(2)]
        wfs = [ED(nc.sbuf_tensor(f"wfsD{i}", [128, H], BF16)) for i in range(2)]
        wor = [ED(nc.sbuf_tensor(f"worD{i}", [128, H], F32)) for i in range(2)]
        wob = [ED(nc.sbuf_tensor(f"wobD{i}", [128, H], BF16)) for i in range(2)]

        prevD = {}
        last_ev = ev_v
        for ot in (range(32) if 'dq' in dparts else ()):
            pb = ot % 2
            waits = list(guardD) if ot == 0 else []
            if pb in prevD:
                waits += [("t", prevD[pb][0])]
            dl = P.emit("sync", lambda e, o_=ot, p_=pb: e.dma_start(
                out=qwb[p_][:], in_=qw[128 * o_:128 * (o_ + 1), :]),
                waits=waits, incs=[(f"dQ{pb}", 16)])
            wv = ws = 0
            for gg in range(32):
                if gg % 2 == 0:
                    ws = P.emit("scalar", lambda e, o_=ot, g_=gg, p_=pb: e.activation(
                        wfo[p_][:, 128 * g_:128 * (g_ + 1)],
                        qwb[p_][:, 128 * g_:128 * (g_ + 1)], AF.Identity,
                        scale=c_sz[:, 32 * o_ + g_:32 * o_ + g_ + 1],
                        bias=c_sz[:, 1024 + 32 * o_ + g_:1024 + 32 * o_ + g_ + 1]),
                        waits=[P.dw(f"dQ{pb}")], incs=[("s", 1)])
                else:
                    wv = P.emit("vector", lambda e, o_=ot, g_=gg, p_=pb: e.tensor_scalar(
                        out=wfo[p_][:, 128 * g_:128 * (g_ + 1)],
                        in0=qwb[p_][:, 128 * g_:128 * (g_ + 1)],
                        scalar1=c_sz[:, 32 * o_ + g_:32 * o_ + g_ + 1],
                        scalar2=c_sz[:, 1024 + 32 * o_ + g_:1024 + 32 * o_ + g_ + 1],
                        op0=ALU.mult, op1=ALU.add),
                        waits=[P.dw(f"dQ{pb}")], incs=[("v", 1)])
            trs = 0
            evl = last_ev
            for grp in range(4):
                half = grp % 2
                for u in range(8):
                    g_ = grp * 8 + u
                    trs = P.emit("tensor", lambda e, gg=g_, hf=half, uu=u, p_=pb: e.transpose(
                        pbf[:, 1024 * hf + 128 * uu:1024 * hf + 128 * (uu + 1)],
                        wfo[p_][:, 128 * gg:128 * (gg + 1)], c_ident[:]),
                        waits=[("v", max(wv, evl)), ("s", ws)], incs=[("t", 1)])
                evl = P.emit("vector", lambda e, g_=grp, hf=half, p_=pb: e.tensor_copy(
                    out=wfs[p_][:, 1024 * g_:1024 * (g_ + 1)],
                    in_=pbf[:, 1024 * hf:1024 * (hf + 1)]),
                    waits=[("t", trs), P.dw(f"dF{pb}")],
                    incs=[("v", 1)])
            last_ev = evl
            dw = P.emit("scalar", lambda e, o_=ot, p_=pb: e.dma_start(
                out=wfth[:, :, 128 * o_:128 * (o_ + 1)].rearrange("g p f -> p g f"),
                in_=wfs[p_][:].rearrange("p (g f) -> p g f", g=32)),
                waits=[("v", evl)], incs=[(f"dF{pb}", 16)])
            prevD[pb] = (trs, dw)

        prevW = {}
        prevCV = {}
        for rt in (range(32) if 'wo' in dparts else ()):
            pb = rt % 2
            dl = P.emit("sync", lambda e, r_=rt, p_=pb: e.dma_start(
                out=wor[p_][:], in_=wo[128 * r_:128 * (r_ + 1), :]),
                waits=[("v", prevCV.get(pb, 0))], incs=[(f"dW{pb}", 16)])
            cv = P.emit("vector", lambda e, p_=pb: e.tensor_copy(
                out=wob[p_][:], in_=wor[p_][:]),
                waits=[P.dw(f"dW{pb}"), P.dw(f"dH{pb}")], incs=[("v", 1)])
            dwo = P.emit("scalar", lambda e, r_=rt, p_=pb: e.dma_start(
                out=wobh[128 * r_:128 * (r_ + 1), :], in_=wob[p_][:]),
                waits=[("v", cv)], incs=[(f"dH{pb}", 16)])
            prevW[pb] = dwo
            prevCV[pb] = cv
        P.barrier()
    if stop_phase <= 4:
        for nm in taps:
            if nm.startswith("kv:"):
                w = nm.split(":")[1]
                obj = {"kt0": kt_s[0], "kt1": kt_s[1], "vt0": vt_s[0], "vt1": vt_s[1]}[w]
                _tap_sbuf(nc, P, w, obj[:], BF16)
        taps = [t for t in taps if not t.startswith("kv:")]
        _emit_taps(nc, P, taps, scratch)
        _finish_early(nc, P, out)
        P.lower(nc, st0)
        return nc

    with ExitStack() as stE:
        EE = stE.enter_context
        qts = EE(nc.sbuf_tensor("qtsE", [128, 32 * 512], BF16))
        tsb = EE(nc.sbuf_tensor("tsbE", [128, 7 * 512], BF16))
        bsl = EE(nc.sbuf_tensor("bslE", [128, NFAR * 32], F32))
        attnT = [EE(nc.sbuf_tensor(f"attnTE{i}", [128, 32 * 512], BF16)) for i in range(2)]
        wosl = [EE(nc.sbuf_tensor(f"woslE{i}", [128, H], BF16)) for i in range(2)]
        stsl = [EE(nc.sbuf_tensor(f"stslE{i}", [128, 1024], BF16)) for i in range(2)]
        wftsl = [EE(nc.sbuf_tensor(f"wftslE{i}", [128, H], BF16)) for i in range(2)]
        gtsl = EE(nc.sbuf_tensor("gtslE", [128, 1024], BF16))
        sgsl = EE(nc.sbuf_tensor("sgslE", [128, 1024], BF16))
        pb_ = [EE(nc.psum_tensor(f"pbE{i}", [128, 512], F32)) for i in range(8)]

        for s in range(2):
            bank_free = {i: ("t", 0) for i in range(8)}
            pp_free = {}
            rcb_guard = [0, 0]
            mul_guard = [0]
            acc_guard = [("t", 0), ("t", 0)]
            rmm_guard = 0
            for qc in range(2):
                q0 = Q0P[(s, qc)]
                ntk = NKT[(s, qc)]
                tcol = 1024 * s + 512 * qc
                tvar0 = (s * 2 + qc) * 7
                guard = [("t", P.cnt["t"])]
                dq_ = P.emit("sync", lambda e, tc=tcol: e.dma_start(
                    out=qts[:].rearrange("p (d q) -> p d q", d=32), in_=qth[:, tc:tc + 512].rearrange("(d p) q -> p d q", p=128)),
                    waits=guard, incs=[("eQ", 16)])
                dt_ = P.emit("sync", lambda e, tv=tvar0: e.dma_start(
                    out=tsb[:].rearrange("p (c f) -> p c f", c=7), in_=tstk[tv:tv + 7].rearrange("c p f -> p c f")),
                    incs=[("eT", 16)])
                eq_full = P.dw("eQ")
                et_full = P.dw("eT")
                sbias = {}
                for fi, h in enumerate(FAR_HEADS):
                    off = BB_OFF[(s, qc)]
                    sbias[h] = P.emit("scalar", lambda e, f_=fi, o_=off, n_=ntk, h_=h: e.activation(
                        bsl[:, 32 * f_:32 * f_ + n_], c_bbase[:, o_:o_ + n_],
                        AF.Identity, scale=SLOPES[h_]),
                        waits=[("s", P.cnt["s"])], incs=[("s", 1)])

                for g2 in range(16):
                    heads = (2 * g2, 2 * g2 + 1)
                    jmins = [_jmin(h, s, qc) for h in heads]
                    j0 = min(jmins)
                    actidx = [0, 0]
                    av_started = [False, False]
                    last_av = [0, 0]
                    last_ac = [("v", 0), ("v", 0)]
                    pend = []

                    def flush_avd():
                        for (i_p, ppi_p, jj_p, ex_p, st_p, sp_p) in pend:
                            av = P.emit("tensor", lambda e, s_=s, jj=jj_p, pi=ppi_p, i_=i_p,
                                        st_=st_p, sp_=sp_p: e.matmul(
                                pb_[4 + i_][:], vt_s[s_][:, 128 * jj:128 * (jj + 1)],
                                ppb[:, 512 * pi:512 * (pi + 1)], start=st_, stop=sp_),
                                waits=[("s", ex_p), bank_free[4 + i_p]], incs=[("t", 1)])
                            eng, sn = (("vector", "v"), ("gpsimd", "g"))[i_p]
                            if st_p:
                                ac = P.emit(eng, lambda e, pi=ppi_p, i_=i_p: e.tensor_copy(
                                    out=accA[:, 512 * i_:512 * (i_ + 1)],
                                    in_=ppb[:, 512 * pi:512 * (pi + 1)]),
                                    waits=[("s", ex_p), acc_guard[i_p]], incs=[(sn, 1)])
                            else:
                                ac = P.emit(eng, lambda e, pi=ppi_p, i_=i_p: e.tensor_tensor(
                                    out=accA[:, 512 * i_:512 * (i_ + 1)],
                                    in0=accA[:, 512 * i_:512 * (i_ + 1)],
                                    in1=ppb[:, 512 * pi:512 * (pi + 1)], op=ALU.add),
                                    waits=[("s", ex_p), last_ac[i_p]], incs=[(sn, 1)])
                            last_ac[i_p] = (sn, ac)
                            last_av[i_p] = av
                            pp_free[ppi_p] = [("t", av), last_ac[i_p]]
                        pend.clear()

                    for j in range(j0, ntk):
                        units = []
                        for i, h in enumerate(heads):
                            if j < jmins[i]:
                                continue
                            sb = 2 * i + (actidx[i] % 2)
                            ppi = 4 * i + (actidx[i] % 4)
                            near = h <= NEAR_MAX
                            diag = (128 * j - q0) >= 0
                            mm = P.emit("tensor", lambda e, s_=s, jj=j, h_=h, b_=sb,
                                        st1=not (near or diag): e.matmul(
                                pb_[b_][:], kt_s[s_][:, 128 * jj:128 * (jj + 1)],
                                qts[:, 512 * h_:512 * (h_ + 1)], start=True, stop=st1),
                                waits=[eq_full, bank_free[sb], et_full], incs=[("t", 1)])
                            if near:
                                cv = (128 * j - q0 + 384) // 128
                                mm = P.emit("tensor", lambda e, b_=sb, c_=cv, h_=h: e.matmul(
                                    pb_[b_][:], c_simh[:, 128 * h_:128 * (h_ + 1)],
                                    tsb[:, 512 * c_:512 * (c_ + 1)],
                                    start=False, stop=True), incs=[("t", 1)])
                            elif diag:
                                cv4 = (128 * j - q0) // 128
                                mm = P.emit("tensor", lambda e, b_=sb, c_=cv4: e.matmul(
                                    pb_[b_][:], c_ident[:],
                                    c_mstk[:, 512 * c_:512 * (c_ + 1)],
                                    start=False, stop=True), incs=[("t", 1)])
                            units.append((i, h, sb, ppi, mm))
                        flush_avd()
                        for (i, h, sb, ppi, mm) in units:
                            wexp = [("t", mm)]
                            if ppi in pp_free:
                                wexp.extend(pp_free[ppi])
                            if h > NEAR_MAX:
                                fi = h - NEAR_MAX - 1
                                wexp.append(("s", sbias[h]))
                                ex = P.emit("scalar", lambda e, b_=sb, pi=ppi, f_=fi, j_=j: e.activation(
                                    ppb[:, 512 * pi:512 * (pi + 1)], pb_[b_][:], AF.Exp,
                                    bias=bsl[:, 32 * f_ + j_:32 * f_ + j_ + 1]),
                                    waits=wexp, incs=[("s", 1)])
                            else:
                                ex = P.emit("scalar", lambda e, b_=sb, pi=ppi: e.activation(
                                    ppb[:, 512 * pi:512 * (pi + 1)], pb_[b_][:], AF.Exp),
                                    waits=wexp, incs=[("s", 1)])
                            bank_free[sb] = ("s", ex)
                            pend.append((i, ppi, j, ex, not av_started[i], j == ntk - 1))
                            av_started[i] = True
                            actidx[i] += 1
                    flush_avd()
                    dmms = {}
                    rcs = {}
                    rcbs = {}
                    for i in range(2):
                        dmms[i] = P.emit("tensor", lambda e, i_=i: e.matmul(
                            pb_[6 + i_][0:1, :], c_onecf[:], accA[:, 512 * i_:512 * (i_ + 1)],
                            start=True, stop=True),
                            waits=[last_ac[i], bank_free[6 + i]], incs=[("t", 1)])
                        acc_guard[i] = ("t", dmms[i])
                    for i in range(2):
                        rcs[i] = P.emit("vector", lambda e, i_=i: e.reciprocal(
                            out=rd_f[i_][:], in_=pb_[6 + i_][0:1, :]),
                            waits=[("t", dmms[i]), ("v", rcb_guard[i])], incs=[("v", 1)])
                    for i in range(2):
                        rcbs[i] = P.emit("vector", lambda e, i_=i: e.tensor_copy(
                            out=rd_b[i_][:], in_=rd_f[i_][:]),
                            waits=[("v", rcs[i]), ("t", rmm_guard)], incs=[("v", 1)])
                        rcb_guard[i] = rcbs[i]
                    prev_mul = 0
                    for i in range(2):
                        h = heads[i]
                        rmm = P.emit("tensor", lambda e, i_=i: e.matmul(
                            pb_[6 + i_][:], c_oner[:], rd_b[i_][:],
                            start=True, stop=True),
                            waits=[("v", rcbs[i])], incs=[("t", 1)])
                        rmm_guard = rmm
                        rcp = P.emit("vector", lambda e, i_=i: e.tensor_copy(
                            out=rs_f[i_][:], in_=pb_[6 + i_][:]),
                            waits=[("t", rmm), ("v", prev_mul), ("v", mul_guard[0])],
                            incs=[("v", 1)])
                        bank_free[6 + i] = ("v", rcp)
                        mul = P.emit("vector", lambda e, h_=h, i_=i, q_=qc: e.tensor_tensor(
                            out=attnT[q_][:, 512 * h_:512 * (h_ + 1)], in0=pb_[4 + i_][:],
                            in1=rs_f[i_][:], op=ALU.mult),
                            waits=[("t", last_av[i]), ("v", rcp)], incs=[("v", 1)])
                        prev_mul = mul
                        mul_guard[0] = mul
                        bank_free[4 + i] = ("v", mul)
            P.barrier()

            prevWo = {}
            evac_guard = {0: [], 1: []}
            for ot in range(32):
                pbu = ot % 2
                dl = P.emit("sync", lambda e, o_=ot, p_=pbu: e.dma_start(
                    out=wosl[p_][:].rearrange("p (j f) -> p j f", j=32),
                    in_=wobh[:, 128 * o_:128 * (o_ + 1)].rearrange("(j p) f -> p j f", p=128)),
                    waits=[("t", prevWo[pbu][1]) if pbu in prevWo else ("t", 0)],
                    incs=[(f"eW{pbu}", 16)])
                bk = 4 * pbu
                mmw = 0
                for ht in range(32):
                    for q_ in range(2):
                        mmw = P.emit("tensor", lambda e, h_=ht, qq=q_, p_=pbu, b_=bk: e.matmul(
                            pb_[b_ + qq][:], wosl[p_][:, 128 * h_:128 * (h_ + 1)],
                            attnT[qq][:, 512 * h_:512 * (h_ + 1)],
                            start=(h_ == 0), stop=(h_ == 31)),
                            waits=[P.dw(f"eW{pbu}")] + evac_guard[pbu], incs=[("t", 1)])
                e1 = P.emit("vector", lambda e, p_=pbu, b_=bk: e.tensor_copy(
                    out=stsl[p_][:, 0:512], in_=pb_[b_][:]),
                    waits=[("t", mmw), P.dw(f"eS{pbu}")], incs=[("v", 1)])
                e2 = P.emit("scalar", lambda e, p_=pbu, b_=bk: e.copy(
                    stsl[p_][:, 512:1024], pb_[b_ + 1][:]),
                    waits=[("t", mmw), P.dw(f"eS{pbu}")], incs=[("s", 1)])
                dst_ = P.emit("scalar", lambda e, o_=ot, p_=pbu, s_=s: e.dma_start(
                    out=sth[128 * o_:128 * (o_ + 1), 1024 * s_:1024 * (s_ + 1)],
                    in_=stsl[p_][:]),
                    waits=[("v", e1), ("s", e2)], incs=[(f"eS{pbu}", 16)])
                prevWo[pbu] = (dst_, mmw)
                evac_guard[pbu] = [("v", e1), ("s", e2)]
            P.barrier()

            xdF = []
            for j in range(32):
                buf = attnT[j // 16]
                coff = 1024 * (j % 16)
                d = P.emit("sync", lambda e, jj=j, b_=buf, c_=coff, s_=s: e.dma_start_transpose(
                    b_[:, c_:c_ + 1024], xnh[1024 * s_:1024 * (s_ + 1), 128 * jj:128 * (jj + 1)]),
                    incs=[("eX", 16)])
                xdF.append(d)
            ex_full = P.dw("eX")
            prevF = {}
            prev_mulf = 0
            sil_guard = {0: 0, 1: 0}
            for ot in range(32):
                pbu = ot % 2
                waits = []
                if pbu in prevF:
                    waits = [("v", prevF[pbu][1])]
                dl = P.emit("sync", lambda e, o_=ot, p_=pbu: e.dma_start(
                    out=wftsl[p_][:].rearrange("p (g f) -> p g f", g=32),
                    in_=wfth[:, :, 128 * o_:128 * (o_ + 1)].rearrange("g p f -> p g f")),
                    waits=waits, incs=[(f"eF{pbu}", 16)])
                dls = P.emit("sync", lambda e, o_=ot, p_=pbu, s_=s: e.dma_start(
                    out=stsl[p_][:], in_=sth[128 * o_:128 * (o_ + 1), 1024 * s_:1024 * (s_ + 1)]),
                    waits=[("v", prev_mulf)], incs=[(f"eR{pbu}", 16)])
                bkf = 2 + 4 * pbu
                mmf = 0
                for gg in range(32):
                    buf = attnT[gg // 16]
                    coff = 1024 * (gg % 16)
                    for q_ in range(2):
                        mmf = P.emit("tensor", lambda e, g_=gg, qq=q_, p_=pbu, b_=buf, c_=coff,
                                     bb=bkf: e.matmul(
                            pb_[bb + qq][:], wftsl[p_][:, 128 * g_:128 * (g_ + 1)],
                            b_[:, c_ + 512 * qq:c_ + 512 * (qq + 1)],
                            start=(g_ == 0), stop=(g_ == 31)),
                            waits=[P.dw(f"eF{pbu}"), ex_full, ("s", sil_guard[pbu])],
                            incs=[("t", 1)])
                sil = 0
                for q_ in range(2):
                    sil = P.emit("scalar", lambda e, o_=ot, qq=q_, bb=bkf: e.activation(
                        gtsl[:, 512 * qq:512 * (qq + 1)], pb_[bb + qq][:], AF.Silu,
                        bias=c_fb[:, o_:o_ + 1]),
                        waits=[("t", mmf), ("v", prev_mulf)], incs=[("s", 1)])
                sil_guard[pbu] = sil
                mulf = P.emit("vector", lambda e, p_=pbu: e.tensor_tensor(
                    out=sgsl[:], in0=gtsl[:], in1=stsl[p_][:], op=ALU.mult),
                    waits=[("s", sil), P.dw(f"eR{pbu}"), P.dw("eG")],
                    incs=[("v", 1)])
                prev_mulf = mulf
                dsg = P.emit("scalar", lambda e, o_=ot, s_=s: e.dma_start(
                    out=sgh[128 * o_:128 * (o_ + 1), 1024 * s_:1024 * (s_ + 1)],
                    in_=sgsl[:]),
                    waits=[("v", mulf)], incs=[("eG", 16)])
                prevF[pbu] = (dsg, mulf)
            P.barrier()

    if stop_phase <= 5:
        _emit_taps(nc, P, taps, scratch)
        _finish_early(nc, P, out)
        P.lower(nc, st0)
        return nc

    with ExitStack() as stF:
        EF = stF.enter_context
        sgt = [EF(nc.sbuf_tensor(f"sgtF{i}", [128, H], BF16)) for i in range(4)]
        xcb0 = [EF(nc.sbuf_tensor(f"xcF{i}", [128, H], F32)) for i in range(2)]
        osb0 = [EF(nc.sbuf_tensor(f"osF{i}", [128, H], F32)) for i in range(2)]
        xcb = xcb0 + xcb0
        osb = osb0 + osb0
        lastF = {}
        lastF2 = {}
        for t_ in range(16):
            pbu = t_ % 4
            wv = [("v", lastF[pbu])] if pbu in lastF else []
            wv2 = [("v", lastF2[t_ % 2])] if (t_ % 2) in lastF2 else []
            d1_ = P.emit("sync", lambda e, tt=t_, p_=pbu: e.dma_start_transpose(
                sgt[p_][:], sgh[:, 128 * tt:128 * (tt + 1)]),
                waits=wv, incs=[(f"fS{pbu}", 16)])
            d2_ = P.emit("scalar", lambda e, tt=t_, p_=pbu: e.dma_start(
                out=xcb[p_][:], in_=xq[128 * tt:128 * (tt + 1), :]),
                waits=wv2, incs=[(f"fX{t_ % 2}", 16)])
            cv = P.emit("vector", lambda e, p_=pbu: e.tensor_tensor(
                out=osb[p_][:], in0=xcb[p_][:], in1=sgt[p_][:], op=ALU.add),
                waits=[P.dw(f"fS{pbu}"), P.dw(f"fX{t_ % 2}"), P.dw(f"fO{t_ % 2}")],
                incs=[("v", 1)])
            do = P.emit("scalar", lambda e, tt=t_, p_=pbu: e.dma_start(
                out=out[128 * tt:128 * (tt + 1), :], in_=osb[p_][:]),
                waits=[("v", cv)], incs=[(f"fO{t_ % 2}", 16)])
            lastF[pbu] = cv
            lastF2[t_ % 2] = cv
        P.emit("sync", lambda e: e.nop(), waits=[P.dw("fO0"), P.dw("fO1")])


    _emit_taps(nc, P, taps, scratch)
    P.lower(nc, st0)
    st0.close()
    return nc


_NC_CACHE = [None]


def kernel(L0_attn_hidden_states, L0_attn_Wq, L0_attn_Wkv, L0_attn_Wo,
           L0_ffn_qweight, L0_ffn_scales, L0_ffn_scaled_zeros, L0_ffn_bias):
    x = np.ascontiguousarray(np.asarray(L0_attn_hidden_states, dtype=np.float32))
    wq_a = np.ascontiguousarray(np.asarray(L0_attn_Wq, dtype=np.float32))
    wkv_a = np.ascontiguousarray(np.asarray(L0_attn_Wkv, dtype=np.float32))
    wo_a = np.ascontiguousarray(np.asarray(L0_attn_Wo, dtype=np.float32))
    qw_a = np.ascontiguousarray(np.asarray(L0_ffn_qweight, dtype=np.int32).reshape(H, H))
    sc_a = np.asarray(L0_ffn_scales, dtype=np.float32).reshape(H, 32)
    zz_a = np.asarray(L0_ffn_scaled_zeros, dtype=np.float32).reshape(H, 32)
    fb_a = np.asarray(L0_ffn_bias, dtype=np.float32).reshape(H)

    sz = np.zeros((128, 2048), np.float32)
    sz[:, 0:1024] = sc_a.reshape(32, 128, 32).transpose(1, 0, 2).reshape(128, 1024)
    sz[:, 1024:2048] = zz_a.reshape(32, 128, 32).transpose(1, 0, 2).reshape(128, 1024)
    fbp = np.ascontiguousarray(fb_a.reshape(32, 128).T)

    if _NC_CACHE[0] is None:
        _NC_CACHE[0] = build_nc()
    nc = _NC_CACHE[0]

    in_maps = []
    for core in range(NCORE):
        b, par = core // 2, core % 2
        xq_c, xkv0_c, xkv1_c = _stage_core(x, b, par)
        tstk, mstk, bbase, eh2f, sel2, ident_b, simh_b = _const_tables(par)
        in_maps.append(dict(
            simh=simh_b,
            xq=xq_c, xkv0=xkv0_c, xkv1=xkv1_c, wq=wq_a, wkv=wkv_a, wo=wo_a,
            qw=qw_a, sz=sz, fb=fbp, tstk=tstk, mstk=mstk, bbase=bbase,
            eh2f=eh2f, sel2=sel2, ident=ident_b,
            onescol=np.ones((128, 1), np.float32).astype(bfdt),
            onescolf=np.ones((128, 1), np.float32),
            onesrow=np.ones((1, 128), np.float32).astype(bfdt)))

    res = run_bass_kernel_spmd(nc, in_maps, core_ids=list(range(NCORE)))

    outx = np.empty((B, S, H), np.float32)
    for core in range(NCORE):
        b, par = core // 2, core % 2
        cks = _chunks_for(par)
        oc = np.asarray(res.results[core]["out"], dtype=np.float32)
        for si, c in enumerate(cks):
            outx[b, 1024 * c:1024 * (c + 1)] = oc[1024 * si:1024 * (si + 1)]
    return outx


# version 20
# speedup vs baseline: 1.0695x; 1.0175x over previous
import math
from contextlib import ExitStack

import numpy as np
import ml_dtypes

import concourse.bass as bass
import concourse.mybir as mybir
from concourse.bass_utils import run_bass_kernel_spmd

F32 = mybir.dt.float32
BF16 = mybir.dt.bfloat16
I32 = mybir.dt.int32
AF = mybir.ActivationFunctionType
ALU = mybir.AluOpType
AX = mybir.AxisListType
bfdt = ml_dtypes.bfloat16

H = 4096
NH = 32
B = 4
S = 4096
NCORE = 8
QR = 2048
SLOT_E = (2048, 4096)
NEAR_MAX = 11
WCOEF = 36.0
KVDEPTH = 4
QSCALE = 128.0 ** -0.5

SLOPES = [2.0 ** (-(i + 1) / 4.0) for i in range(NH)]
WIN = [min(int(math.ceil(WCOEF / s)), 8192) for s in SLOPES]
Q0P = {(s, qc): SLOT_E[s] - 1024 + 512 * qc for s in range(2) for qc in range(2)}
NKT = {k: (v + 512) // 128 for k, v in Q0P.items()}
BB_OFF = {}
_o = 0
for _s in range(2):
    for _qc in range(2):
        BB_OFF[(_s, _qc)] = _o
        _o += NKT[(_s, _qc)]
NBV = _o
FAR_HEADS = [h for h in range(NH) if h > NEAR_MAX]
NFAR = len(FAR_HEADS)


def _jmin(h, s, qc):
    return max(0, (Q0P[(s, qc)] - WIN[h]) // 128)


def _chunks_for(par):
    return ([0, 3], [1, 2])[par]


def _stage_core(x, b, par):
    cks = _chunks_for(par)
    xb = x[b]
    xq = np.concatenate([xb[1024 * c:1024 * (c + 1)] for c in cks], axis=0)
    kv = []
    for s in range(2):
        e = SLOT_E[s]
        r = 1024 * (cks[s] + 1)
        lo = max(0, r - e)
        w = xb[lo:r]
        if w.shape[0] < e:
            w = np.concatenate([np.zeros((e - w.shape[0], H), np.float32), w], axis=0)
        kv.append(np.ascontiguousarray(w))
    return np.ascontiguousarray(xq), kv[0], kv[1]


def _const_tables(par):
    cks = _chunks_for(par)
    pads = [max(0, SLOT_E[s] - 1024 * (cks[s] + 1)) for s in range(2)]
    p = np.arange(128)
    f = np.arange(512)
    tstk = np.zeros((28, 128, 512), np.float32)
    i = 0
    for s in range(2):
        for qc in range(2):
            q0 = Q0P[(s, qc)]
            for cv in range(7):
                c = -384 + 128 * cv
                t = (c + p[:, None] - f[None, :]).astype(np.float32)
                bad = (c + p[:, None] > f[None, :]) | ((q0 + c + p[:, None]) < pads[s])
                t[bad] = -1e33
                tstk[i] = t
                i += 1
    mstk = np.zeros((4, 128, 512), np.float32)
    for cv in range(4):
        c = 128 * cv
        mstk[cv][(c + p[:, None]) > f[None, :]] = -1e30
    bbase = np.zeros((128, NBV), np.float32)
    for s in range(2):
        for qc in range(2):
            q0 = Q0P[(s, qc)]
            off = BB_OFF[(s, qc)]
            for j in range(NKT[(s, qc)]):
                v = (128 * j + p - (q0 + 511)).astype(np.float32)
                v[(128 * j + p) < pads[s]] = -1e30
                bbase[:, off + j] = v
    eh2f = np.zeros((128, 4), np.float32)
    eh2f[:, 0] = 1.0
    eh2f[:, 3] = 1.0
    sel2 = np.zeros((2, 256), np.float32)
    sel2[0, 0:128] = 1.0
    sel2[1, 128:256] = 1.0
    ident = np.eye(128, dtype=np.float32)
    simh = np.zeros((128, 32 * 128), np.float32)
    for h in range(NH):
        simh[:, 128 * h:128 * (h + 1)] = np.eye(128, dtype=np.float32) * SLOPES[h]
    return (tstk.astype(bfdt), mstk.astype(bfdt), bbase, eh2f, sel2.astype(bfdt),
            ident.astype(bfdt), simh.astype(bfdt))


class Prog:
    ENGINES = ("sync", "tensor", "vector", "scalar", "gpsimd")

    def __init__(self):
        self.events = []
        self.cnt = {}
        self._waited = {}

    def sem(self, *names):
        for n in names:
            self.cnt[n] = 0

    ENGSEMS = ("t", "v", "s", "g")

    def emit(self, eng, fn, waits=(), incs=()):
        w = []
        for sname, v in waits:
            if v is None or v <= 0:
                continue
            self.cnt.setdefault(sname, 0)
            assert v <= self.cnt[sname], (sname, v, self.cnt[sname])
            if sname not in self.ENGSEMS:
                assert v == self.cnt[sname], ("unsound dma wait", sname, v, self.cnt[sname])
            key = (eng, sname)
            if v > self._waited.get(key, -1):
                self._waited[key] = v
                w.append((sname, v))
        for sname, n in incs:
            self.cnt[sname] = self.cnt.setdefault(sname, 0) + n
        self.events.append((eng, tuple(w), fn, tuple(incs)))
        return self.cnt[incs[0][0]] if incs else None

    def dw(self, stream):
        return (stream, self.cnt.get(stream, 0))

    def allw(self):
        return [(sname, v) for sname, v in self.cnt.items() if v > 0]

    def barrier(self):
        snap = dict(self.cnt)
        for eng in self.ENGINES:
            self.emit(eng, lambda e: e.nop(),
                      waits=[(sname, v) for sname, v in snap.items() if v > 0])

    def lower(self, nc, sem_stack):
        sem_handles = {n: sem_stack.enter_context(nc.semaphore("sem_" + n))
                       for n in self.cnt}
        with nc.Block() as block:
            for eng_name in self.ENGINES:
                evs = [ev for ev in self.events if ev[0] == eng_name]
                if not evs:
                    continue

                def make(my_evs):
                    def body(eng):
                        for (_e, waits, fn, incs) in my_evs:
                            for (sname, v) in waits:
                                eng.wait_ge(sem_handles[sname], v)
                            ins = fn(eng)
                            for (sname, n) in incs:
                                ins.then_inc(sem_handles[sname], n)
                    return body
                getattr(block, eng_name)(make(evs))


def _emit_taps(nc, P, taps, scratch):
    for name in taps:
        src_ap = scratch[name]
        tout = nc.dram_tensor("tap_" + name, list(src_ap.shape), src_ap.dtype,
                              kind="ExternalOutput").ap()
        dn = P.emit("sync", lambda e, o_=tout, i_=src_ap: e.dma_start(out=o_, in_=i_),
                    waits=P.allw(), incs=[("tp", 16)])
        P.emit("sync", lambda e: e.nop(), waits=[("tp", dn)])


def _tap_sbuf(nc, P, name, ap, dt):
    tout = nc.dram_tensor("tap_" + name, list(ap.shape), dt, kind="ExternalOutput").ap()
    dn = P.emit("sync", lambda e: e.dma_start(out=tout, in_=ap), waits=P.allw(),
                incs=[("tp", 16)])
    P.emit("sync", lambda e: e.nop(), waits=[("tp", dn)])


def _finish_early(nc, P, out):
    with nc.sbuf_tensor("zfin", [128, H], F32) as zf:
        zv = P.emit("vector", lambda e: e.memset(zf[:], 0.0), waits=P.allw(), incs=[("v", 1)])
        dn = 0
        for t_ in range(QR // 128):
            dn = P.emit("scalar", lambda e, tt=t_: e.dma_start(
                out=out[128 * tt:128 * (tt + 1), :], in_=zf[:]),
                waits=[("v", zv)], incs=[("fz", 16)])
        P.emit("sync", lambda e: e.nop(), waits=[("fz", dn)])


def build_nc(stop_phase=9, taps=(), dparts=('vtr', 'dq', 'wo')):
    nc = bass.Bass("TRN2", target_bir_lowering=False)
    P = Prog()
    P.sem("t", "v", "s", "g")

    def din(name, shape, dt):
        return nc.dram_tensor(name, shape, dt, kind="ExternalInput").ap()

    xq = din("xq", [QR, H], F32)
    xkv0 = din("xkv0", [SLOT_E[0], H], F32)
    xkv1 = din("xkv1", [SLOT_E[1], H], F32)
    wq = din("wq", [H, H], F32)
    wkv = din("wkv", [H, 256], F32)
    wo = din("wo", [H, H], F32)
    qw = din("qw", [H, H], I32)
    sz = din("sz", [128, 2048], F32)
    fb = din("fb", [128, 32], F32)
    tstk = din("tstk", [28, 128, 512], BF16)
    mstk = din("mstk", [4, 128, 512], BF16)
    simh = din("simh", [128, 32 * 128], BF16)
    bbase = din("bbase", [128, NBV], F32)
    eh2f = din("eh2f", [128, 4], F32)
    sel2 = din("sel2", [2, 256], BF16)
    ident = din("ident", [128, 128], BF16)
    onescol = din("onescol", [128, 1], BF16)
    onescolf = din("onescolf", [128, 1], F32)
    onesrow = din("onesrow", [1, 128], BF16)

    out = nc.dram_tensor("out", [QR, H], F32, kind="ExternalOutput").ap()

    xnh = nc.dram_tensor("xnh", [QR + SLOT_E[0] + SLOT_E[1], H], BF16, kind="Internal").ap()
    qth = nc.dram_tensor("qth", [H, QR], BF16, kind="Internal").ap()
    wqbh = nc.dram_tensor("wqbh", [H, H], BF16, kind="Internal").ap()
    wobh = nc.dram_tensor("wobh", [H, H], BF16, kind="Internal").ap()
    wfth = nc.dram_tensor("wfth", [32, 128, H], BF16, kind="Internal").ap()
    sth = nc.dram_tensor("sth", [H, QR], BF16, kind="Internal").ap()
    sgh = nc.dram_tensor("sgh", [H, QR], BF16, kind="Internal").ap()
    scratch = dict(xnh=xnh, qth=qth, wqbh=wqbh, wobh=wobh, wfth=wfth, sth=sth, sgh=sgh)

    st0 = ExitStack()
    E = st0.enter_context

    c_ident = E(nc.sbuf_tensor("c_ident", [128, 128], BF16))
    c_mstk = E(nc.sbuf_tensor("c_mstk", [128, 4 * 512], BF16))
    c_simh = E(nc.sbuf_tensor("c_simh", [128, 32 * 128], BF16))
    c_bbase = E(nc.sbuf_tensor("c_bbase", [128, NBV], F32))
    c_fb = E(nc.sbuf_tensor("c_fb", [128, 32], F32))
    c_onec = E(nc.sbuf_tensor("c_onec", [128, 1], BF16))
    c_onecf = E(nc.sbuf_tensor("c_onecf", [128, 1], F32))
    c_oner = E(nc.sbuf_tensor("c_oner", [1, 128], BF16))
    kt_s = [E(nc.sbuf_tensor(f"kt{s}", [128, SLOT_E[s]], BF16)) for s in range(2)]
    vt_s = [E(nc.sbuf_tensor(f"vt{s}", [128, SLOT_E[s]], BF16)) for s in range(2)]
    ppb = E(nc.sbuf_tensor("ppb", [128, 6 * 512], BF16))
    accA = E(nc.sbuf_tensor("accA", [128, 2 * 512], F32))
    rd_f = [E(nc.sbuf_tensor(f"rd_f{i}", [1, 512], F32)) for i in range(2)]
    rd_b = [E(nc.sbuf_tensor(f"rd_b{i}", [1, 512], BF16)) for i in range(2)]
    rs_f0 = E(nc.sbuf_tensor("rs_f0", [128, 512], F32))
    rs_f = [rs_f0, rs_f0]

    with ExitStack() as stA:
        EA = stA.enter_context
        xb = [EA(nc.sbuf_tensor(f"xA{i}", [128, H], F32)) for i in range(2)]
        xnb = [EA(nc.sbuf_tensor(f"xnA{i}", [128, H], BF16)) for i in range(2)]
        sq = EA(nc.sbuf_tensor("sqA", [128, H], BF16))
        stt = [EA(nc.sbuf_tensor(f"stA{i}", [128, 8], F32)) for i in range(2)]

        for dst, src in ((c_ident, ident), (c_bbase, bbase),
                         (c_fb, fb), (c_simh, simh),
                         (c_onec, onescol), (c_oner, onesrow),
                         (c_onecf, onescolf)):
            P.emit("sync", lambda e, d=dst, s_=src: e.dma_start(out=d[:], in_=s_),
                   incs=[("aC", 16)])
        P.emit("sync", lambda e: e.dma_start(
            out=c_mstk[:].rearrange("p (c f) -> p c f", c=4), in_=mstk.rearrange("c p f -> p c f")), incs=[("aC", 16)])

        srcs = [(xq, 0, QR // 128), (xkv0, QR // 128, SLOT_E[0] // 128),
                (xkv1, (QR + SLOT_E[0]) // 128, SLOT_E[1] // 128)]
        tiles = []
        for src, base_t, n in srcs:
            for tloc in range(n):
                tiles.append((src, base_t + tloc, tloc))
        last_use = [None, None]
        xnb_guard = [0, 0]
        sq_guard = [0]
        for i, (src, trow, tloc) in enumerate(tiles):
            pb = i % 2
            waits = []
            if last_use[pb] is not None:
                waits = [("v", last_use[pb][0]), ("s", last_use[pb][1])]
            dl = P.emit("sync", lambda e, sr=src, tl=tloc, p_=pb: e.dma_start(
                out=xb[p_][:], in_=sr[128 * tl:128 * (tl + 1), :]),
                waits=waits, incs=[(f"aX{pb}", 16)])
            st_ = stt[pb]
            v1 = P.emit("vector", lambda e, p_=pb, st=st_: e.reduce_sum(
                out=st[:, 0:1], in_=xb[p_][:], axis=AX.X),
                waits=[P.dw(f"aX{pb}")], incs=[("v", 1)])
            s1g = P.emit("scalar", lambda e, p_=pb, st=st_: e.activation(
                sq[:], xb[p_][:], AF.Square, accum_out=st[:, 1:2]),
                waits=[P.dw(f"aX{pb}")], incs=[("s", 1)])
            v2 = P.emit("vector", lambda e, st=st_: e.tensor_scalar(
                out=st[:, 2:3], in0=st[:, 0:1], scalar1=1.0 / H, scalar2=None,
                op0=ALU.mult), waits=[("v", v1)], incs=[("v", 1)])
            v3 = P.emit("vector", lambda e, st=st_: e.scalar_tensor_tensor(
                out=st[:, 3:4], in0=st[:, 2:3], scalar=-1.0, in1=st[:, 2:3],
                op0=ALU.mult, op1=ALU.mult),
                waits=[("v", v2)], incs=[("v", 1)])
            v4a = P.emit("vector", lambda e, st=st_: e.scalar_tensor_tensor(
                out=st[:, 4:5], in0=st[:, 1:2], scalar=1.0 / H, in1=st[:, 3:4],
                op0=ALU.mult, op1=ALU.add),
                waits=[("s", s1g), ("v", v3)], incs=[("v", 1)])
            v4 = P.emit("vector", lambda e, st=st_: e.tensor_scalar(
                out=st[:, 4:5], in0=st[:, 4:5], scalar1=1e-5, scalar2=None,
                op0=ALU.add), waits=[("v", v4a)], incs=[("v", 1)])
            s2 = P.emit("scalar", lambda e, st=st_: e.activation(
                st[:, 5:6], st[:, 4:5], AF.Ln),
                waits=[("v", v4)], incs=[("s", 1)])
            s3 = P.emit("scalar", lambda e, st=st_: e.activation(
                st[:, 6:7], st[:, 5:6], AF.Exp, scale=-0.5),
                waits=[("s", s2)], incs=[("s", 1)])
            v5 = P.emit("vector", lambda e, st=st_: e.scalar_tensor_tensor(
                out=st[:, 7:8], in0=st[:, 2:3], scalar=-1.0, in1=st[:, 6:7],
                op0=ALU.mult, op1=ALU.mult),
                waits=[("s", s3)], incs=[("v", 1)])
            s4 = P.emit("scalar", lambda e, p_=pb, st=st_: e.activation(
                xnb[p_][:], xb[p_][:], AF.Identity, scale=st[:, 6:7], bias=st[:, 7:8]),
                waits=[("v", v5), ("s", s3), (f"aN{pb}", xnb_guard[pb])], incs=[("s", 1)])
            dxn = P.emit("scalar", lambda e, p_=pb, tr=trow: e.dma_start(
                out=xnh[128 * tr:128 * (tr + 1), :], in_=xnb[p_][:]),
                waits=[("s", s4)], incs=[(f"aN{pb}", 16)])
            last_use[pb] = (v1, s4)
            xnb_guard[pb] = dxn
        P.barrier()
    if stop_phase <= 1:
        raise_stop = True
    else:
        raise_stop = False
    if raise_stop:
        _emit_taps(nc, P, taps, scratch)
        _finish_early(nc, P, out)
        P.lower(nc, st0)
        return nc

    vt_tmp = {}
    stB = ExitStack()
    EB = stB.enter_context
    vt_tmp[0] = EB(nc.sbuf_tensor("vtt0", [128, SLOT_E[0]], BF16))
    vt_tmp[1] = EB(nc.sbuf_tensor("vtt1", [128, SLOT_E[1]], BF16))
    with ExitStack() as stB2:
        EB2 = stB2.enter_context
        xnt = [EB2(nc.sbuf_tensor(f"xntB{i}", [128, 2048], BF16)) for i in range(4)]
        wkv_raw = EB2(nc.sbuf_tensor("wkv_raw", [128, 32 * 256], F32))
        wkv_bf = EB2(nc.sbuf_tensor("wkv_bf", [128, 32 * 256], BF16))
        pbig = EB2(nc.psum_tensor("pbigB", [128, 4096], F32))
        dkv = P.emit("sync", lambda e: e.dma_start(
            out=wkv_raw[:].rearrange("p (j f) -> p j f", j=32), in_=wkv.rearrange("(j p) f -> p j f", p=128)), incs=[("bW", 16)])
        vcast = P.emit("vector", lambda e: e.tensor_copy(out=wkv_bf[:], in_=wkv_raw[:]),
                       waits=[("bW", dkv)], incs=[("v", 1)])
        mm_done = 0
        mmj_done = {}
        gj = [0]
        for s in range(2):
            e_s = SLOT_E[s]
            row0 = (QR if s == 0 else QR + SLOT_E[0])
            for half in range(e_s // 2048):
                r0h = row0 + 2048 * half
                for j in range(32):
                    pb = gj[0] % 4
                    dl = P.emit("sync", lambda e, jj=j, p_=pb, r0=r0h: e.dma_start_transpose(
                        xnt[p_][:, 0:2048], xnh[r0:r0 + 2048, 128 * jj:128 * (jj + 1)]),
                        waits=[("v", vcast), ("t", mmj_done.get(gj[0] - 4, 0))],
                        incs=[(f"bX{pb}", 16)])
                    for ch in range(4):
                        P.emit("tensor", lambda e, jj=j, p_=pb, c_=ch: e.matmul(
                            pbig[:, 512 * c_:512 * (c_ + 1)],
                            wkv_bf[:, 256 * jj:256 * jj + 128],
                            xnt[p_][:, 512 * c_:512 * (c_ + 1)],
                            start=(jj == 0), stop=(jj == 31)),
                            waits=[P.dw(f"bX{pb}")], incs=[("t", 1)])
                        mm_done = P.emit("tensor", lambda e, jj=j, p_=pb, c_=ch: e.matmul(
                            pbig[:, 2048 + 512 * c_:2048 + 512 * (c_ + 1)],
                            wkv_bf[:, 256 * jj + 128:256 * jj + 256],
                            xnt[p_][:, 512 * c_:512 * (c_ + 1)],
                            start=(jj == 0), stop=(jj == 31)),
                            incs=[("t", 1)])
                    mmj_done[gj[0]] = mm_done
                    gj[0] += 1
                wv = ws = 0
                for ch in range(4):
                    if ch % 2 == 0:
                        wv = P.emit("vector", lambda e, c_=ch, s_=s, hf=half: e.tensor_copy(
                            out=kt_s[s_][:, 2048 * hf + 512 * c_:2048 * hf + 512 * (c_ + 1)],
                            in_=pbig[:, 512 * c_:512 * (c_ + 1)]),
                            waits=[("t", mm_done)], incs=[("v", 1)])
                        wv = P.emit("vector", lambda e, c_=ch, s_=s, hf=half: e.tensor_copy(
                            out=vt_tmp[s_][:, 2048 * hf + 512 * c_:2048 * hf + 512 * (c_ + 1)],
                            in_=pbig[:, 2048 + 512 * c_:2048 + 512 * (c_ + 1)]),
                            waits=[("t", mm_done)], incs=[("v", 1)])
                    else:
                        ws = P.emit("scalar", lambda e, c_=ch, s_=s, hf=half: e.copy(
                            kt_s[s_][:, 2048 * hf + 512 * c_:2048 * hf + 512 * (c_ + 1)],
                            pbig[:, 512 * c_:512 * (c_ + 1)]),
                            waits=[("t", mm_done)], incs=[("s", 1)])
                        ws = P.emit("scalar", lambda e, c_=ch, s_=s, hf=half: e.copy(
                            vt_tmp[s_][:, 2048 * hf + 512 * c_:2048 * hf + 512 * (c_ + 1)],
                            pbig[:, 2048 + 512 * c_:2048 + 512 * (c_ + 1)]),
                            waits=[("t", mm_done)], incs=[("s", 1)])
                P.emit("tensor", lambda e: e.nop(), waits=[("v", wv), ("s", ws)])
        P.barrier()
    if stop_phase <= 2:
        for nm in taps:
            if nm.startswith("kv:"):
                w = nm.split(":")[1]
                obj = {"kt0": kt_s[0], "kt1": kt_s[1], "vtt0": vt_tmp[0], "vtt1": vt_tmp[1]}[w]
                _tap_sbuf(nc, P, w, obj[:], BF16)
        taps = [t for t in taps if not t.startswith("kv:")]
        _emit_taps(nc, P, taps, scratch)
        _finish_early(nc, P, out)
        P.lower(nc, st0)
        return nc

    with ExitStack() as stC:
        EC = stC.enter_context
        xntq = [EC(nc.sbuf_tensor(f"xntqC{j}", [128, 1024], BF16)) for j in range(32)]
        wqr = [EC(nc.sbuf_tensor(f"wqrC{i}", [128, H], F32)) for i in range(2)]
        wqb = [EC(nc.sbuf_tensor(f"wqbC{i}", [128, H], BF16)) for i in range(2)]
        qst = [EC(nc.sbuf_tensor(f"qstC{i}", [128, 1024], BF16)) for i in range(2)]
        pq = EC(nc.psum_tensor("pqC", [128, 4096], F32))

        prev = {}
        for half in range(2):
            tcol = 1024 * half
            xd = []
            tguard = P.cnt["t"]
            for j in range(32):
                d = P.emit("sync", lambda e, jj=j, tc=tcol: e.dma_start_transpose(
                    xntq[jj][:], xnh[tc:tc + 1024, 128 * jj:128 * (jj + 1)]),
                    waits=[("t", tguard)], incs=[("cQ", 16)])
                xd.append(d)
            cq_full = P.dw("cQ")
            for dt in range(32):
                pb = dt % 2
                waits = []
                if pb in prev:
                    waits = [("v", prev[pb][1]), ("s", prev[pb][2]), P.dw(f"cQT{pb}")]
                if half == 0:
                    dl = P.emit("sync", lambda e, d_=dt, p_=pb: e.dma_start(
                        out=wqr[p_][:].rearrange("p (j f) -> p j f", j=32),
                        in_=wq[:, 128 * d_:128 * (d_ + 1)].rearrange("(j p) f -> p j f", p=128)),
                        waits=waits, incs=[(f"cW{pb}", 16)])
                    cv = P.emit("vector", lambda e, p_=pb: e.tensor_scalar(
                        out=wqb[p_][:], in0=wqr[p_][:], scalar1=QSCALE, scalar2=None,
                        op0=ALU.mult),
                        waits=[P.dw(f"cW{pb}"), P.dw(f"cWB{pb}")] + waits, incs=[("v", 1)])
                    P.emit("scalar", lambda e, d_=dt, p_=pb: e.dma_start(
                        out=wqbh[:, 128 * d_:128 * (d_ + 1)].rearrange("(j p) f -> p j f", p=128),
                        in_=wqb[p_][:].rearrange("p (j f) -> p j f", j=32)),
                        waits=[("v", cv)], incs=[(f"cWB{pb}", 16)])
                    ready = [("v", cv)]
                else:
                    wb_guard = [P.dw("cWB0"), P.dw("cWB1")] if dt == 0 else []
                    dl = P.emit("sync", lambda e, d_=dt, p_=pb: e.dma_start(
                        out=wqb[p_][:].rearrange("p (j f) -> p j f", j=32),
                        in_=wqbh[:, 128 * d_:128 * (d_ + 1)].rearrange("(j p) f -> p j f", p=128)),
                        waits=waits + wb_guard, incs=[(f"cW{pb}", 16)])
                    ready = [P.dw(f"cW{pb}")]
                mm_done = 0
                for j in range(32):
                    for ch in range(2):
                        mm_done = P.emit("tensor", lambda e, jj=j, c_=ch, p_=pb: e.matmul(
                            pq[:, 2048 * p_ + 512 * c_:2048 * p_ + 512 * (c_ + 1)],
                            wqb[p_][:, 128 * jj:128 * (jj + 1)],
                            xntq[jj][:, 512 * c_:512 * (c_ + 1)],
                            start=(jj == 0), stop=(jj == 31)),
                            waits=ready + [cq_full], incs=[("t", 1)])
                evv = P.emit("vector", lambda e, p_=pb: e.tensor_copy(
                    out=qst[p_][:, 0:512], in_=pq[:, 2048 * p_:2048 * p_ + 512]),
                    waits=[("t", mm_done)], incs=[("v", 1)])
                evs = P.emit("scalar", lambda e, p_=pb: e.copy(
                    qst[p_][:, 512:1024], pq[:, 2048 * p_ + 512:2048 * p_ + 1024]),
                    waits=[("t", mm_done)], incs=[("s", 1)])
                dq = P.emit("scalar", lambda e, d_=dt, p_=pb, tc=tcol: e.dma_start(
                    out=qth[128 * d_:128 * (d_ + 1), tc:tc + 1024], in_=qst[p_][:]),
                    waits=[("v", evv), ("s", evs)], incs=[(f"cQT{pb}", 16)])
                prev[pb] = (mm_done, evv, evs, dq)
        P.barrier()
    if stop_phase <= 3:
        _emit_taps(nc, P, taps, scratch)
        _finish_early(nc, P, out)
        P.lower(nc, st0)
        return nc

    with ExitStack() as stD:
        ED = stD.enter_context
        pbf = ED(nc.psum_tensor("pbfD", [128, 2048], BF16))

        tr_done = 0
        ev_v = P.cnt["v"]
        for s in (range(2) if 'vtr' in dparts else ()):
            nt = SLOT_E[s] // 128
            for grp in range(nt // 8):
                half = grp % 2
                for u in range(8):
                    t_ = grp * 8 + u
                    tr_done = P.emit("tensor", lambda e, s_=s, tt=t_, hf=half, uu=u: e.transpose(
                        pbf[:, 1024 * hf + 128 * uu:1024 * hf + 128 * (uu + 1)],
                        vt_tmp[s_][:, 128 * tt:128 * (tt + 1)], c_ident[:]),
                        waits=[("v", ev_v)], incs=[("t", 1)])
                ev_v = P.emit("vector", lambda e, s_=s, g_=grp, hf=half: e.tensor_copy(
                    out=vt_s[s_][:, 1024 * g_:1024 * (g_ + 1)],
                    in_=pbf[:, 1024 * hf:1024 * (hf + 1)]),
                    waits=[("t", tr_done)], incs=[("v", 1)])
        stB.close()
        guardD = [("t", P.cnt["t"]), ("v", P.cnt["v"])]
        c_sz = ED(nc.sbuf_tensor("c_sz", [128, 2048], F32))
        dsz = P.emit("sync", lambda e: e.dma_start(out=c_sz[:], in_=sz),
                     waits=guardD, incs=[("dSZ", 16)])
        P.emit("scalar", lambda e: e.nop(), waits=[P.dw("dSZ")])
        P.emit("vector", lambda e: e.nop(), waits=[P.dw("dSZ")])
        qwb = [ED(nc.sbuf_tensor(f"qwD{i}", [128, H], I32)) for i in range(2)]
        wfo = [ED(nc.sbuf_tensor(f"wfoD{i}", [128, H], BF16)) for i in range(2)]
        wfs = [ED(nc.sbuf_tensor(f"wfsD{i}", [128, H], BF16)) for i in range(2)]
        wor = [ED(nc.sbuf_tensor(f"worD{i}", [128, H], F32)) for i in range(2)]
        wob = [ED(nc.sbuf_tensor(f"wobD{i}", [128, H], BF16)) for i in range(2)]

        prevD = {}
        last_ev = ev_v
        for ot in (range(32) if 'dq' in dparts else ()):
            pb = ot % 2
            waits = list(guardD) if ot == 0 else []
            if pb in prevD:
                waits += [("t", prevD[pb][0])]
            dl = P.emit("sync", lambda e, o_=ot, p_=pb: e.dma_start(
                out=qwb[p_][:], in_=qw[128 * o_:128 * (o_ + 1), :]),
                waits=waits, incs=[(f"dQ{pb}", 16)])
            wv = ws = 0
            for gg in range(32):
                if gg % 2 == 0:
                    ws = P.emit("scalar", lambda e, o_=ot, g_=gg, p_=pb: e.activation(
                        wfo[p_][:, 128 * g_:128 * (g_ + 1)],
                        qwb[p_][:, 128 * g_:128 * (g_ + 1)], AF.Identity,
                        scale=c_sz[:, 32 * o_ + g_:32 * o_ + g_ + 1],
                        bias=c_sz[:, 1024 + 32 * o_ + g_:1024 + 32 * o_ + g_ + 1]),
                        waits=[P.dw(f"dQ{pb}")], incs=[("s", 1)])
                else:
                    wv = P.emit("vector", lambda e, o_=ot, g_=gg, p_=pb: e.tensor_scalar(
                        out=wfo[p_][:, 128 * g_:128 * (g_ + 1)],
                        in0=qwb[p_][:, 128 * g_:128 * (g_ + 1)],
                        scalar1=c_sz[:, 32 * o_ + g_:32 * o_ + g_ + 1],
                        scalar2=c_sz[:, 1024 + 32 * o_ + g_:1024 + 32 * o_ + g_ + 1],
                        op0=ALU.mult, op1=ALU.add),
                        waits=[P.dw(f"dQ{pb}")], incs=[("v", 1)])
            trs = 0
            evl = last_ev
            for grp in range(4):
                half = grp % 2
                for u in range(8):
                    g_ = grp * 8 + u
                    trs = P.emit("tensor", lambda e, gg=g_, hf=half, uu=u, p_=pb: e.transpose(
                        pbf[:, 1024 * hf + 128 * uu:1024 * hf + 128 * (uu + 1)],
                        wfo[p_][:, 128 * gg:128 * (gg + 1)], c_ident[:]),
                        waits=[("v", max(wv, evl)), ("s", ws)], incs=[("t", 1)])
                evl = P.emit("vector", lambda e, g_=grp, hf=half, p_=pb: e.tensor_copy(
                    out=wfs[p_][:, 1024 * g_:1024 * (g_ + 1)],
                    in_=pbf[:, 1024 * hf:1024 * (hf + 1)]),
                    waits=[("t", trs), P.dw(f"dF{pb}")],
                    incs=[("v", 1)])
            last_ev = evl
            dw = P.emit("scalar", lambda e, o_=ot, p_=pb: e.dma_start(
                out=wfth[:, :, 128 * o_:128 * (o_ + 1)].rearrange("g p f -> p g f"),
                in_=wfs[p_][:].rearrange("p (g f) -> p g f", g=32)),
                waits=[("v", evl)], incs=[(f"dF{pb}", 16)])
            prevD[pb] = (trs, dw)

        prevW = {}
        prevCV = {}
        for rt in (range(32) if 'wo' in dparts else ()):
            pb = rt % 2
            dl = P.emit("sync", lambda e, r_=rt, p_=pb: e.dma_start(
                out=wor[p_][:], in_=wo[128 * r_:128 * (r_ + 1), :]),
                waits=[("v", prevCV.get(pb, 0))], incs=[(f"dW{pb}", 16)])
            cv = P.emit("vector", lambda e, p_=pb: e.tensor_copy(
                out=wob[p_][:], in_=wor[p_][:]),
                waits=[P.dw(f"dW{pb}"), P.dw(f"dH{pb}")], incs=[("v", 1)])
            dwo = P.emit("scalar", lambda e, r_=rt, p_=pb: e.dma_start(
                out=wobh[128 * r_:128 * (r_ + 1), :], in_=wob[p_][:]),
                waits=[("v", cv)], incs=[(f"dH{pb}", 16)])
            prevW[pb] = dwo
            prevCV[pb] = cv
        P.barrier()
    if stop_phase <= 4:
        for nm in taps:
            if nm.startswith("kv:"):
                w = nm.split(":")[1]
                obj = {"kt0": kt_s[0], "kt1": kt_s[1], "vt0": vt_s[0], "vt1": vt_s[1]}[w]
                _tap_sbuf(nc, P, w, obj[:], BF16)
        taps = [t for t in taps if not t.startswith("kv:")]
        _emit_taps(nc, P, taps, scratch)
        _finish_early(nc, P, out)
        P.lower(nc, st0)
        return nc

    with ExitStack() as stE:
        EE = stE.enter_context
        qts = EE(nc.sbuf_tensor("qtsE", [128, 32 * 512], BF16))
        tsb = EE(nc.sbuf_tensor("tsbE", [128, 7 * 512], BF16))
        bsl = EE(nc.sbuf_tensor("bslE", [128, NFAR * 32], F32))
        attnT = [EE(nc.sbuf_tensor(f"attnTE{i}", [128, 32 * 512], BF16)) for i in range(2)]
        wosl = [EE(nc.sbuf_tensor(f"woslE{i}", [128, H], BF16)) for i in range(2)]
        stsl = [EE(nc.sbuf_tensor(f"stslE{i}", [128, 1024], BF16)) for i in range(2)]
        wftsl = [EE(nc.sbuf_tensor(f"wftslE{i}", [128, H], BF16)) for i in range(2)]
        gtsl = EE(nc.sbuf_tensor("gtslE", [128, 1024], BF16))
        sgsl = EE(nc.sbuf_tensor("sgslE", [128, 1024], BF16))
        pb_ = [EE(nc.psum_tensor(f"pbE{i}", [128, 512], F32)) for i in range(8)]

        for s in range(2):
            bank_free = {i: ("t", 0) for i in range(8)}
            pp_free = {}
            rcb_guard = [0, 0]
            mul_guard = [0]
            acc_guard = [("t", 0), ("t", 0)]
            rmm_guard = 0
            for qc in range(2):
                q0 = Q0P[(s, qc)]
                ntk = NKT[(s, qc)]
                tcol = 1024 * s + 512 * qc
                tvar0 = (s * 2 + qc) * 7
                guard = [("t", P.cnt["t"])]
                dq_ = P.emit("sync", lambda e, tc=tcol: e.dma_start(
                    out=qts[:].rearrange("p (d q) -> p d q", d=32), in_=qth[:, tc:tc + 512].rearrange("(d p) q -> p d q", p=128)),
                    waits=guard, incs=[("eQ", 16)])
                dt_ = P.emit("sync", lambda e, tv=tvar0: e.dma_start(
                    out=tsb[:].rearrange("p (c f) -> p c f", c=7), in_=tstk[tv:tv + 7].rearrange("c p f -> p c f")),
                    incs=[("eT", 16)])
                eq_full = P.dw("eQ")
                et_full = P.dw("eT")
                sbias = {}
                for fi, h in enumerate(FAR_HEADS):
                    off = BB_OFF[(s, qc)]
                    sbias[h] = P.emit("scalar", lambda e, f_=fi, o_=off, n_=ntk, h_=h: e.activation(
                        bsl[:, 32 * f_:32 * f_ + n_], c_bbase[:, o_:o_ + n_],
                        AF.Identity, scale=SLOPES[h_]),
                        waits=[("s", P.cnt["s"])], incs=[("s", 1)])

                for g2 in range(16):
                    heads = (2 * g2, 2 * g2 + 1)
                    jmins = [_jmin(h, s, qc) for h in heads]
                    j0 = min(jmins)
                    actidx = [0, 0]
                    av_started = [False, False]
                    last_av = [0, 0]
                    last_ac = [("v", 0), ("v", 0)]
                    pend = []

                    def flush_avd():
                        for (i_p, ppi_p, jj_p, ex_p, st_p, sp_p) in pend:
                            av = P.emit("tensor", lambda e, s_=s, jj=jj_p, pi=ppi_p, i_=i_p,
                                        st_=st_p, sp_=sp_p: e.matmul(
                                pb_[6 + i_][:], vt_s[s_][:, 128 * jj:128 * (jj + 1)],
                                ppb[:, 512 * pi:512 * (pi + 1)], start=st_, stop=sp_),
                                waits=[("s", ex_p), bank_free[6 + i_p]], incs=[("t", 1)])
                            eng, sn = (("vector", "v"), ("gpsimd", "g"))[i_p]
                            if st_p:
                                ac = P.emit(eng, lambda e, pi=ppi_p, i_=i_p: e.tensor_copy(
                                    out=accA[:, 512 * i_:512 * (i_ + 1)],
                                    in_=ppb[:, 512 * pi:512 * (pi + 1)]),
                                    waits=[("s", ex_p), acc_guard[i_p]], incs=[(sn, 1)])
                            else:
                                ac = P.emit(eng, lambda e, pi=ppi_p, i_=i_p: e.tensor_tensor(
                                    out=accA[:, 512 * i_:512 * (i_ + 1)],
                                    in0=accA[:, 512 * i_:512 * (i_ + 1)],
                                    in1=ppb[:, 512 * pi:512 * (pi + 1)], op=ALU.add),
                                    waits=[("s", ex_p), last_ac[i_p]], incs=[(sn, 1)])
                            last_ac[i_p] = (sn, ac)
                            last_av[i_p] = av
                            pp_free[ppi_p] = [("t", av), last_ac[i_p]]
                        pend.clear()

                    for j in range(j0, ntk):
                        units = []
                        for i, h in enumerate(heads):
                            if j < jmins[i]:
                                continue
                            sb = 3 * i + (actidx[i] % 3)
                            ppi = 3 * i + (actidx[i] % 3)
                            near = h <= NEAR_MAX
                            diag = (128 * j - q0) >= 0
                            mm = P.emit("tensor", lambda e, s_=s, jj=j, h_=h, b_=sb,
                                        st1=not (near or diag): e.matmul(
                                pb_[b_][:], kt_s[s_][:, 128 * jj:128 * (jj + 1)],
                                qts[:, 512 * h_:512 * (h_ + 1)], start=True, stop=st1),
                                waits=[eq_full, bank_free[sb], et_full], incs=[("t", 1)])
                            if near:
                                cv = (128 * j - q0 + 384) // 128
                                mm = P.emit("tensor", lambda e, b_=sb, c_=cv, h_=h: e.matmul(
                                    pb_[b_][:], c_simh[:, 128 * h_:128 * (h_ + 1)],
                                    tsb[:, 512 * c_:512 * (c_ + 1)],
                                    start=False, stop=True), incs=[("t", 1)])
                            elif diag:
                                cv4 = (128 * j - q0) // 128
                                mm = P.emit("tensor", lambda e, b_=sb, c_=cv4: e.matmul(
                                    pb_[b_][:], c_ident[:],
                                    c_mstk[:, 512 * c_:512 * (c_ + 1)],
                                    start=False, stop=True), incs=[("t", 1)])
                            units.append((i, h, sb, ppi, mm))
                        flush_avd()
                        for (i, h, sb, ppi, mm) in units:
                            wexp = [("t", mm)]
                            if ppi in pp_free:
                                wexp.extend(pp_free[ppi])
                            if h > NEAR_MAX:
                                fi = h - NEAR_MAX - 1
                                wexp.append(("s", sbias[h]))
                                ex = P.emit("scalar", lambda e, b_=sb, pi=ppi, f_=fi, j_=j: e.activation(
                                    ppb[:, 512 * pi:512 * (pi + 1)], pb_[b_][:], AF.Exp,
                                    bias=bsl[:, 32 * f_ + j_:32 * f_ + j_ + 1]),
                                    waits=wexp, incs=[("s", 1)])
                            else:
                                ex = P.emit("scalar", lambda e, b_=sb, pi=ppi: e.activation(
                                    ppb[:, 512 * pi:512 * (pi + 1)], pb_[b_][:], AF.Exp),
                                    waits=wexp, incs=[("s", 1)])
                            bank_free[sb] = ("s", ex)
                            pend.append((i, ppi, j, ex, not av_started[i], j == ntk - 1))
                            av_started[i] = True
                            actidx[i] += 1
                    flush_avd()
                    dmms = {}
                    rcs = {}
                    rcbs = {}
                    for i in range(2):
                        dmms[i] = P.emit("tensor", lambda e, i_=i: e.matmul(
                            pb_[3 * i_][0:1, :], c_onecf[:], accA[:, 512 * i_:512 * (i_ + 1)],
                            start=True, stop=True),
                            waits=[last_ac[i], bank_free[3 * i]], incs=[("t", 1)])
                        acc_guard[i] = ("t", dmms[i])
                    for i in range(2):
                        rcs[i] = P.emit("vector", lambda e, i_=i: e.reciprocal(
                            out=rd_f[i_][:], in_=pb_[3 * i_][0:1, :]),
                            waits=[("t", dmms[i]), ("v", rcb_guard[i])], incs=[("v", 1)])
                    for i in range(2):
                        rcbs[i] = P.emit("vector", lambda e, i_=i: e.tensor_copy(
                            out=rd_b[i_][:], in_=rd_f[i_][:]),
                            waits=[("v", rcs[i]), ("t", rmm_guard)], incs=[("v", 1)])
                        rcb_guard[i] = rcbs[i]
                    prev_mul = 0
                    for i in range(2):
                        h = heads[i]
                        rmm = P.emit("tensor", lambda e, i_=i: e.matmul(
                            pb_[3 * i_][:], c_oner[:], rd_b[i_][:],
                            start=True, stop=True),
                            waits=[("v", rcbs[i])], incs=[("t", 1)])
                        rmm_guard = rmm
                        rcp = P.emit("vector", lambda e, i_=i: e.tensor_copy(
                            out=rs_f[i_][:], in_=pb_[3 * i_][:]),
                            waits=[("t", rmm), ("v", prev_mul), ("v", mul_guard[0])],
                            incs=[("v", 1)])
                        bank_free[3 * i] = ("v", rcp)
                        mul = P.emit("vector", lambda e, h_=h, i_=i, q_=qc: e.tensor_tensor(
                            out=attnT[q_][:, 512 * h_:512 * (h_ + 1)], in0=pb_[6 + i_][:],
                            in1=rs_f[i_][:], op=ALU.mult),
                            waits=[("t", last_av[i]), ("v", rcp)], incs=[("v", 1)])
                        prev_mul = mul
                        mul_guard[0] = mul
                        bank_free[6 + i] = ("v", mul)
            P.barrier()

            prevWo = {}
            evac_guard = {0: [], 1: []}
            for ot in range(32):
                pbu = ot % 2
                dl = P.emit("sync", lambda e, o_=ot, p_=pbu: e.dma_start(
                    out=wosl[p_][:].rearrange("p (j f) -> p j f", j=32),
                    in_=wobh[:, 128 * o_:128 * (o_ + 1)].rearrange("(j p) f -> p j f", p=128)),
                    waits=[("t", prevWo[pbu][1]) if pbu in prevWo else ("t", 0)],
                    incs=[(f"eW{pbu}", 16)])
                bk = 4 * pbu
                mmw = 0
                for ht in range(32):
                    for q_ in range(2):
                        mmw = P.emit("tensor", lambda e, h_=ht, qq=q_, p_=pbu, b_=bk: e.matmul(
                            pb_[b_ + qq][:], wosl[p_][:, 128 * h_:128 * (h_ + 1)],
                            attnT[qq][:, 512 * h_:512 * (h_ + 1)],
                            start=(h_ == 0), stop=(h_ == 31)),
                            waits=[P.dw(f"eW{pbu}")] + evac_guard[pbu], incs=[("t", 1)])
                e1 = P.emit("vector", lambda e, p_=pbu, b_=bk: e.tensor_copy(
                    out=stsl[p_][:, 0:512], in_=pb_[b_][:]),
                    waits=[("t", mmw), P.dw(f"eS{pbu}")], incs=[("v", 1)])
                e2 = P.emit("scalar", lambda e, p_=pbu, b_=bk: e.copy(
                    stsl[p_][:, 512:1024], pb_[b_ + 1][:]),
                    waits=[("t", mmw), P.dw(f"eS{pbu}")], incs=[("s", 1)])
                dst_ = P.emit("scalar", lambda e, o_=ot, p_=pbu, s_=s: e.dma_start(
                    out=sth[128 * o_:128 * (o_ + 1), 1024 * s_:1024 * (s_ + 1)],
                    in_=stsl[p_][:]),
                    waits=[("v", e1), ("s", e2)], incs=[(f"eS{pbu}", 16)])
                prevWo[pbu] = (dst_, mmw)
                evac_guard[pbu] = [("v", e1), ("s", e2)]
            P.barrier()

            xdF = []
            for j in range(32):
                buf = attnT[j // 16]
                coff = 1024 * (j % 16)
                d = P.emit("sync", lambda e, jj=j, b_=buf, c_=coff, s_=s: e.dma_start_transpose(
                    b_[:, c_:c_ + 1024], xnh[1024 * s_:1024 * (s_ + 1), 128 * jj:128 * (jj + 1)]),
                    incs=[("eX", 16)])
                xdF.append(d)
            ex_full = P.dw("eX")
            prevF = {}
            prev_mulf = 0
            sil_guard = {0: 0, 1: 0}
            for ot in range(32):
                pbu = ot % 2
                waits = []
                if pbu in prevF:
                    waits = [("v", prevF[pbu][1])]
                dl = P.emit("sync", lambda e, o_=ot, p_=pbu: e.dma_start(
                    out=wftsl[p_][:].rearrange("p (g f) -> p g f", g=32),
                    in_=wfth[:, :, 128 * o_:128 * (o_ + 1)].rearrange("g p f -> p g f")),
                    waits=waits, incs=[(f"eF{pbu}", 16)])
                dls = P.emit("sync", lambda e, o_=ot, p_=pbu, s_=s: e.dma_start(
                    out=stsl[p_][:], in_=sth[128 * o_:128 * (o_ + 1), 1024 * s_:1024 * (s_ + 1)]),
                    waits=[("v", prev_mulf)], incs=[(f"eR{pbu}", 16)])
                bkf = 2 + 4 * pbu
                mmf = 0
                for gg in range(32):
                    buf = attnT[gg // 16]
                    coff = 1024 * (gg % 16)
                    for q_ in range(2):
                        mmf = P.emit("tensor", lambda e, g_=gg, qq=q_, p_=pbu, b_=buf, c_=coff,
                                     bb=bkf: e.matmul(
                            pb_[bb + qq][:], wftsl[p_][:, 128 * g_:128 * (g_ + 1)],
                            b_[:, c_ + 512 * qq:c_ + 512 * (qq + 1)],
                            start=(g_ == 0), stop=(g_ == 31)),
                            waits=[P.dw(f"eF{pbu}"), ex_full, ("s", sil_guard[pbu])],
                            incs=[("t", 1)])
                sil = 0
                for q_ in range(2):
                    sil = P.emit("scalar", lambda e, o_=ot, qq=q_, bb=bkf: e.activation(
                        gtsl[:, 512 * qq:512 * (qq + 1)], pb_[bb + qq][:], AF.Silu,
                        bias=c_fb[:, o_:o_ + 1]),
                        waits=[("t", mmf), ("v", prev_mulf)], incs=[("s", 1)])
                sil_guard[pbu] = sil
                mulf = P.emit("vector", lambda e, p_=pbu: e.tensor_tensor(
                    out=sgsl[:], in0=gtsl[:], in1=stsl[p_][:], op=ALU.mult),
                    waits=[("s", sil), P.dw(f"eR{pbu}"), P.dw("eG")],
                    incs=[("v", 1)])
                prev_mulf = mulf
                dsg = P.emit("scalar", lambda e, o_=ot, s_=s: e.dma_start(
                    out=sgh[128 * o_:128 * (o_ + 1), 1024 * s_:1024 * (s_ + 1)],
                    in_=sgsl[:]),
                    waits=[("v", mulf)], incs=[("eG", 16)])
                prevF[pbu] = (dsg, mulf)
            P.barrier()

    if stop_phase <= 5:
        _emit_taps(nc, P, taps, scratch)
        _finish_early(nc, P, out)
        P.lower(nc, st0)
        return nc

    with ExitStack() as stF:
        EF = stF.enter_context
        sgt = [EF(nc.sbuf_tensor(f"sgtF{i}", [128, H], BF16)) for i in range(4)]
        xcb0 = [EF(nc.sbuf_tensor(f"xcF{i}", [128, H], F32)) for i in range(2)]
        osb0 = [EF(nc.sbuf_tensor(f"osF{i}", [128, H], F32)) for i in range(2)]
        xcb = xcb0 + xcb0
        osb = osb0 + osb0
        lastF = {}
        lastF2 = {}
        for t_ in range(16):
            pbu = t_ % 4
            wv = [("v", lastF[pbu])] if pbu in lastF else []
            wv2 = [("v", lastF2[t_ % 2])] if (t_ % 2) in lastF2 else []
            d1_ = P.emit("sync", lambda e, tt=t_, p_=pbu: e.dma_start_transpose(
                sgt[p_][:], sgh[:, 128 * tt:128 * (tt + 1)]),
                waits=wv, incs=[(f"fS{pbu}", 16)])
            d2_ = P.emit("scalar", lambda e, tt=t_, p_=pbu: e.dma_start(
                out=xcb[p_][:], in_=xq[128 * tt:128 * (tt + 1), :]),
                waits=wv2, incs=[(f"fX{t_ % 2}", 16)])
            cv = P.emit("vector", lambda e, p_=pbu: e.tensor_tensor(
                out=osb[p_][:], in0=xcb[p_][:], in1=sgt[p_][:], op=ALU.add),
                waits=[P.dw(f"fS{pbu}"), P.dw(f"fX{t_ % 2}"), P.dw(f"fO{t_ % 2}")],
                incs=[("v", 1)])
            do = P.emit("scalar", lambda e, tt=t_, p_=pbu: e.dma_start(
                out=out[128 * tt:128 * (tt + 1), :], in_=osb[p_][:]),
                waits=[("v", cv)], incs=[(f"fO{t_ % 2}", 16)])
            lastF[pbu] = cv
            lastF2[t_ % 2] = cv
        P.emit("sync", lambda e: e.nop(), waits=[P.dw("fO0"), P.dw("fO1")])


    _emit_taps(nc, P, taps, scratch)
    P.lower(nc, st0)
    st0.close()
    return nc


_NC_CACHE = [None]


def kernel(L0_attn_hidden_states, L0_attn_Wq, L0_attn_Wkv, L0_attn_Wo,
           L0_ffn_qweight, L0_ffn_scales, L0_ffn_scaled_zeros, L0_ffn_bias):
    x = np.ascontiguousarray(np.asarray(L0_attn_hidden_states, dtype=np.float32))
    wq_a = np.ascontiguousarray(np.asarray(L0_attn_Wq, dtype=np.float32))
    wkv_a = np.ascontiguousarray(np.asarray(L0_attn_Wkv, dtype=np.float32))
    wo_a = np.ascontiguousarray(np.asarray(L0_attn_Wo, dtype=np.float32))
    qw_a = np.ascontiguousarray(np.asarray(L0_ffn_qweight, dtype=np.int32).reshape(H, H))
    sc_a = np.asarray(L0_ffn_scales, dtype=np.float32).reshape(H, 32)
    zz_a = np.asarray(L0_ffn_scaled_zeros, dtype=np.float32).reshape(H, 32)
    fb_a = np.asarray(L0_ffn_bias, dtype=np.float32).reshape(H)

    sz = np.zeros((128, 2048), np.float32)
    sz[:, 0:1024] = sc_a.reshape(32, 128, 32).transpose(1, 0, 2).reshape(128, 1024)
    sz[:, 1024:2048] = zz_a.reshape(32, 128, 32).transpose(1, 0, 2).reshape(128, 1024)
    fbp = np.ascontiguousarray(fb_a.reshape(32, 128).T)

    if _NC_CACHE[0] is None:
        _NC_CACHE[0] = build_nc()
    nc = _NC_CACHE[0]

    in_maps = []
    for core in range(NCORE):
        b, par = core // 2, core % 2
        xq_c, xkv0_c, xkv1_c = _stage_core(x, b, par)
        tstk, mstk, bbase, eh2f, sel2, ident_b, simh_b = _const_tables(par)
        in_maps.append(dict(
            simh=simh_b,
            xq=xq_c, xkv0=xkv0_c, xkv1=xkv1_c, wq=wq_a, wkv=wkv_a, wo=wo_a,
            qw=qw_a, sz=sz, fb=fbp, tstk=tstk, mstk=mstk, bbase=bbase,
            eh2f=eh2f, sel2=sel2, ident=ident_b,
            onescol=np.ones((128, 1), np.float32).astype(bfdt),
            onescolf=np.ones((128, 1), np.float32),
            onesrow=np.ones((1, 128), np.float32).astype(bfdt)))

    res = run_bass_kernel_spmd(nc, in_maps, core_ids=list(range(NCORE)))

    outx = np.empty((B, S, H), np.float32)
    for core in range(NCORE):
        b, par = core // 2, core % 2
        cks = _chunks_for(par)
        oc = np.asarray(res.results[core]["out"], dtype=np.float32)
        for si, c in enumerate(cks):
            outx[b, 1024 * c:1024 * (c + 1)] = oc[1024 * si:1024 * (si + 1)]
    return outx
